# version 43
# baseline (speedup 1.0000x reference)
import os
import numpy as np
import contextlib
import concourse.bass as bass
import concourse.mybir as mybir
from concourse.bass_utils import run_bass_kernel_spmd

F32 = mybir.dt.float32
BF16 = mybir.dt.bfloat16
F8 = mybir.dt.float8e4
I32 = mybir.dt.int32
AF = mybir.ActivationFunctionType
ALU = mybir.AluOpType
AX = mybir.AxisListType

NCORES = 8
B_CORE = 2
SEQ = 2048
D = 1024
H = 16
DH = 64
CTX = 1536
LAT = 512
NT = SEQ // 128
NC8 = D // 128
LN_EPS = 1e-5
C_CLAMP = 80.0

ENGINES = ("sync", "tensor", "scalar", "vector", "gpsimd")
_SEMS_FOR_DEBUG = {}
_CTR_FOR_DEBUG = {}
_PHASE = int(os.environ.get("KERN_PHASE", "99"))
_NSEQ = int(os.environ.get("KERN_NSEQ", "2"))
_NTQB = int(os.environ.get("KERN_NTQB", "4"))
_NPAIR = int(os.environ.get("KERN_NPAIR", "8"))
_EP = int(os.environ.get("KERN_EP", "9"))
_EPDELAY = int(os.environ.get("KERN_EPDELAY", "4"))
_EP2 = int(os.environ.get("KERN_EP2", "9"))


class Walk:

    def __init__(self, active, eng, sems):
        self.active = active
        self.eng = eng
        self.sems = sems
        self.ctr = {k: 0 for k in sems}

    AUTO_CHAIN = {"vector": "dve", "scalar": "act"}

    def op(self, ename, fn, waits=(), inc=None, n=1):
        rw = [(s, (self.ctr[s] if v is None else v)) for (s, v) in waits]
        ac = self.AUTO_CHAIN.get(ename)
        if ac is not None and inc == ac:
            rw.append((ac, self.ctr[ac]))
        if self.active == ename:
            for s, v in rw:
                if v > 0:
                    self.eng.wait_ge(self.sems[s], v)
            inst = fn(self.eng)
            if inc is not None and inst is not None:
                inst.then_inc(self.sems[inc], n)
        if inc is not None:
            self.ctr[inc] += n
        return self.ctr[inc] if inc is not None else None

    def final_wait(self, ename, sem):
        v = self.ctr[sem]
        if self.active == ename:
            self.eng.wait_ge(self.sems[sem], v)


def cur(s):
    return (s, None)


def at(s, v):
    return (s, v)


def build_nc():
    nc = bass.Bass("TRN2", target_bir_lowering=False)

    x_d = nc.dram_tensor("x", [B_CORE, SEQ, D], F32, kind="ExternalInput").ap()
    wqT_d = nc.dram_tensor("wqT", [D, D], F32, kind="ExternalInput").ap()
    wkvT_d = nc.dram_tensor("wkvT", [D, D], F32, kind="ExternalInput").ap()
    woT_d = nc.dram_tensor("woT", [D, D], F32, kind="ExternalInput").ap()
    wf_d = nc.dram_tensor("wf", [D, D], F32, kind="ExternalInput").ap()
    bo_d = nc.dram_tensor("bo", [D], F32, kind="ExternalInput").ap()
    fb_d = nc.dram_tensor("fb", [D], F32, kind="ExternalInput").ap()
    g_d = nc.dram_tensor("lng", [DH], F32, kind="ExternalInput").ap()
    bb_d = nc.dram_tensor("lnb", [DH], F32, kind="ExternalInput").ap()
    cosk_d = nc.dram_tensor("cosk", [128, NT * DH], BF16, kind="ExternalInput").ap()
    sinswk_d = nc.dram_tensor("sinswk", [128, NT * DH], BF16, kind="ExternalInput").ap()
    mask_d = nc.dram_tensor("mask", [128, 128], BF16, kind="ExternalInput").ap()
    out_d = nc.dram_tensor("out", [B_CORE, SEQ, D], F32, kind="ExternalOutput").ap()

    _dbg = os.environ.get("KERN_DBG") == "1"
    _ik = "ExternalOutput" if _dbg else "Internal"
    ln1_s = nc.dram_tensor("ln1_s", [SEQ, D], BF16, kind=_ik).ap()
    q_s = nc.dram_tensor("q_s", [SEQ, D], BF16, kind=_ik).ap()
    lkv_s = nc.dram_tensor("lkv_s", [SEQ, D], BF16, kind=_ik).ap()
    xd_s = nc.dram_tensor("xd_s", [SEQ, D], BF16, kind=_ik).ap()
    wdq_s = nc.dram_tensor("wdq_s", [D, D], BF16, kind=_ik).ap()
    wq_b = nc.dram_tensor("wq_b", [NC8, 128, D], BF16, kind=_ik).ap()
    wkv_b = nc.dram_tensor("wkv_b", [NC8, 128, D], BF16, kind=_ik).ap()
    wo_b = nc.dram_tensor("wo_b", [NC8, 128, D], BF16, kind=_ik).ap()
    if _dbg:
        dbgb_d = nc.dram_tensor("dbgb", [4, 128, D], BF16, kind="ExternalOutput").ap()
        dbgf_d = nc.dram_tensor("dbgf", [4, 128, D], F32, kind="ExternalOutput").ap()
    mq_s = nc.dram_tensor("mq_s", [128], F32, kind="Internal").ap()
    mk_s = nc.dram_tensor("mk_s", [128], F32, kind="Internal").ap()

    with contextlib.ExitStack() as stack:
        E = stack.enter_context
        slotA = E(nc.sbuf_tensor("slotA", [128, NC8, D], BF16))
        slotB = E(nc.sbuf_tensor("slotB", [128, NC8, D], BF16))
        wstage = E(nc.sbuf_tensor("wstage", [128, 2, 512], F32))
        cosk = E(nc.sbuf_tensor("cosk_t", [128, NT, DH], BF16))
        sinswk = E(nc.sbuf_tensor("sinswk_t", [128, NT, DH], BF16))
        masks = E(nc.sbuf_tensor("masks", [128, 128], BF16))
        grep = E(nc.sbuf_tensor("grep", [128, DH], BF16))
        brep = E(nc.sbuf_tensor("brep", [128, DH], BF16))
        gbf32 = E(nc.sbuf_tensor("gbf32", [128, 2, DH], F32))
        bo_bf = E(nc.sbuf_tensor("bo_bf", [1, D], BF16))
        fb_bf = E(nc.sbuf_tensor("fb_bf", [1, D], BF16))
        ones64 = E(nc.sbuf_tensor("ones64", [128, DH], BF16))
        ones1b = E(nc.sbuf_tensor("ones1b", [1, 128], BF16))
        zeros1b = E(nc.sbuf_tensor("zeros1b", [1, 128], BF16))
        ones1f = E(nc.sbuf_tensor("ones1f", [1, 128], F32))

        xstage = E(nc.sbuf_tensor("xstage", [128, 2, D], F32))
        ln1tm = E(nc.sbuf_tensor("ln1tm", [128, 2, D], BF16))
        ropeA = E(nc.sbuf_tensor("ropeA", [128, D], BF16))
        ropeB = E(nc.sbuf_tensor("ropeB", [128, D], BF16))
        lkv_tm = E(nc.sbuf_tensor("lkv_tm", [128, NT, D], BF16))
        lkv_fm = E(nc.sbuf_tensor("lkv_fm", [128, NC8, SEQ], BF16))
        q_fm = E(nc.sbuf_tensor("q_fm", [128, NC8, SEQ], BF16))
        p_buf = E(nc.sbuf_tensor("p_buf", [128, 3, D], BF16))
        out_all = E(nc.sbuf_tensor("out_all", [128, 2, NC8, LAT], BF16))
        div_t = E(nc.sbuf_tensor("div_t", [128, LAT], F32))
        y_t = E(nc.sbuf_tensor("y_t", [128, 4, D], BF16))
        ln2_t = E(nc.sbuf_tensor("ln2_t", [128, D], F32))
        xprime = E(nc.sbuf_tensor("xprime", [128, D], F32))
        xd_fm = E(nc.sbuf_tensor("xd_fm", [128, NC8, LAT], BF16))
        st_sum = E(nc.sbuf_tensor("st_sum", [128, 1], F32))
        st_sum2 = E(nc.sbuf_tensor("st_sum2", [128, 1], F32))
        st_sq = E(nc.sbuf_tensor("st_sq", [128, 1], F32))
        st_m = E(nc.sbuf_tensor("st_m", [128, 1], F32))
        st_v = E(nc.sbuf_tensor("st_v", [128, 1], F32))
        st_r = E(nc.sbuf_tensor("st_r", [128, 1], F32))
        st_beta = E(nc.sbuf_tensor("st_beta", [128, 1], F32))
        st_t = E(nc.sbuf_tensor("st_t", [128, 1], F32))
        hsum = E(nc.sbuf_tensor("hsum", [128, H], F32))
        hsq = E(nc.sbuf_tensor("hsq", [128, H], F32))
        hm = E(nc.sbuf_tensor("hm", [128, H], F32))
        hv = E(nc.sbuf_tensor("hv", [128, H], F32))
        hr = E(nc.sbuf_tensor("hr", [128, H], F32))
        hmb = E(nc.sbuf_tensor("hmb", [128, H], BF16))
        hrb = E(nc.sbuf_tensor("hrb", [128, H], BF16))
        mq_run = E(nc.sbuf_tensor("mq_run", [128, 1], F32))
        mk_run = E(nc.sbuf_tensor("mk_run", [128, 1], F32))
        red_t = E(nc.sbuf_tensor("red_t", [128, 1], F32))
        c_sc = E(nc.sbuf_tensor("c_sc", [1, 4], F32))
        negc = E(nc.sbuf_tensor("negc", [128, 2], F32))
        bs_max = E(nc.sbuf_tensor("bs_max", [128, 32], F32))
        bs_s = E(nc.sbuf_tensor("bs_s", [128, 32], F32))
        bs_s64 = E(nc.sbuf_tensor("bs_s64", [128, 32], F32))
        constv = E(nc.sbuf_tensor("constv", [128, 8], F32))
        mqn2 = E(nc.sbuf_tensor("mqn2", [128, H], F32))
        mkn2 = E(nc.sbuf_tensor("mkn2", [128, H], F32))

        psA = E(nc.psum_tensor("psA", [128, D], F32))
        psB = E(nc.psum_tensor("psB", [128, D], F32))
        psC = E(nc.psum_tensor("psC", [128, 512], F32))
        psD = E(nc.psum_tensor("psD", [128, 512], F32))
        psE = E(nc.psum_tensor("psE", [128, D], F32))

        _cvals = [LN_EPS, -0.5, -1.0, 0.125, 1.0, 0.0, -2.0794415416798357, 0.5]
        for _i, _cv in enumerate(_cvals):
            nc.const_aps.register(F32, _cv, constv[:, _i:_i + 1])

        sem_names = ["dma", "dw0", "dw1", "out", "pe", "act", "dve", "gp", "dtx", "dxr", "dxw",
                     "dln", "dtp", "dfq", "dtq"]
        sems = {k: E(nc.semaphore("s_" + k)) for k in sem_names}
        block = E(nc.Block())

        sq_scr = y_t[:, 0:2, :].bitcast(F32).rearrange("p a b -> p (a b)")
        kvbf = p_buf[:, 0, :]
        c_row = div_t[0:1, 0:256]
        rope_o = xstage[:, 1, 0:512].bitcast(BF16)
        xd_tm = xprime[:, 0:512].bitcast(BF16)
        y8_t = ln2_t[:, 0:256].bitcast(F8)
        ep_sq = xstage[:, 1, :]
        rope_o2 = div_t[:].bitcast(BF16)
        rope_o3 = xstage[:, 1, 512:1024].bitcast(BF16)
        pasq = ropeA[:].bitcast(F32)

        def schedule(w: Walk):
            t = lambda fn, waits=(), inc="pe", n=1: w.op("tensor", fn, waits, inc, n)
            a = lambda fn, waits=(), inc="act", n=1: w.op("scalar", fn, waits, inc, n)
            v = lambda fn, waits=(), inc="dve", n=1: w.op("vector", fn, waits, inc, n)
            y = lambda fn, waits=(), inc="dma", n=16: w.op("sync", fn, waits, inc, n)
            go = lambda fn, waits=(), inc="out", n=16: w.op("gpsimd", fn, waits, inc, n)
            ybar = lambda: w.op("sync", lambda e: None, [cur("dma")], None)

            def newton_rstd(first_waits=()):
                a(lambda e: e.activation(out=st_r[:], in_=st_v[:], func=AF.Ln),
                  waits=[cur("dve")] + list(first_waits))
                a(lambda e: e.activation(out=st_r[:], in_=st_r[:], func=AF.Exp, scale=-0.5))

            wq_done = []
            wcast = []
            last_S = [0]
            pend_chain2 = [None]
            ffn_mark = {0: 0, 1: 0}
            ep_out_mark = [0]

            y(lambda e: e.dma_start(out=cosk[:], in_=cosk_d.rearrange("p (t k) -> p t k", k=DH)))
            y(lambda e: e.dma_start(out=sinswk[:], in_=sinswk_d.rearrange("p (t k) -> p t k", k=DH)))
            y(lambda e: e.dma_start(out=masks[:], in_=mask_d[:]))
            y(lambda e: e.dma_start(out=gbf32[:, 0, :], in_=g_d[None, :].broadcast_to([128, DH])))
            y(lambda e: e.dma_start(out=gbf32[:, 1, :], in_=bb_d[None, :].broadcast_to([128, DH])))
            y(lambda e: e.dma_start(out=wstage[0:1, 0, :], in_=bo_d[None, 0:512]))
            y(lambda e: e.dma_start(out=wstage[0:1, 1, :], in_=bo_d[None, 512:1024]))
            ld0 = y(lambda e: e.dma_start(out=c_row[:], in_=fb_d[None, 0:256]))
            v(lambda e: e.tensor_copy(out=grep[:], in_=gbf32[:, 0, :]), waits=[at("dma", ld0)])
            v(lambda e: e.tensor_copy(out=brep[:], in_=gbf32[:, 1, :]))
            v(lambda e: e.tensor_copy(out=bo_bf[0:1, 0:512], in_=wstage[0:1, 0, :]))
            bcst = v(lambda e: e.tensor_copy(out=bo_bf[0:1, 512:1024], in_=wstage[0:1, 1, :]))
            y(lambda e: e.dma_start(out=wstage[0:1, 0, :], in_=fb_d[None, 0:512]),
              waits=[at("dve", bcst)])
            ldf = y(lambda e: e.dma_start(out=wstage[0:1, 1, :], in_=fb_d[None, 512:1024]))
            v(lambda e: e.tensor_copy(out=fb_bf[0:1, 0:512], in_=wstage[0:1, 0, :]),
              waits=[at("dma", ldf)])
            v(lambda e: e.tensor_copy(out=fb_bf[0:1, 512:1024], in_=wstage[0:1, 1, :]))
            for _i, _cv in enumerate([LN_EPS, -0.5, -1.0, 0.125, 1.0, 0.0, -2.0794415416798357, 0.5]):
                v(lambda e, _i=_i, _cv=_cv: e.memset(constv[:, _i:_i + 1], _cv))
            v(lambda e: e.memset(ones64[:], 1.0))
            v(lambda e: e.memset(ones1b[:], 1.0))
            v(lambda e: e.memset(zeros1b[:], 0.0))
            v(lambda e: e.memset(ones1f[:], 1.0))

            for r in range(8):
                wsfull = wstage[:].rearrange("p a b -> p (a b)")
                dw = y(lambda e, r=r, wsfull=wsfull: e.dma_start(
                    out=wsfull, in_=wf_d[128 * r:128 * (r + 1), :]),
                    waits=[cur("dve") if r == 0 else at("dve", wq_done[r - 1])])
                v(lambda e, wsfull=wsfull: e.tensor_reduce(
                    out=bs_max[:, :, None], in_=wsfull.rearrange("p (b k) -> p b k", k=32),
                    axis=AX.X, op=ALU.max, apply_absolute_value=True),
                  waits=[at("dma", dw)])
                v(lambda e: e.tensor_scalar(out=bs_s[:], in0=bs_max[:], scalar1=1.0 / 7.5, scalar2=1e-30,
                                            op0=ALU.mult, op1=ALU.max))
                v(lambda e: e.tensor_scalar_mul(out=bs_s64[:], in0=bs_s[:], scalar1=64.0))
                v(lambda e: e.reciprocal(out=bs_max[:], in_=bs_s64[:]))
                v(lambda e, wsfull=wsfull: e.tensor_tensor(
                    out=xprime[:].rearrange("p (b k) -> p b k", k=32),
                    in0=wsfull.rearrange("p (b k) -> p b k", k=32),
                    in1=bs_max[:, :, None].broadcast_to([128, 32, 32]), op=ALU.mult))
                v(lambda e: e.tensor_copy(out=y8_t, in_=xprime[:]))
                wq_done.append(
                    v(lambda e: e.tensor_tensor(
                        out=xd_tm.rearrange("p (b k) -> p b k", k=32),
                        in0=y8_t.rearrange("p (b k) -> p b k", k=32),
                        in1=bs_s64[:, :, None].broadcast_to([128, 32, 32]), op=ALU.mult)))
                y(lambda e, r=r: e.dma_start(out=wdq_s[128 * r:128 * (r + 1), :], in_=xd_tm),
                  waits=[at("dve", wq_done[r])])

            for s in range(min(B_CORE, _NSEQ)):
                if _PHASE < 2:
                    break
                def emit_ln1_tp():
                    ybar()
                    w.op("sync", lambda e: None, [cur("dln")], None)
                    for c in range(NC8):
                        w.op("sync", lambda e, c=c: e.dma_start(
                            out=q_fm[:, c, :], in_=ln1_s[:, 128 * c:128 * (c + 1)],
                            transpose=True),
                            ([at("pe", last_S[0])] if (s > 0 and c == 0) else ()),
                            "dtp", 16)
                    return w.ctr["dtp"]

                tp_ln1 = 0
                if s > 0 and _PHASE >= 3:
                    tp_ln1 = emit_ln1_tp()

                if s == 0:
                    wload_q = []
                    for wd_ap, slot, wb in [(wqT_d, slotA, wq_b), (wkvT_d, slotB, wkv_b)]:
                        for c in range(NC8):
                            for hf in range(2):
                                def wload(wd_ap=wd_ap, slot=slot, wb=wb, c=c, hf=hf):
                                    psem = "dw0" if hf == 0 else "dw1"
                                    dw = w.op("sync", lambda e, wd_ap=wd_ap, c=c, hf=hf: e.dma_start(
                                        out=wstage[:, hf, :],
                                        in_=wd_ap[128 * c:128 * (c + 1), 512 * hf:512 * (hf + 1)]),
                                        [at("dve", wcast[-2] if len(wcast) >= 2 else 0)], psem, 16)
                                    wcast.append(
                                        v(lambda e, slot=slot, c=c, hf=hf: e.tensor_copy(
                                            out=slot[:, c, 512 * hf:512 * (hf + 1)],
                                            in_=wstage[:, hf, :]),
                                          waits=[at(psem, dw)]))
                                    go(lambda e, slot=slot, wb=wb, c=c, hf=hf: e.dma_start(
                                        out=wb[c, :, 512 * hf:512 * (hf + 1)],
                                        in_=slot[:, c, 512 * hf:512 * (hf + 1)]),
                                       waits=[at("dve", wcast[-1])])
                                wload_q.append(wload)
                else:
                    wbkv_mark = 0
                    w.op("sync", lambda e: None, [cur("pe"), cur("out")], None)
                    for c in range(NC8):
                        y(lambda e, c=c: e.dma_start(out=slotA[:, c, :], in_=wq_b[c]))
                        y(lambda e, c=c: e.dma_start(out=slotB[:, c, :], in_=wkv_b[c]))
                    ybar()
                    wkvq_ready = ("dma", w.ctr["dma"])

                p1_marks = {}
                for tt in range(NT if (_PHASE >= 3 and s == 0) else 0):
                    sl = tt % 2
                    if tt < 2:
                        xw = [cur("dve"), cur("act")] if s > 0 else []
                    else:
                        wv, wa = p1_marks[("stageA", tt - 2)]
                        xw = [at("dve", wv), at("act", wa)]
                    dx = y(lambda e, s=s, tt=tt, sl=sl: e.dma_start(
                        out=xstage[:, sl, :], in_=x_d[s, 128 * tt:128 * (tt + 1), :]),
                        waits=xw)
                    v(lambda e, sl=sl: e.tensor_reduce(
                        out=st_sum[:], in_=xstage[:, sl, :], axis=AX.X, op=ALU.add),
                      waits=[at("dma", dx)])
                    sqm = a(lambda e, sl=sl: e.activation(
                        out=sq_scr, in_=xstage[:, sl, :], func=AF.Square, accum_out=st_sq[:]),
                        waits=[at("dma", dx)] + ([cur("dve")] if (tt == 0 and s > 0) else []))
                    v(lambda e: e.tensor_scalar_mul(out=st_m[:], in0=st_sum[:], scalar1=1.0 / D))
                    v(lambda e: e.tensor_scalar_mul(out=st_v[:], in0=st_sq[:], scalar1=1.0 / D),
                      waits=[cur("act")])
                    v(lambda e: e.tensor_tensor(out=red_t[:], in0=st_m[:], in1=st_m[:], op=ALU.mult))
                    v(lambda e: e.tensor_sub(out=st_v[:], in0=st_v[:], in1=red_t[:]))
                    a(lambda e: e.activation(out=st_r[:], in_=st_v[:], func=AF.Ln, bias=LN_EPS),
                      waits=[cur("dve")])
                    a(lambda e: e.activation(out=st_r[:], in_=st_r[:], func=AF.Exp, scale=-0.5))
                    lnw = v(lambda e, sl=sl: e.tensor_scalar(
                        out=ln1tm[:, sl, :], in0=xstage[:, sl, :],
                        scalar1=st_m[:, 0:1], scalar2=st_r[:, 0:1],
                        op0=ALU.subtract, op1=ALU.mult),
                        waits=[cur("act"), cur("dma")])
                    y(lambda e, tt=tt, sl=sl: e.dma_start(
                        out=ln1_s[128 * tt:128 * (tt + 1), :], in_=ln1tm[:, sl, :]),
                      waits=[at("dve", lnw)])
                    p1_marks[("stageA", tt)] = (lnw, sqm)
                    for _ in range(2):
                        if wload_q:
                            wload_q.pop(0)()

                if s == 0:
                    while wload_q:
                        wload_q.pop(0)()
                    wkvq_ready = ("dve", wcast[-1])
                    wbkv_mark = w.ctr["out"]
                if _PHASE >= 3 and s == 0:
                    tp_ln1 = emit_ln1_tp()

                qbf = out_all[:, 0, 0:2, :].rearrange("p a b -> p (a b)")
                qropeA = out_all[:, 0, 2:4, :].rearrange("p a b -> p (a b)")
                qropeB = out_all[:, 0, 4:6, :].rearrange("p a b -> p (a b)")
                qrope_o = out_all[:, 0, 6:8, :].rearrange("p a b -> p (a b)")
                NB = NT if _PHASE >= 3 else 0
                for tt in range(NB + 2):
                    if tt < NB:
                        sl = tt % 2
                        kv_ps = psA if sl == 0 else psB
                        q_odd = tt % 2 == 1
                        kvw0 = [at("dtp", tp_ln1),
                                at("act", p1_marks.get(("kvE", tt - 2), 0)),
                                at(*wkvq_ready)]
                        if tt < 2:
                            kvw0.append(cur("act"))
                        for fb in range(2):
                            for c in range(NC8):
                                t(lambda e, fb=fb, c=c, tt=tt, kv_ps=kv_ps: e.matmul(
                                    kv_ps[:, 512 * fb:512 * (fb + 1)],
                                    q_fm[:, c, 128 * tt:128 * (tt + 1)],
                                    slotB[:, c, 512 * fb:512 * (fb + 1)],
                                    start=(c == 0), stop=(c == NC8 - 1)),
                                  waits=(kvw0 if (fb == 0 and c == 0) else ()))
                        pe_kv = w.ctr["pe"]
                        qw0 = [at("act", p1_marks.get(("qE", tt - 2), 0)),
                               at(*wkvq_ready)]
                        if tt < 2 and s > 0:
                            qw0 = [cur("act"), cur("dve"), at(*wkvq_ready)]
                        for fb in range(2):
                            q_ps = (psE[:, 512 * fb:512 * (fb + 1)] if q_odd
                                    else (psC[:] if fb == 0 else psD[:]))
                            for c in range(NC8):
                                t(lambda e, fb=fb, c=c, tt=tt, q_ps=q_ps: e.matmul(
                                    q_ps,
                                    q_fm[:, c, 128 * tt:128 * (tt + 1)],
                                    slotA[:, c, 512 * fb:512 * (fb + 1)],
                                    start=(c == 0), stop=(c == NC8 - 1)),
                                  waits=(qw0 if (fb == 0 and c == 0) else ()))
                        pe_q = w.ctr["pe"]
                        p1_marks[("peq", tt)] = pe_q
                        kvE = a(lambda e, kv_ps=kv_ps: e.activation(
                            out=kvbf, in_=kv_ps[:], func=AF.Identity),
                            waits=[at("pe", pe_kv),
                                   at("dve", p1_marks.get(("kvO2", tt - 1), 0))])
                        p1_marks[("kvE", tt)] = kvE
                        if q_odd:
                            qE = a(lambda e: e.activation(
                                out=qbf, in_=psE[:], func=AF.Identity),
                                waits=[at("pe", pe_q),
                                       at("dve", p1_marks.get(("qO2", tt - 1), 0))])
                        else:
                            a(lambda e: e.activation(
                                out=qbf[:, 0:512], in_=psC[:], func=AF.Identity),
                              waits=[at("pe", pe_q),
                                     at("dve", p1_marks.get(("qO2", tt - 1), 0))])
                            qE = a(lambda e: e.activation(
                                out=qbf[:, 512:1024], in_=psD[:], func=AF.Identity))
                        p1_marks[("qE", tt)] = qE

                    if tt < NB:
                        ro = (rope_o, rope_o2, rope_o3)[tt % 3]
                        p1_marks[("ro", tt)] = ro
                        v(lambda e, tt=tt: e.tensor_tensor(
                            out=ropeB[:].rearrange("p (h k) -> p h k", k=DH),
                            in0=kvbf.rearrange("p (h k) -> p h k", k=DH),
                            in1=cosk[:, tt, None, :].broadcast_to([128, H, DH]), op=ALU.mult),
                          waits=[at("act", kvE)])
                        kvO2 = v(lambda e, tt=tt: e.tensor_tensor(
                            out=ropeA[:].rearrange("p (h two k) -> p h two k", two=2, k=32),
                            in0=kvbf.rearrange("p (h two k) -> p h two k", two=2, k=32)[:, :, ::-1, :],
                            in1=sinswk[:, tt, :].rearrange("p (two k) -> p two k", two=2)[:, None, :, :]
                                .broadcast_to([128, H, 2, 32]),
                            op=ALU.mult))
                        p1_marks[("kvO2", tt)] = kvO2
                        p1_marks[("kvr", tt)] = v(lambda e, ro=ro: e.tensor_add(
                            out=ro, in0=ropeB[:], in1=ropeA[:]),
                            waits=[at("act", p1_marks.get(("hq", tt - 3), 0))])
                        v(lambda e, tt=tt: e.tensor_tensor(
                            out=qropeA.rearrange("p (h k) -> p h k", k=DH),
                            in0=qbf.rearrange("p (h k) -> p h k", k=DH),
                            in1=cosk[:, tt, None, :].broadcast_to([128, H, DH]), op=ALU.mult),
                          waits=[at("act", qE)])
                        qO2 = v(lambda e, tt=tt: e.tensor_tensor(
                            out=qropeB.rearrange("p (h two k) -> p h two k", two=2, k=32),
                            in0=qbf.rearrange("p (h two k) -> p h two k", two=2, k=32)[:, :, ::-1, :],
                            in1=sinswk[:, tt, :].rearrange("p (two k) -> p two k", two=2)[:, None, :, :]
                                .broadcast_to([128, H, 2, 32]),
                            op=ALU.mult))
                        p1_marks[("qO2", tt)] = qO2
                        qoro = (qrope_o, ln1tm[:, 0, :], ln1tm[:, 1, :])[tt % 3]
                        p1_marks[("qoro", tt)] = qoro
                        p1_marks[("qro", tt)] = v(lambda e, qoro=qoro: e.tensor_add(
                            out=qoro, in0=qropeA, in1=qropeB),
                            waits=([at("dma", p1_marks[("dq", tt - 3)])]
                                   if ("dq", tt - 3) in p1_marks else [])
                                  + ([at("act", p1_marks[("qsq", tt - 3)])]
                                     if ("qsq", tt - 3) in p1_marks else []))
                    if 2 <= tt <= NB + 1:
                        u = tt - 2
                        hsqr = v(lambda e: e.tensor_reduce(
                            out=hsq[:, :, None], in_=sq_scr.rearrange("p (h k) -> p h k", k=DH),
                            axis=AX.X, op=ALU.add),
                            waits=[at("act", p1_marks[("hq", u)])])
                        v(lambda e: e.tensor_scalar_mul(out=hm[:], in0=hsum[:], scalar1=1.0 / DH))
                        v(lambda e: e.tensor_scalar_mul(out=hv[:], in0=hsq[:], scalar1=1.0 / DH))
                        v(lambda e: e.tensor_tensor(out=hsq[:], in0=hm[:], in1=hm[:], op=ALU.mult))
                        v(lambda e: e.tensor_sub(out=hv[:], in0=hv[:], in1=hsq[:]))
                        a(lambda e: e.activation(out=hr[:], in_=hv[:], func=AF.Ln, bias=LN_EPS),
                          waits=[cur("dve")])
                        a(lambda e: e.activation(out=hr[:], in_=hr[:], func=AF.Exp, scale=-0.5))
                        v(lambda e: e.tensor_copy(out=hmb[:], in_=hm[:]), waits=[cur("act")])
                        v(lambda e: e.tensor_copy(out=hrb[:], in_=hr[:]))
                        v(lambda e, u=u: e.tensor_sub(
                            out=ropeA[:].rearrange("p (h k) -> p h k", k=DH),
                            in0=p1_marks[("ro", u)].rearrange("p (h k) -> p h k", k=DH),
                            in1=hmb[:, :, None].broadcast_to([128, H, DH])))
                        v(lambda e: e.tensor_tensor(
                            out=ropeA[:].rearrange("p (h k) -> p h k", k=DH),
                            in0=ropeA[:].rearrange("p (h k) -> p h k", k=DH),
                            in1=hrb[:, :, None].broadcast_to([128, H, DH]), op=ALU.mult))
                        v(lambda e: e.tensor_tensor(
                            out=ropeA[:].rearrange("p (h k) -> p h k", k=DH),
                            in0=ropeA[:].rearrange("p (h k) -> p h k", k=DH),
                            in1=grep[:, None, :].broadcast_to([128, H, DH]), op=ALU.mult))
                        lkvw = v(lambda e, u=u: e.tensor_tensor(
                            out=lkv_tm[:, u, :].rearrange("p (h k) -> p h k", k=DH),
                            in0=ropeA[:].rearrange("p (h k) -> p h k", k=DH),
                            in1=brep[:, None, :].broadcast_to([128, H, DH]), op=ALU.add),
                            waits=([cur("pe")] if (s > 0 and u == 0) else ()))
                        y(lambda e, u=u: e.dma_start(
                            out=lkv_s[128 * u:128 * (u + 1), :], in_=lkv_tm[:, u, :]),
                          waits=[at("dve", lkvw)])
                        ksq = a(lambda e, u=u: e.activation(out=sq_scr, in_=lkv_tm[:, u, :],
                                                            func=AF.Square),
                                waits=[at("dve", lkvw)])
                        knr = v(lambda e: e.tensor_reduce(
                            out=hsq[:, :, None], in_=sq_scr.rearrange("p (h k) -> p h k", k=DH),
                            axis=AX.X, op=ALU.add),
                            waits=[at("act", ksq)])
                        if u == 0:
                            v(lambda e: e.tensor_copy(out=mkn2[:], in_=hsq[:]))
                        else:
                            v(lambda e: e.tensor_max(out=mkn2[:], in0=mkn2[:], in1=hsq[:]))
                        qsq = a(lambda e, u=u: e.activation(
                            out=sq_scr[:], in_=p1_marks[("qoro", u)], func=AF.Square),
                            waits=[at("dve", p1_marks[("qro", u)]), at("dve", knr)])
                        p1_marks[("qsq", u)] = qsq
                        qnr = v(lambda e: e.tensor_reduce(
                            out=hsq[:, :, None], in_=sq_scr.rearrange("p (h k) -> p h k", k=DH),
                            axis=AX.X, op=ALU.add),
                            waits=[at("act", qsq)])
                        p1_marks[("qnr", u)] = qnr
                        if u == 0:
                            v(lambda e: e.tensor_copy(out=mqn2[:], in_=hsq[:]))
                        else:
                            v(lambda e: e.tensor_max(out=mqn2[:], in0=mqn2[:], in1=hsq[:]))
                        p1_marks[("dq", u)] = y(lambda e, u=u: e.dma_start(
                            out=q_s[128 * u:128 * (u + 1), :], in_=p1_marks[("qoro", u)]),
                            waits=[at("dve", p1_marks[("qro", u)])])

                    if 1 <= tt <= NB:
                        u1 = tt - 1
                        v(lambda e, u1=u1: e.tensor_reduce(
                            out=hsum[:, :, None],
                            in_=p1_marks[("ro", u1)].rearrange("p (h k) -> p h k", k=DH),
                            axis=AX.X, op=ALU.add))
                        p1_marks[("hq", u1)] = a(lambda e, u1=u1: e.activation(
                            out=sq_scr, in_=p1_marks[("ro", u1)], func=AF.Square),
                            waits=[at("dve", p1_marks[("kvr", u1)]),
                                   at("dve", p1_marks.get(("qnr", u1 - 1), 0))])

                peB = w.ctr["pe"]

                tp_pair = {}
                if _PHASE >= 3:
                    ybar()
                    for c in range(NC8):
                        psem = "dtq" if c >= 4 else "dtp"
                        w.op("sync", lambda e, c=c: e.dma_start(
                            out=lkv_fm[:, c, :], in_=lkv_s[:, 128 * c:128 * (c + 1)],
                            transpose=True),
                            ([at("pe", peB)] if c == 0 else ()), psem, 16)
                        mk = w.op("sync", lambda e, c=c: e.dma_start(
                            out=q_fm[:, c, :], in_=q_s[:, 128 * c:128 * (c + 1)],
                            transpose=True), (), psem, 16)
                        if c in (3, 7):
                            for c2 in range(c - 3, c + 1):
                                tp_pair[c2] = (psem, mk)

                ybar()

                if _PHASE < 4:
                    continue
                v(lambda e: e.tensor_reduce(out=mq_run[:], in_=mqn2[:], axis=AX.X, op=ALU.max))
                v(lambda e: e.tensor_reduce(out=mk_run[:], in_=mkn2[:], axis=AX.X, op=ALU.max))
                dmq = w.op("gpsimd", lambda e: e.dma_start(out=mq_s[:], in_=mq_run[:, 0]),
                           [cur("dve")], "dfq", 16)
                dmk = w.op("gpsimd", lambda e: e.dma_start(out=mk_s[:], in_=mk_run[:, 0]),
                           (), "dfq", 16)
                w.op("gpsimd", lambda e: e.dma_start(out=c_row[0:1, 0:128], in_=mq_s[None, :]),
                     [at("dfq", dmk)], "dfq", 16)
                dcr = w.op("gpsimd", lambda e: e.dma_start(out=c_row[0:1, 128:256], in_=mk_s[None, :]),
                           (), "dfq", 16)
                v(lambda e: e.tensor_reduce(out=c_sc[:, 0:1], in_=c_row[0:1, 0:128], axis=AX.X, op=ALU.max),
                  waits=[at("dfq", dcr)])
                v(lambda e: e.tensor_reduce(out=c_sc[:, 1:2], in_=c_row[0:1, 128:256], axis=AX.X, op=ALU.max))
                a(lambda e: e.activation(out=c_sc[:, 0:1], in_=c_sc[:, 0:1], func=AF.Ln),
                  waits=[cur("dve")])
                a(lambda e: e.activation(out=c_sc[:, 1:2], in_=c_sc[:, 1:2], func=AF.Ln))
                v(lambda e: e.tensor_add(out=c_sc[:, 2:3], in0=c_sc[:, 0:1], in1=c_sc[:, 1:2]),
                  waits=[cur("act")])
                a(lambda e: e.activation(out=c_sc[:, 2:3], in_=c_sc[:, 2:3], func=AF.Exp,
                                         scale=0.5, bias=-2.0794415416798357),
                  waits=[cur("dve")])
                cst = v(lambda e: e.tensor_scalar(out=c_sc[:, 3:4], in0=c_sc[:, 2:3],
                                                  scalar1=-1.0, scalar2=-C_CLAMP,
                                                  op0=ALU.mult, op1=ALU.max),
                        waits=[cur("act")])
                cbc = t(lambda e: e.matmul(psC[:, 0:1], ones1f[:], c_sc[:, 3:4],
                                           start=True, stop=True),
                        waits=[at("dve", cst)])
                stab = a(lambda e, s=s: e.activation(out=negc[:, s:s + 1], in_=psC[:, 0:1],
                                                     func=AF.Identity),
                         waits=[at("pe", cbc)])

                ybar()
                if s == 0:
                    for c in range(NC8):
                        for hf in range(2):
                            psem = "dw0" if hf == 0 else "dw1"
                            dw = w.op("sync", lambda e, c=c, hf=hf: e.dma_start(
                                out=wstage[:, hf, :],
                                in_=woT_d[128 * c:128 * (c + 1), 512 * hf:512 * (hf + 1)]),
                                [at("dve", wcast[-2] if len(wcast) >= 2 else 0)], psem, 16)
                            wcast.append(
                                v(lambda e, c=c, hf=hf: e.tensor_copy(
                                    out=slotA[:, c, 512 * hf:512 * (hf + 1)],
                                    in_=wstage[:, hf, :]),
                                  waits=[at(psem, dw), at("pe", pe_q),
                                         at("out", wbkv_mark)]))
                            go(lambda e, c=c, hf=hf: e.dma_start(
                                out=wo_b[c, :, 512 * hf:512 * (hf + 1)],
                                in_=slotA[:, c, 512 * hf:512 * (hf + 1)]),
                               waits=[at("dve", wcast[-1])])
                else:
                    w.op("sync", lambda e: None, [at("pe", pe_q), cur("out")], None)
                    for c in range(NC8):
                        y(lambda e, c=c: e.dma_start(out=slotA[:, c, :], in_=wo_b[c]))
                for r in range(8):
                    for c in range(NC8):
                        y(lambda e, r=r, c=c: e.dma_start(
                            out=slotB[:, c, 128 * r:128 * (r + 1)],
                            in_=wdq_s[128 * r:128 * (r + 1), 128 * c:128 * (c + 1)],
                            transpose=True),
                          waits=([at("pe", pe_kv), at("out", wbkv_mark)]
                                 if (r == 0 and c == 0) else ()))
                ybar()
                wready = (w.ctr["dve"], w.ctr["dma"])

                if _PHASE < 5:
                    continue
                exp_marks = {0: w.ctr["act"], 1: w.ctr["act"]}
                prev_ep_pe = w.ctr["pe"]
                pend_chunks = [[{"mms": [], "chain": (lambda: None)}
                                for _ in range(max(1, _EPDELAY // 4))]]
                eps_state = {"psE_dve": w.ctr["dve"], "psE_act": w.ctr["act"],
                             "ffn_pe": w.ctr["pe"], "xst_dve": w.ctr["dve"],
                             "xdw_dma": w.ctr["dxw"]}
                attn_state = {"norm": w.ctr["dve"], "div": w.ctr["act"], "pending_div": None}
                norm_done = {}
                S_pre = {}
                st_guard = [w.ctr["dve"]]
                pa_lndma = {}
                len_cp_done = [0]

                pa_dxm = {}

                def mk_pa_dx(tt, s2):
                    def thunk():
                        pa_dxm[tt] = w.op("sync", lambda e, tt=tt: e.dma_start(
                            out=xstage[:, 0, :], in_=x_d[s2, 128 * tt:128 * (tt + 1), :]),
                            [at("dve", eps_state["xst_dve"])], "dxr", 16)
                        eps_state["xst_dve"] = max(eps_state["xst_dve"], 0)
                    return thunk

                def mk_pa_cp(tt, s2):
                    def thunk():
                        sl = tt % 2
                        dx = pa_dxm[tt]
                        v(lambda e: e.tensor_reduce(
                            out=st_sum[:], in_=xstage[:, 0, :], axis=AX.X, op=ALU.add),
                          waits=[at("dxr", dx), at("dve", st_guard[0])])
                        sqm = a(lambda e: e.activation(
                            out=ep_sq, in_=xstage[:, 0, :], func=AF.Square, accum_out=st_sq[:]),
                            waits=[at("dxr", dx), at("dve", st_guard[0])])
                        v(lambda e: e.tensor_scalar_mul(out=st_m[:], in0=st_sum[:], scalar1=1.0 / D))
                        v(lambda e: e.tensor_scalar_mul(out=st_v[:], in0=st_sq[:], scalar1=1.0 / D),
                          waits=[at("act", sqm)])
                        v(lambda e: e.tensor_tensor(out=red_t[:], in0=st_m[:], in1=st_m[:],
                                                    op=ALU.mult))
                        v(lambda e: e.tensor_sub(out=st_v[:], in0=st_v[:], in1=red_t[:]))
                        v(lambda e: e.tensor_scalar_add(out=st_v[:], in0=st_v[:], scalar1=LN_EPS))
                        v(lambda e: e.tensor_scalar(out=st_t[:].bitcast(I32), in0=st_v[:].bitcast(I32),
                                                    scalar1=1, scalar2=None,
                                                    op0=ALU.logical_shift_right))
                        v(lambda e: e.tensor_scalar(out=st_t[:].bitcast(I32), in0=st_t[:].bitcast(I32),
                                                    scalar1=0, scalar2=None, op0=ALU.bitwise_not))
                        v(lambda e: e.tensor_scalar(out=st_r[:].bitcast(I32), in0=st_t[:].bitcast(I32),
                                                    scalar1=0x5F3759DF + 1, scalar2=None, op0=ALU.add))
                        for _ in range(3):
                            v(lambda e: e.tensor_tensor(out=red_t[:], in0=st_r[:], in1=st_r[:],
                                                        op=ALU.mult))
                            v(lambda e: e.tensor_tensor(out=red_t[:], in0=red_t[:], in1=st_v[:],
                                                        op=ALU.mult))
                            v(lambda e: e.tensor_scalar(out=red_t[:], in0=red_t[:],
                                                        scalar1=-0.5, scalar2=1.5,
                                                        op0=ALU.mult, op1=ALU.add))
                            v(lambda e: e.tensor_tensor(out=st_r[:], in0=st_r[:], in1=red_t[:],
                                                        op=ALU.mult))
                        lnw = v(lambda e, sl=sl: e.tensor_scalar(
                            out=ln1tm[:, sl, :], in0=xstage[:, 0, :],
                            scalar1=st_m[:, 0:1], scalar2=st_r[:, 0:1],
                            op0=ALU.subtract, op1=ALU.mult),
                            waits=([at("dln", pa_lndma[tt - 1])]
                                   if tt - 1 in pa_lndma else []))
                        eps_state["xst_dve"] = lnw
                        st_guard[0] = lnw
                        pa_lndma[tt] = w.op("sync", lambda e, tt=tt, sl=sl: e.dma_start(
                            out=ln1_s[128 * tt:128 * (tt + 1), :], in_=ln1tm[:, sl, :]),
                            [at("dve", lnw)], "dln", 16)
                    return thunk

                if s + 1 < min(B_CORE, _NSEQ) and _PHASE >= 3:
                    pa_dx_q = [mk_pa_dx(tt, s + 1) for tt in range(NT)]
                    pa_cp_q = [mk_pa_cp(tt, s + 1) for tt in range(NT)]
                else:
                    pa_dx_q, pa_cp_q = [], []
                for tqb in range(min(4, _NTQB)):
                    if tqb == 0:
                        tks = list(range(NT))
                        qc0 = CTX
                    else:
                        tks = list(range(12))
                        qc0 = (tqb - 1) * 512
                    for pair in range(min(8, _NPAIR)):
                        hA, hB = 2 * pair, 2 * pair + 1

                        def emit_S_for(pair2, qc2, kt, extra=()):
                            sl = kt % 2
                            s_ps = psA if sl == 0 else psB
                            ao = 128 * (kt - 12) if (tqb == 0 and kt >= 12) else 0
                            sw = [at("act", exp_marks[sl]), at(*tp_pair[pair2])] + list(extra)
                            t(lambda e, pair2=pair2, kt=kt, s_ps=s_ps, qc2=qc2, ao=ao: e.matmul(
                                s_ps[:, ao:512],
                                lkv_fm[0:64, pair2, 128 * kt:128 * (kt + 1)],
                                q_fm[0:64, pair2, qc2 + ao:qc2 + 512],
                                start=True, stop=True, tile_position=(0, 0)),
                              waits=sw)
                            pes = t(lambda e, pair2=pair2, kt=kt, s_ps=s_ps, qc2=qc2, ao=ao: e.matmul(
                                s_ps[:, 512 + ao:1024],
                                lkv_fm[64:128, pair2, 128 * kt:128 * (kt + 1)],
                                q_fm[64:128, pair2, qc2 + ao:qc2 + 512],
                                start=True, stop=True, tile_position=(64, 0)))
                            last_S[0] = pes
                            return pes

                        def emit_S(kt, extra=()):
                            if (pair, kt) in S_pre:
                                return S_pre.pop((pair, kt))
                            return emit_S_for(pair, qc0, kt, extra)

                        def emit_av(kt, psl, need, ao=0, hA=hA, hB=hB):
                            t(lambda e, kt=kt, hA=hA, psl=psl, ao=ao: e.matmul(
                                psC[0:64, ao:512], lkv_tm[:, kt, DH * hA:DH * (hA + 1)],
                                p_buf[:, psl, ao:512],
                                start=False, stop=False, tile_position=(0, 0)),
                              waits=need)
                            t(lambda e, kt=kt, hB=hB, psl=psl, ao=ao: e.matmul(
                                psC[64:128, ao:512], lkv_tm[:, kt, DH * hB:DH * (hB + 1)],
                                p_buf[:, psl, 512 + ao:1024],
                                start=False, stop=False, tile_position=(0, 64)))
                            t(lambda e, psl=psl, ao=ao: e.matmul(
                                psD[0:64, ao:512], ones64[:, 0:64], p_buf[:, psl, ao:512],
                                start=False, stop=False, tile_position=(0, 0)))
                            t(lambda e, psl=psl, ao=ao: e.matmul(
                                psD[64:128, ao:512], ones64[:, 0:64], p_buf[:, psl, 512 + ao:1024],
                                start=False, stop=False, tile_position=(0, 64)))

                        pes_next = emit_S(tks[0])
                        pend_av = []
                        for i, kt in enumerate(tks):
                            sl = kt % 2
                            psl = kt % 3
                            s_ps = psA if sl == 0 else psB
                            pes = pes_next
                            ao_e = 128 * (kt - 12) if (tqb == 0 and kt >= 12) else 0
                            em = a(lambda e, psl=psl, s_ps=s_ps, s=s, ao_e=ao_e: e.activation(
                                out=p_buf[:, psl, :].rearrange(
                                    "p (two c) -> p two c", two=2)[:, :, ao_e:],
                                in_=s_ps[:].rearrange(
                                    "p (two c) -> p two c", two=2)[:, :, ao_e:],
                                func=AF.Exp,
                                bias=negc[:, s:s + 1], scale=0.125),
                                waits=[at("pe", pes)])
                            exp_marks[sl] = em
                            need = [at("act", em)]
                            a_off = 128 * (kt - 12) if (tqb == 0 and kt >= 12) else 0
                            if tqb == 0 and kt >= 12:
                                mm = v(lambda e, psl=psl, a_off=a_off: e.tensor_tensor(
                                    out=p_buf[:, psl, :].rearrange("p (two c) -> p two c", two=2)[:, :, a_off:a_off + 128],
                                    in0=p_buf[:, psl, :].rearrange("p (two c) -> p two c", two=2)[:, :, a_off:a_off + 128],
                                    in1=masks[:, None, :].broadcast_to([128, 2, 128]),
                                    op=ALU.mult),
                                    waits=[at("act", em)])
                                need.append(at("dve", mm))
                            if i == 2:
                                if attn_state["pending_div"] is not None:
                                    attn_state["pending_div"]()
                                    attn_state["pending_div"] = None
                                zw = [at("dve", attn_state["norm"]), at("act", attn_state["div"])]
                                if pair == 0:
                                    zw.append(at("pe", prev_ep_pe))
                                t(lambda e: e.matmul(psC[:], zeros1b[:], bo_bf[0:1, 0:512],
                                                     start=True, stop=False),
                                  waits=zw)
                                t(lambda e: e.matmul(psD[:], zeros1b[:], bo_bf[0:1, 0:512],
                                                     start=True, stop=False))
                            pend_av.append((kt, psl, need, a_off))
                            if i >= 2:
                                emit_av(*pend_av.pop(0))
                            if i + 1 < len(tks):
                                pes_next = emit_S(tks[i + 1])
                            if i == 1 and pend_chain2[0] is not None:
                                pend_chain2[0]()
                                pend_chain2[0] = None
                            if (_PHASE >= 6 and pend_chunks and pend_chunks[0]
                                    and i >= (4 if pend_chunks[0][0].get("late") else 1)):
                                mm_q = pend_chunks[0][0]["mms"]
                                for _ in range(2):
                                    if mm_q:
                                        mm_q.pop(0)()
                            if i == 5 and pa_cp_q and len(pa_dxm) > len_cp_done[0]:
                                pa_cp_q.pop(0)()
                                len_cp_done[0] += 1
                        for args in pend_av:
                            emit_av(*args)
                        pend_av = []
                        t(lambda e: e.matmul(psC[:], zeros1b[:], bo_bf[0:1, 0:512],
                                             start=False, stop=True))
                        avr = t(lambda e: e.matmul(psD[:], zeros1b[:], bo_bf[0:1, 0:512],
                                                   start=False, stop=True))

                        def mk_div(pair=pair, tqb=tqb, avr=avr):
                            def doit():
                                a(lambda e: e.activation(out=ep_sq[:, 0:512], in_=psD[:], func=AF.Ln),
                                  waits=[at("pe", avr), at("dve", attn_state["norm"])])
                                dvm = a(lambda e: e.activation(out=div_t[:], in_=ep_sq[:, 0:512],
                                                               func=AF.Exp, scale=-1.0))
                                attn_state["div"] = dvm
                                nm = v(lambda e, pair=pair, tqb=tqb: e.tensor_tensor(
                                    out=out_all[:, tqb % 2, pair, :], in0=psC[:], in1=div_t[:],
                                    op=ALU.mult),
                                    waits=[at("act", dvm)])
                                attn_state["norm"] = nm
                                norm_done[tqb] = nm
                            return doit

                        attn_state["pending_div"] = mk_div()
                        if _PHASE >= 6 and pend_chunks and pend_chunks[0]:
                            ch = pend_chunks[0].pop(0)
                            for f in ch["mms"]:
                                f()
                            pend_chain2[0] = ch["chain"]()
                            if not pend_chunks[0]:
                                pend_chunks.pop(0)
                        if pair < 7:
                            for kt2 in range(2):
                                S_pre[(pair + 1, kt2)] = emit_S_for(pair + 1, qc0, kt2)
                        elif tqb < min(4, _NTQB) - 1:
                            for kt2 in range(2):
                                S_pre[(0, kt2)] = emit_S_for(0, tqb * 512, kt2)
                        if _dbg and s == 0 and tqb == 0 and pair == 0:
                            attn_state["pending_div"]()
                            attn_state["pending_div"] = None
                            y(lambda e: e.dma_start(out=dbgf_d[0, :, 0:512], in_=div_t[:]),
                              waits=[at("dve", attn_state["norm"])])
                            y(lambda e: e.dma_start(out=dbgb_d[1, :, 0:512], in_=out_all[:, 0, :]))
                            y(lambda e: e.dma_start(out=dbgf_d[1, :, 0:2], in_=negc[:]))
                        if pa_dx_q:
                            pa_dx_q.pop(0)()

                    def make_ep_chunks(s=s, tqb=tqb, alt=False):
                        par = tqb % 2
                        eps = {}
                        cdst = {"dve": 0, "act": 0}

                        def tgt(k, fb):
                            if alt and k % 2 == 1:
                                return psC[:] if fb == 0 else psD[:]
                            return psE[:, 512 * fb:512 * (fb + 1)]

                        def partA(ts4):
                            row0 = tqb * 512 + ts4 * 128
                            pe_wo = w.ctr["pe"]
                            isalt = alt and ts4 % 2 == 1
                            pslo = psC[:] if isalt else psE[:, 0:512]
                            pshi = psD[:] if isalt else psE[:, 512:1024]
                            dxr = w.op("sync", lambda e, s=s, row0=row0: e.dma_start(
                                out=xstage[:, 0, :], in_=x_d[s, row0:row0 + 128, :]),
                                [at("dve", eps_state["xst_dve"])], "dxr", 16)
                            if isalt:
                                a(lambda e: e.activation(out=ep_sq[:, 0:512], in_=psC[:],
                                                         func=AF.Square, accum_out=st_sq[:]),
                                  waits=[at("pe", pe_wo), at("dve", st_guard[0])])
                                sqa = a(lambda e: e.activation(out=ep_sq[:, 512:1024], in_=psD[:],
                                                               func=AF.Square, accum_out=st_t[:]))
                                cdst["act"] = sqa
                            else:
                                sqa = a(lambda e: e.activation(out=ep_sq, in_=psE[:],
                                                               func=AF.Square,
                                                               accum_out=st_sq[:]),
                                        waits=[at("pe", pe_wo), at("dve", st_guard[0])])
                                eps_state["psE_act"] = sqa
                            v(lambda e, pslo=pslo: e.tensor_reduce(out=st_sum[:], in_=pslo,
                                                                   axis=AX.X, op=ALU.add),
                              waits=[at("pe", pe_wo), at("act", sqa)])
                            v(lambda e, pshi=pshi: e.tensor_reduce(out=st_sum2[:], in_=pshi,
                                                                   axis=AX.X, op=ALU.add))
                            if isalt:
                                v(lambda e: e.tensor_add(out=st_sq[:], in0=st_sq[:], in1=st_t[:]))
                            v(lambda e: e.tensor_add(out=st_sum[:], in0=st_sum[:], in1=st_sum2[:]))
                            v(lambda e: e.tensor_scalar_mul(out=st_m[:], in0=st_sum[:], scalar1=1.0 / D))
                            v(lambda e: e.tensor_scalar_mul(out=st_v[:], in0=st_sq[:], scalar1=1.0 / D),
                              waits=[cur("act")])
                            v(lambda e: e.tensor_tensor(out=red_t[:], in0=st_m[:], in1=st_m[:], op=ALU.mult))
                            v(lambda e: e.tensor_sub(out=st_v[:], in0=st_v[:], in1=red_t[:]))
                            v(lambda e: e.tensor_scalar_add(out=st_v[:], in0=st_v[:], scalar1=LN_EPS))
                            newton_rstd()
                            v(lambda e: e.tensor_tensor(out=st_beta[:], in0=st_m[:], in1=st_r[:], op=ALU.mult),
                              waits=[cur("act")])
                            v(lambda e: e.tensor_scalar_mul(out=st_beta[:], in0=st_beta[:], scalar1=-1.0))
                            xpw = v(lambda e: e.tensor_scalar_add(out=xprime[:], in0=xstage[:, 0, :],
                                                                  scalar1=st_beta[:, 0:1]),
                                    waits=[at("dxr", dxr), at("dxw", eps_state["xdw_dma"])])
                            eps_state["xst_dve"] = xpw
                            if isalt:
                                v(lambda e, ts4=ts4: e.scalar_tensor_tensor(
                                    out=y_t[:, ts4, 0:512], in0=psC[:], scalar=st_r[:, 0:1],
                                    in1=xprime[:, 0:512], op0=ALU.mult, op1=ALU.add))
                                yw = v(lambda e, ts4=ts4: e.scalar_tensor_tensor(
                                    out=y_t[:, ts4, 512:1024], in0=psD[:], scalar=st_r[:, 0:1],
                                    in1=xprime[:, 512:1024], op0=ALU.mult, op1=ALU.add))
                                cdst["dve"] = yw
                            else:
                                yw = v(lambda e, ts4=ts4: e.scalar_tensor_tensor(
                                    out=y_t[:, ts4, :], in0=psE[:], scalar=st_r[:, 0:1],
                                    in1=xprime[:],
                                    op0=ALU.mult, op1=ALU.add))
                                eps_state["psE_dve"] = yw
                            if ts4 == 3:
                                partA_post(ts4, row0, yw)
                                return None
                            return lambda: partA_post(ts4, row0, yw)

                        def partA_post(ts4, row0, yw):
                            v(lambda e, ts4=ts4: e.tensor_reduce(out=st_sum[:], in_=y_t[:, ts4, :],
                                                                 axis=AX.X, op=ALU.add))
                            sqb = a(lambda e, ts4=ts4: e.activation(out=ep_sq, in_=y_t[:, ts4, :],
                                                                    func=AF.Square, accum_out=st_sq[:]),
                                    waits=[at("dve", yw)])
                            v(lambda e: e.tensor_scalar_mul(out=st_m[:], in0=st_sum[:], scalar1=1.0 / D))
                            v(lambda e: e.tensor_scalar_mul(out=st_v[:], in0=st_sq[:], scalar1=1.0 / D),
                              waits=[at("act", sqb)])
                            v(lambda e: e.tensor_tensor(out=red_t[:], in0=st_m[:], in1=st_m[:], op=ALU.mult))
                            v(lambda e: e.tensor_sub(out=st_v[:], in0=st_v[:], in1=red_t[:]))
                            v(lambda e: e.tensor_scalar_add(out=st_v[:], in0=st_v[:], scalar1=LN_EPS))
                            newton_rstd()
                            v(lambda e, ts4=ts4: e.tensor_scalar(out=ln2_t[:], in0=y_t[:, ts4, :],
                                                                 scalar1=st_m[:, 0:1], scalar2=st_r[:, 0:1],
                                                                 op0=ALU.subtract, op1=ALU.mult),
                              waits=[cur("act"), at("out", ep_out_mark[0])])
                            st_guard[0] = w.ctr["dve"]
                            v(lambda e: e.tensor_reduce(
                                out=bs_max[:, :, None], in_=ln2_t[:].rearrange("p (b k) -> p b k", k=32),
                                axis=AX.X, op=ALU.max, apply_absolute_value=True))
                            v(lambda e: e.tensor_scalar(out=bs_s[:], in0=bs_max[:], scalar1=1.0 / 7.5,
                                                        scalar2=1e-30, op0=ALU.mult, op1=ALU.max))
                            v(lambda e: e.tensor_scalar_mul(out=bs_s64[:], in0=bs_s[:], scalar1=64.0))
                            v(lambda e: e.reciprocal(out=bs_max[:], in_=bs_s64[:]))
                            v(lambda e: e.tensor_tensor(
                                out=xprime[:].rearrange("p (b k) -> p b k", k=32),
                                in0=ln2_t[:].rearrange("p (b k) -> p b k", k=32),
                                in1=bs_max[:, :, None].broadcast_to([128, 32, 32]), op=ALU.mult))
                            v(lambda e: e.tensor_copy(out=y8_t, in_=xprime[:]))
                            xdw = v(lambda e: e.tensor_tensor(
                                out=xd_tm.rearrange("p (b k) -> p b k", k=32),
                                in0=y8_t.rearrange("p (b k) -> p b k", k=32),
                                in1=bs_s64[:, :, None].broadcast_to([128, 32, 32]), op=ALU.mult))
                            eps_state["xdw_dma"] = w.op("sync", lambda e, row0=row0: e.dma_start(
                                out=xd_s[row0:row0 + 128, :], in_=xd_tm),
                                [at("dve", xdw)], "dxw", 16)

                        def partB(ts4):
                            row0 = tqb * 512 + ts4 * 128
                            pe_ffn = w.ctr["pe"]
                            isalt = alt and ts4 % 2 == 1
                            pslo = psC[:] if isalt else psE[:, 0:512]
                            pshi = psD[:] if isalt else psE[:, 512:1024]
                            if isalt:
                                a(lambda e: e.activation(out=ep_sq[:, 0:512], in_=psC[:],
                                                         func=AF.Square, accum_out=st_sq[:]),
                                  waits=[at("pe", pe_ffn), at("dve", st_guard[0])])
                                sqc = a(lambda e: e.activation(out=ep_sq[:, 512:1024], in_=psD[:],
                                                               func=AF.Square, accum_out=st_t[:]))
                                cdst["act"] = sqc
                            else:
                                sqc = a(lambda e: e.activation(out=ep_sq, in_=psE[:],
                                                               func=AF.Square,
                                                               accum_out=st_sq[:]),
                                        waits=[at("pe", pe_ffn), at("dve", st_guard[0])])
                                eps_state["psE_act"] = sqc
                            v(lambda e, pslo=pslo: e.tensor_reduce(out=st_sum[:], in_=pslo,
                                                                   axis=AX.X, op=ALU.add),
                              waits=[at("pe", pe_ffn), at("act", sqc)])
                            v(lambda e, pshi=pshi: e.tensor_reduce(out=st_sum2[:], in_=pshi,
                                                                   axis=AX.X, op=ALU.add))
                            if isalt:
                                v(lambda e: e.tensor_add(out=st_sq[:], in0=st_sq[:], in1=st_t[:]))
                            v(lambda e: e.tensor_add(out=st_sum[:], in0=st_sum[:], in1=st_sum2[:]))
                            v(lambda e: e.tensor_scalar_mul(out=st_m[:], in0=st_sum[:], scalar1=1.0 / D))
                            v(lambda e: e.tensor_scalar_mul(out=st_v[:], in0=st_sq[:], scalar1=1.0 / D),
                              waits=[cur("act")])
                            v(lambda e: e.tensor_tensor(out=red_t[:], in0=st_m[:], in1=st_m[:], op=ALU.mult))
                            v(lambda e: e.tensor_sub(out=st_v[:], in0=st_v[:], in1=red_t[:]))
                            v(lambda e: e.tensor_scalar_add(out=st_v[:], in0=st_v[:], scalar1=LN_EPS))
                            return lambda: partB_post(ts4, row0)

                        def partB_post(ts4, row0):
                            isalt = alt and ts4 % 2 == 1
                            newton_rstd()
                            v(lambda e: e.tensor_tensor(out=st_beta[:], in0=st_m[:], in1=st_r[:], op=ALU.mult),
                              waits=[cur("act")])
                            v(lambda e: e.tensor_scalar_mul(out=st_beta[:], in0=st_beta[:], scalar1=-1.0))
                            v(lambda e, ts4=ts4: e.tensor_scalar_add(out=xprime[:], in0=y_t[:, ts4, :],
                                                                     scalar1=st_beta[:, 0:1]))
                            if isalt:
                                v(lambda e: e.scalar_tensor_tensor(
                                    out=ln2_t[:, 0:512], in0=psC[:], scalar=st_r[:, 0:1],
                                    in1=xprime[:, 0:512], op0=ALU.mult, op1=ALU.add),
                                    waits=[at("out", ep_out_mark[0])])
                                ow = v(lambda e: e.scalar_tensor_tensor(
                                    out=ln2_t[:, 512:1024], in0=psD[:], scalar=st_r[:, 0:1],
                                    in1=xprime[:, 512:1024], op0=ALU.mult, op1=ALU.add))
                                cdst["dve"] = ow
                            else:
                                ow = v(lambda e: e.scalar_tensor_tensor(
                                    out=ln2_t[:], in0=psE[:], scalar=st_r[:, 0:1], in1=xprime[:],
                                    op0=ALU.mult, op1=ALU.add),
                                    waits=[at("out", ep_out_mark[0])])
                                eps_state["psE_dve"] = ow
                            st_guard[0] = w.ctr["dve"]
                            ep_out_mark[0] = go(lambda e, s=s, row0=row0: e.dma_start(
                                out=out_d[s, row0:row0 + 128, :], in_=ln2_t[:]),
                                waits=[at("dve", ow)])

                        def mm_waits(k):
                            if alt and k % 2 == 1:
                                if cdst["dve"] == 0:
                                    return [at("dve", max(attn_state["norm"],
                                                          norm_done.get(tqb, 0))),
                                            at("dma", wready[1]),
                                            at("act", attn_state["div"])]
                                return [at("dve", max(cdst["dve"], norm_done.get(tqb, 0))),
                                        at("dma", wready[1]), at("act", cdst["act"])]
                            return [at("dve", max(eps_state["psE_dve"],
                                                  norm_done.get(tqb, 0))),
                                    at("dma", wready[1]),
                                    at("act", eps_state["psE_act"])]

                        def a_mms(k, fb):
                            out = []
                            for c in range(NC8):
                                out.append(lambda c=c, k=k, fb=fb: t(
                                    lambda e, fb=fb, c=c, ts4=k: e.matmul(
                                        tgt(k, fb),
                                        out_all[:, par, c, 128 * ts4:128 * (ts4 + 1)],
                                        slotA[:, c, 512 * fb:512 * (fb + 1)],
                                        start=(c == 0), stop=False),
                                    waits=(mm_waits(k) if (fb == 0 and c == 0) else ())))
                            out.append(lambda k=k, fb=fb: t(lambda e, k=k, fb=fb: e.matmul(
                                tgt(k, fb),
                                ones1b[:], bo_bf[0:1, 512 * fb:512 * (fb + 1)],
                                start=False, stop=True)))
                            return out

                        def b_mms(k, fb):
                            out = []

                            def first(k=k, fb=fb):
                                if k == 0 and fb == 0:
                                    w.op("sync", lambda e: None,
                                         [cur("dma"), cur("dxw"),
                                          at("pe", eps_state["ffn_pe"])], None)
                                    for c2 in range(NC8):
                                        w.op("sync", lambda e, c2=c2: e.dma_start(
                                            out=xd_fm[:, c2, :],
                                            in_=xd_s[tqb * 512:(tqb + 1) * 512,
                                                     128 * c2:128 * (c2 + 1)],
                                            transpose=True), (), "dtx", 16)
                                    eps["tpxd"] = w.ctr["dtx"]
                                t(lambda e, fb=fb, ts4=k, k=k: e.matmul(
                                    tgt(k, fb),
                                    xd_fm[:, 0, 128 * ts4:128 * (ts4 + 1)],
                                    slotB[:, 0, 512 * fb:512 * (fb + 1)],
                                    start=True, stop=False),
                                  waits=(([at("dtx", eps["tpxd"])] + mm_waits(k))
                                         if fb == 0 else ()))
                            out.append(first)
                            for c in range(1, NC8):
                                out.append(lambda c=c, k=k, fb=fb: t(
                                    lambda e, fb=fb, c=c, ts4=k, k=k: e.matmul(
                                        tgt(k, fb),
                                        xd_fm[:, c, 128 * ts4:128 * (ts4 + 1)],
                                        slotB[:, c, 512 * fb:512 * (fb + 1)],
                                        start=False, stop=False)))
                            out.append(lambda k=k, fb=fb: t(lambda e, k=k, fb=fb: e.matmul(
                                tgt(k, fb),
                                ones1b[:], fb_bf[0:1, 512 * fb:512 * (fb + 1)],
                                start=False, stop=True)))
                            return out

                        ckl = []
                        for k in range(4):
                            ckl.append({"mms": a_mms(k, 0) + a_mms(k, 1),
                                        "chain": (lambda k=k: partA(k))})
                        for k in range(4):
                            ckl.append({"mms": b_mms(k, 0) + b_mms(k, 1),
                                        "chain": (lambda k=k: partB(k)),
                                        "late": k == 0})
                        return ckl

                    if _PHASE >= 6:
                        chunks = make_ep_chunks(alt=(tqb == 3))
                        if tqb == 3:
                            if attn_state["pending_div"] is not None:
                                attn_state["pending_div"]()
                                attn_state["pending_div"] = None
                            if pend_chain2[0] is not None:
                                pend_chain2[0]()
                                pend_chain2[0] = None
                            for lst in pend_chunks:
                                for ch in lst:
                                    for f in ch["mms"]:
                                        f()
                                    c2 = ch["chain"]()
                                    if c2 is not None:
                                        c2()
                            pend_chunks.clear()
                            for ch in chunks:
                                for f in ch["mms"]:
                                    f()
                                c2 = ch["chain"]()
                                if c2 is not None:
                                    c2()
                        else:
                            pend_chunks.append(chunks)
                    prev_ep_pe = w.ctr["pe"]
                while pa_cp_q:
                    pa_cp_q.pop(0)()
                if attn_state["pending_div"] is not None:
                    attn_state["pending_div"]()
                    attn_state["pending_div"] = None

            w.final_wait("gpsimd", "out")

        for en in ENGINES:
            def make_body(en):
                def body(eng):
                    wk = Walk(en, eng, sems)
                    schedule(wk)
                    _CTR_FOR_DEBUG.update(wk.ctr)
                return body
            getattr(block, en)(make_body(en))
        _SEMS_FOR_DEBUG.update(sems)

    return nc


def _host_constants():
    import ml_dtypes
    inv = 1.0 / (10000.0 ** (np.arange(0, DH, 2, dtype=np.float64) / DH))
    pos = np.arange(SEQ, dtype=np.float64)
    fr = pos[:, None] * inv[None, :]
    emb = np.concatenate([fr, fr], axis=1)
    cos = np.cos(emb)
    sin = np.sin(emb)
    sinsw = np.concatenate([-sin[:, :32], sin[:, 32:]], axis=1)

    def to_tiles(tab, scale):
        t = (tab * scale).reshape(NT, 128, DH).transpose(1, 0, 2).reshape(128, NT * DH)
        return np.ascontiguousarray(t).astype(ml_dtypes.bfloat16)

    cosk = to_tiles(cos, 1.0)
    sinswk = to_tiles(sinsw, 1.0)

    p = np.arange(128)[:, None]
    c = np.arange(128)[None, :]
    mask = np.ascontiguousarray((c >= p).astype(np.float32)).astype(ml_dtypes.bfloat16)
    return cosk, sinswk, mask


_NC_CACHE = None


def kernel(L0_attn_x, L0_attn_Wq, L0_attn_Wkv, L0_attn_Wo, L0_attn_bo,
           L0_attn_ln_g, L0_attn_ln_b, L0_ffn_weight, L0_ffn_bias):
    global _NC_CACHE
    x = np.ascontiguousarray(np.asarray(L0_attn_x, dtype=np.float32))
    wqT = np.ascontiguousarray(np.asarray(L0_attn_Wq, dtype=np.float32).T)
    wkvT = np.ascontiguousarray(np.asarray(L0_attn_Wkv, dtype=np.float32).T)
    woT = np.ascontiguousarray(np.asarray(L0_attn_Wo, dtype=np.float32).T)
    wf = np.ascontiguousarray(np.asarray(L0_ffn_weight, dtype=np.float32))
    bo = np.ascontiguousarray(np.asarray(L0_attn_bo, dtype=np.float32))
    fb = np.ascontiguousarray(np.asarray(L0_ffn_bias, dtype=np.float32))
    g = np.ascontiguousarray(np.asarray(L0_attn_ln_g, dtype=np.float32))
    bb = np.ascontiguousarray(np.asarray(L0_attn_ln_b, dtype=np.float32))

    cosk, sinswk, mask = _host_constants()

    if _NC_CACHE is None:
        _NC_CACHE = build_nc()
    nc = _NC_CACHE

    in_maps = []
    for i in range(NCORES):
        in_maps.append({
            "x": np.ascontiguousarray(x[B_CORE * i:B_CORE * (i + 1)]),
            "wqT": wqT, "wkvT": wkvT, "woT": woT, "wf": wf,
            "bo": bo, "fb": fb, "lng": g, "lnb": bb,
            "cosk": cosk, "sinswk": sinswk, "mask": mask,
        })
    res = run_bass_kernel_spmd(nc, in_maps, core_ids=list(range(NCORES)))
    out = np.concatenate([np.asarray(res.results[i]["out"]) for i in range(NCORES)], axis=0)
    return out.astype(np.float32)


# version 44
# speedup vs baseline: 1.1782x; 1.1782x over previous
import os
import numpy as np
import contextlib
import concourse.bass as bass
import concourse.mybir as mybir
from concourse.bass_utils import run_bass_kernel_spmd

F32 = mybir.dt.float32
BF16 = mybir.dt.bfloat16
F8 = mybir.dt.float8e4
I32 = mybir.dt.int32
AF = mybir.ActivationFunctionType
ALU = mybir.AluOpType
AX = mybir.AxisListType

NCORES = 8
B_CORE = 2
SEQ = 2048
D = 1024
H = 16
DH = 64
CTX = 1536
LAT = 512
NT = SEQ // 128
NC8 = D // 128
LN_EPS = 1e-5
C_CLAMP = 80.0

ENGINES = ("sync", "tensor", "scalar", "vector", "gpsimd")
_SEMS_FOR_DEBUG = {}
_CTR_FOR_DEBUG = {}
_PHASE = int(os.environ.get("KERN_PHASE", "99"))
_NSEQ = int(os.environ.get("KERN_NSEQ", "2"))
_NTQB = int(os.environ.get("KERN_NTQB", "4"))
_NPAIR = int(os.environ.get("KERN_NPAIR", "8"))
_EP = int(os.environ.get("KERN_EP", "9"))
_EPDELAY = int(os.environ.get("KERN_EPDELAY", "4"))
_EP2 = int(os.environ.get("KERN_EP2", "9"))


class Walk:

    def __init__(self, active, eng, sems):
        self.active = active
        self.eng = eng
        self.sems = sems
        self.ctr = {k: 0 for k in sems}

    AUTO_CHAIN = {"vector": "dve", "scalar": "act"}

    def op(self, ename, fn, waits=(), inc=None, n=1):
        rw = [(s, (self.ctr[s] if v is None else v)) for (s, v) in waits]
        ac = self.AUTO_CHAIN.get(ename)
        if ac is not None and inc == ac:
            rw.append((ac, self.ctr[ac]))
        if self.active == ename:
            for s, v in rw:
                if v > 0:
                    self.eng.wait_ge(self.sems[s], v)
            inst = fn(self.eng)
            if inc is not None and inst is not None:
                inst.then_inc(self.sems[inc], n)
        if inc is not None:
            self.ctr[inc] += n
        return self.ctr[inc] if inc is not None else None

    def final_wait(self, ename, sem):
        v = self.ctr[sem]
        if self.active == ename:
            self.eng.wait_ge(self.sems[sem], v)


def cur(s):
    return (s, None)


def at(s, v):
    return (s, v)


def build_nc():
    nc = bass.Bass("TRN2", target_bir_lowering=False)

    x_d = nc.dram_tensor("x", [B_CORE, SEQ, D], F32, kind="ExternalInput").ap()
    wqT_d = nc.dram_tensor("wqT", [D, D], F32, kind="ExternalInput").ap()
    wkvT_d = nc.dram_tensor("wkvT", [D, D], F32, kind="ExternalInput").ap()
    woT_d = nc.dram_tensor("woT", [D, D], F32, kind="ExternalInput").ap()
    wf_d = nc.dram_tensor("wf", [D, D], F32, kind="ExternalInput").ap()
    bo_d = nc.dram_tensor("bo", [D], F32, kind="ExternalInput").ap()
    fb_d = nc.dram_tensor("fb", [D], F32, kind="ExternalInput").ap()
    g_d = nc.dram_tensor("lng", [DH], F32, kind="ExternalInput").ap()
    bb_d = nc.dram_tensor("lnb", [DH], F32, kind="ExternalInput").ap()
    cosk_d = nc.dram_tensor("cosk", [128, NT * DH], BF16, kind="ExternalInput").ap()
    sinswk_d = nc.dram_tensor("sinswk", [128, NT * DH], BF16, kind="ExternalInput").ap()
    mask_d = nc.dram_tensor("mask", [128, 128], BF16, kind="ExternalInput").ap()
    out_d = nc.dram_tensor("out", [B_CORE, SEQ, D], F32, kind="ExternalOutput").ap()

    _dbg = os.environ.get("KERN_DBG") == "1"
    _ik = "ExternalOutput" if _dbg else "Internal"
    ln1_s = nc.dram_tensor("ln1_s", [SEQ, D], BF16, kind=_ik).ap()
    q_s = nc.dram_tensor("q_s", [SEQ, D], BF16, kind=_ik).ap()
    lkv_s = nc.dram_tensor("lkv_s", [SEQ, D], BF16, kind=_ik).ap()
    xd_s = nc.dram_tensor("xd_s", [SEQ, D], BF16, kind=_ik).ap()
    wdq_s = nc.dram_tensor("wdq_s", [D, D], BF16, kind=_ik).ap()
    wq_b = nc.dram_tensor("wq_b", [NC8, 128, D], BF16, kind=_ik).ap()
    wkv_b = nc.dram_tensor("wkv_b", [NC8, 128, D], BF16, kind=_ik).ap()
    wo_b = nc.dram_tensor("wo_b", [NC8, 128, D], BF16, kind=_ik).ap()
    wd_b = nc.dram_tensor("wd_b", [NC8, 128, D], BF16, kind=_ik).ap()
    if _dbg:
        dbgb_d = nc.dram_tensor("dbgb", [4, 128, D], BF16, kind="ExternalOutput").ap()
        dbgf_d = nc.dram_tensor("dbgf", [4, 128, D], F32, kind="ExternalOutput").ap()
    mq_s = nc.dram_tensor("mq_s", [128], F32, kind="Internal").ap()
    mk_s = nc.dram_tensor("mk_s", [128], F32, kind="Internal").ap()

    with contextlib.ExitStack() as stack:
        E = stack.enter_context
        slotA = E(nc.sbuf_tensor("slotA", [128, NC8, D], BF16))
        slotB = E(nc.sbuf_tensor("slotB", [128, NC8, D], BF16))
        wstage = E(nc.sbuf_tensor("wstage", [128, 2, 512], F32))
        cosk = E(nc.sbuf_tensor("cosk_t", [128, NT, DH], BF16))
        sinswk = E(nc.sbuf_tensor("sinswk_t", [128, NT, DH], BF16))
        masks = E(nc.sbuf_tensor("masks", [128, 128], BF16))
        grep = E(nc.sbuf_tensor("grep", [128, DH], BF16))
        brep = E(nc.sbuf_tensor("brep", [128, DH], BF16))
        gbf32 = E(nc.sbuf_tensor("gbf32", [128, 2, DH], F32))
        bo_bf = E(nc.sbuf_tensor("bo_bf", [1, D], BF16))
        fb_bf = E(nc.sbuf_tensor("fb_bf", [1, D], BF16))
        ones64 = E(nc.sbuf_tensor("ones64", [128, DH], BF16))
        ones1b = E(nc.sbuf_tensor("ones1b", [1, 128], BF16))
        zeros1b = E(nc.sbuf_tensor("zeros1b", [1, 128], BF16))
        ones1f = E(nc.sbuf_tensor("ones1f", [1, 128], F32))

        xstage = E(nc.sbuf_tensor("xstage", [128, 2, D], F32))
        ln1tm = E(nc.sbuf_tensor("ln1tm", [128, 2, D], BF16))
        ropeA = E(nc.sbuf_tensor("ropeA", [128, D], BF16))
        ropeB = E(nc.sbuf_tensor("ropeB", [128, D], BF16))
        lkv_tm = E(nc.sbuf_tensor("lkv_tm", [128, NT, D], BF16))
        lkv_fm = E(nc.sbuf_tensor("lkv_fm", [128, NC8, SEQ], BF16))
        q_fm = E(nc.sbuf_tensor("q_fm", [128, NC8, SEQ], BF16))
        p_buf = E(nc.sbuf_tensor("p_buf", [128, 3, D], BF16))
        out_all = E(nc.sbuf_tensor("out_all", [128, 2, NC8, LAT], BF16))
        div_t = E(nc.sbuf_tensor("div_t", [128, LAT], F32))
        y_t = E(nc.sbuf_tensor("y_t", [128, 4, D], BF16))
        ln2_t = E(nc.sbuf_tensor("ln2_t", [128, D], F32))
        xprime = E(nc.sbuf_tensor("xprime", [128, D], F32))
        xd_fm = E(nc.sbuf_tensor("xd_fm", [128, NC8, LAT], BF16))
        st_sum = E(nc.sbuf_tensor("st_sum", [128, 1], F32))
        st_sum2 = E(nc.sbuf_tensor("st_sum2", [128, 1], F32))
        st_sq = E(nc.sbuf_tensor("st_sq", [128, 1], F32))
        st_m = E(nc.sbuf_tensor("st_m", [128, 1], F32))
        st_v = E(nc.sbuf_tensor("st_v", [128, 1], F32))
        st_r = E(nc.sbuf_tensor("st_r", [128, 1], F32))
        st_beta = E(nc.sbuf_tensor("st_beta", [128, 1], F32))
        st_t = E(nc.sbuf_tensor("st_t", [128, 1], F32))
        hsum = E(nc.sbuf_tensor("hsum", [128, H], F32))
        hsq = E(nc.sbuf_tensor("hsq", [128, H], F32))
        hm = E(nc.sbuf_tensor("hm", [128, H], F32))
        hv = E(nc.sbuf_tensor("hv", [128, H], F32))
        hr = E(nc.sbuf_tensor("hr", [128, H], F32))
        hmb = E(nc.sbuf_tensor("hmb", [128, H], BF16))
        hrb = E(nc.sbuf_tensor("hrb", [128, H], BF16))
        mq_run = E(nc.sbuf_tensor("mq_run", [128, 1], F32))
        mk_run = E(nc.sbuf_tensor("mk_run", [128, 1], F32))
        red_t = E(nc.sbuf_tensor("red_t", [128, 1], F32))
        c_sc = E(nc.sbuf_tensor("c_sc", [1, 4], F32))
        negc = E(nc.sbuf_tensor("negc", [128, 2], F32))
        bs_max = E(nc.sbuf_tensor("bs_max", [128, 32], F32))
        bs_s = E(nc.sbuf_tensor("bs_s", [128, 32], F32))
        bs_s64 = E(nc.sbuf_tensor("bs_s64", [128, 32], F32))
        constv = E(nc.sbuf_tensor("constv", [128, 8], F32))
        mqn2 = E(nc.sbuf_tensor("mqn2", [128, H], F32))
        mkn2 = E(nc.sbuf_tensor("mkn2", [128, H], F32))

        psA = E(nc.psum_tensor("psA", [128, D], F32))
        psB = E(nc.psum_tensor("psB", [128, D], F32))
        psC = E(nc.psum_tensor("psC", [128, 512], F32))
        psD = E(nc.psum_tensor("psD", [128, 512], F32))
        psE = E(nc.psum_tensor("psE", [128, D], F32))

        _cvals = [LN_EPS, -0.5, -1.0, 0.125, 1.0, 0.0, -2.0794415416798357, 0.5]
        for _i, _cv in enumerate(_cvals):
            nc.const_aps.register(F32, _cv, constv[:, _i:_i + 1])

        sem_names = ["dma", "dw0", "dw1", "out", "pe", "act", "dve", "gp", "dtx", "dxr", "dxw",
                     "dln", "dtp", "dfq", "dtq"]
        sems = {k: E(nc.semaphore("s_" + k)) for k in sem_names}
        block = E(nc.Block())

        sq_scr = y_t[:, 0:2, :].bitcast(F32).rearrange("p a b -> p (a b)")
        kvbf = p_buf[:, 0, :]
        c_row = div_t[0:1, 0:256]
        rope_o = xstage[:, 1, 0:512].bitcast(BF16)
        xd_tm = xprime[:, 0:512].bitcast(BF16)
        y8_t = ln2_t[:, 0:256].bitcast(F8)
        ep_sq = xstage[:, 1, :]
        rope_o2 = div_t[:].bitcast(BF16)
        rope_o3 = xstage[:, 1, 512:1024].bitcast(BF16)
        pasq = ropeA[:].bitcast(F32)

        def schedule(w: Walk):
            t = lambda fn, waits=(), inc="pe", n=1: w.op("tensor", fn, waits, inc, n)
            a = lambda fn, waits=(), inc="act", n=1: w.op("scalar", fn, waits, inc, n)
            v = lambda fn, waits=(), inc="dve", n=1: w.op("vector", fn, waits, inc, n)
            y = lambda fn, waits=(), inc="dma", n=16: w.op("sync", fn, waits, inc, n)
            go = lambda fn, waits=(), inc="out", n=16: w.op("gpsimd", fn, waits, inc, n)
            ybar = lambda: w.op("sync", lambda e: None, [cur("dma")], None)

            def newton_rstd(first_waits=()):
                a(lambda e: e.activation(out=st_r[:], in_=st_v[:], func=AF.Ln),
                  waits=[cur("dve")] + list(first_waits))
                a(lambda e: e.activation(out=st_r[:], in_=st_r[:], func=AF.Exp, scale=-0.5))

            wq_done = []
            wcast = []
            last_S = [0]
            pend_chain2 = [None]
            ffn_mark = {0: 0, 1: 0}
            ep_out_mark = [0]

            y(lambda e: e.dma_start(out=cosk[:], in_=cosk_d.rearrange("p (t k) -> p t k", k=DH)))
            y(lambda e: e.dma_start(out=sinswk[:], in_=sinswk_d.rearrange("p (t k) -> p t k", k=DH)))
            y(lambda e: e.dma_start(out=masks[:], in_=mask_d[:]))
            y(lambda e: e.dma_start(out=gbf32[:, 0, :], in_=g_d[None, :].broadcast_to([128, DH])))
            y(lambda e: e.dma_start(out=gbf32[:, 1, :], in_=bb_d[None, :].broadcast_to([128, DH])))
            y(lambda e: e.dma_start(out=wstage[0:1, 0, :], in_=bo_d[None, 0:512]))
            y(lambda e: e.dma_start(out=wstage[0:1, 1, :], in_=bo_d[None, 512:1024]))
            ld0 = y(lambda e: e.dma_start(out=c_row[:], in_=fb_d[None, 0:256]))
            v(lambda e: e.tensor_copy(out=grep[:], in_=gbf32[:, 0, :]), waits=[at("dma", ld0)])
            v(lambda e: e.tensor_copy(out=brep[:], in_=gbf32[:, 1, :]))
            v(lambda e: e.tensor_copy(out=bo_bf[0:1, 0:512], in_=wstage[0:1, 0, :]))
            bcst = v(lambda e: e.tensor_copy(out=bo_bf[0:1, 512:1024], in_=wstage[0:1, 1, :]))
            y(lambda e: e.dma_start(out=wstage[0:1, 0, :], in_=fb_d[None, 0:512]),
              waits=[at("dve", bcst)])
            ldf = y(lambda e: e.dma_start(out=wstage[0:1, 1, :], in_=fb_d[None, 512:1024]))
            v(lambda e: e.tensor_copy(out=fb_bf[0:1, 0:512], in_=wstage[0:1, 0, :]),
              waits=[at("dma", ldf)])
            v(lambda e: e.tensor_copy(out=fb_bf[0:1, 512:1024], in_=wstage[0:1, 1, :]))
            for _i, _cv in enumerate([LN_EPS, -0.5, -1.0, 0.125, 1.0, 0.0, -2.0794415416798357, 0.5]):
                v(lambda e, _i=_i, _cv=_cv: e.memset(constv[:, _i:_i + 1], _cv))
            v(lambda e: e.memset(ones64[:], 1.0))
            v(lambda e: e.memset(ones1b[:], 1.0))
            v(lambda e: e.memset(zeros1b[:], 0.0))
            v(lambda e: e.memset(ones1f[:], 1.0))

            for r in range(8):
                wsfull = wstage[:].rearrange("p a b -> p (a b)")
                dw = y(lambda e, r=r, wsfull=wsfull: e.dma_start(
                    out=wsfull, in_=wf_d[128 * r:128 * (r + 1), :]),
                    waits=[cur("dve") if r == 0 else at("dve", wq_done[r - 1])])
                v(lambda e, wsfull=wsfull: e.tensor_reduce(
                    out=bs_max[:, :, None], in_=wsfull.rearrange("p (b k) -> p b k", k=32),
                    axis=AX.X, op=ALU.max, apply_absolute_value=True),
                  waits=[at("dma", dw)])
                v(lambda e: e.tensor_scalar(out=bs_s[:], in0=bs_max[:], scalar1=1.0 / 7.5, scalar2=1e-30,
                                            op0=ALU.mult, op1=ALU.max))
                v(lambda e: e.tensor_scalar_mul(out=bs_s64[:], in0=bs_s[:], scalar1=64.0))
                v(lambda e: e.reciprocal(out=bs_max[:], in_=bs_s64[:]))
                v(lambda e, wsfull=wsfull: e.tensor_tensor(
                    out=xprime[:].rearrange("p (b k) -> p b k", k=32),
                    in0=wsfull.rearrange("p (b k) -> p b k", k=32),
                    in1=bs_max[:, :, None].broadcast_to([128, 32, 32]), op=ALU.mult))
                v(lambda e: e.tensor_copy(out=y8_t, in_=xprime[:]))
                wq_done.append(
                    v(lambda e: e.tensor_tensor(
                        out=xd_tm.rearrange("p (b k) -> p b k", k=32),
                        in0=y8_t.rearrange("p (b k) -> p b k", k=32),
                        in1=bs_s64[:, :, None].broadcast_to([128, 32, 32]), op=ALU.mult)))
                y(lambda e, r=r: e.dma_start(out=wdq_s[128 * r:128 * (r + 1), :], in_=xd_tm),
                  waits=[at("dve", wq_done[r])])

            for s in range(min(B_CORE, _NSEQ)):
                if _PHASE < 2:
                    break
                def emit_ln1_tp():
                    ybar()
                    w.op("sync", lambda e: None, [cur("dln")], None)
                    for c in range(NC8):
                        w.op("sync", lambda e, c=c: e.dma_start(
                            out=q_fm[:, c, :], in_=ln1_s[:, 128 * c:128 * (c + 1)],
                            transpose=True),
                            ([at("pe", last_S[0])] if (s > 0 and c == 0) else ()),
                            "dtp", 16)
                    return w.ctr["dtp"]

                tp_ln1 = 0
                if s > 0 and _PHASE >= 3:
                    tp_ln1 = emit_ln1_tp()

                if s == 0:
                    wload_q = []
                    for wd_ap, slot, wb in [(wqT_d, slotA, wq_b), (wkvT_d, slotB, wkv_b)]:
                        for c in range(NC8):
                            for hf in range(2):
                                def wload(wd_ap=wd_ap, slot=slot, wb=wb, c=c, hf=hf):
                                    psem = "dw0" if hf == 0 else "dw1"
                                    dw = w.op("sync", lambda e, wd_ap=wd_ap, c=c, hf=hf: e.dma_start(
                                        out=wstage[:, hf, :],
                                        in_=wd_ap[128 * c:128 * (c + 1), 512 * hf:512 * (hf + 1)]),
                                        [at("dve", wcast[-2] if len(wcast) >= 2 else 0)], psem, 16)
                                    wcast.append(
                                        v(lambda e, slot=slot, c=c, hf=hf: e.tensor_copy(
                                            out=slot[:, c, 512 * hf:512 * (hf + 1)],
                                            in_=wstage[:, hf, :]),
                                          waits=[at(psem, dw)]))
                                    go(lambda e, slot=slot, wb=wb, c=c, hf=hf: e.dma_start(
                                        out=wb[c, :, 512 * hf:512 * (hf + 1)],
                                        in_=slot[:, c, 512 * hf:512 * (hf + 1)]),
                                       waits=[at("dve", wcast[-1])])
                                wload_q.append(wload)
                else:
                    wbkv_mark = 0
                    w.op("sync", lambda e: None, [cur("pe"), cur("out")], None)
                    for c in range(NC8):
                        y(lambda e, c=c: e.dma_start(out=slotA[:, c, :], in_=wq_b[c]))
                        y(lambda e, c=c: e.dma_start(out=slotB[:, c, :], in_=wkv_b[c]))
                    ybar()
                    wkvq_ready = ("dma", w.ctr["dma"])

                p1_marks = {}
                for tt in range(NT if (_PHASE >= 3 and s == 0) else 0):
                    sl = tt % 2
                    if tt < 2:
                        xw = [cur("dve"), cur("act")] if s > 0 else []
                    else:
                        wv, wa = p1_marks[("stageA", tt - 2)]
                        xw = [at("dve", wv), at("act", wa)]
                    dx = y(lambda e, s=s, tt=tt, sl=sl: e.dma_start(
                        out=xstage[:, sl, :], in_=x_d[s, 128 * tt:128 * (tt + 1), :]),
                        waits=xw)
                    v(lambda e, sl=sl: e.tensor_reduce(
                        out=st_sum[:], in_=xstage[:, sl, :], axis=AX.X, op=ALU.add),
                      waits=[at("dma", dx)])
                    sqm = a(lambda e, sl=sl: e.activation(
                        out=sq_scr, in_=xstage[:, sl, :], func=AF.Square, accum_out=st_sq[:]),
                        waits=[at("dma", dx)] + ([cur("dve")] if (tt == 0 and s > 0) else []))
                    v(lambda e: e.tensor_scalar_mul(out=st_m[:], in0=st_sum[:], scalar1=1.0 / D))
                    v(lambda e: e.tensor_scalar_mul(out=st_v[:], in0=st_sq[:], scalar1=1.0 / D),
                      waits=[cur("act")])
                    v(lambda e: e.tensor_tensor(out=red_t[:], in0=st_m[:], in1=st_m[:], op=ALU.mult))
                    v(lambda e: e.tensor_sub(out=st_v[:], in0=st_v[:], in1=red_t[:]))
                    a(lambda e: e.activation(out=st_r[:], in_=st_v[:], func=AF.Ln, bias=LN_EPS),
                      waits=[cur("dve")])
                    a(lambda e: e.activation(out=st_r[:], in_=st_r[:], func=AF.Exp, scale=-0.5))
                    lnw = v(lambda e, sl=sl: e.tensor_scalar(
                        out=ln1tm[:, sl, :], in0=xstage[:, sl, :],
                        scalar1=st_m[:, 0:1], scalar2=st_r[:, 0:1],
                        op0=ALU.subtract, op1=ALU.mult),
                        waits=[cur("act"), cur("dma")])
                    y(lambda e, tt=tt, sl=sl: e.dma_start(
                        out=ln1_s[128 * tt:128 * (tt + 1), :], in_=ln1tm[:, sl, :]),
                      waits=[at("dve", lnw)])
                    p1_marks[("stageA", tt)] = (lnw, sqm)
                    for _ in range(2):
                        if wload_q:
                            wload_q.pop(0)()

                if s == 0:
                    while wload_q:
                        wload_q.pop(0)()
                    wkvq_ready = ("dve", wcast[-1])
                    wbkv_mark = w.ctr["out"]
                if _PHASE >= 3 and s == 0:
                    tp_ln1 = emit_ln1_tp()

                qbf = out_all[:, 0, 0:2, :].rearrange("p a b -> p (a b)")
                qropeA = out_all[:, 0, 2:4, :].rearrange("p a b -> p (a b)")
                qropeB = out_all[:, 0, 4:6, :].rearrange("p a b -> p (a b)")
                qrope_o = out_all[:, 0, 6:8, :].rearrange("p a b -> p (a b)")
                NB = NT if _PHASE >= 3 else 0
                for tt in range(NB + 2):
                    if tt < NB:
                        sl = tt % 2
                        kv_ps = psA if sl == 0 else psB
                        q_odd = tt % 2 == 1
                        kvw0 = [at("dtp", tp_ln1),
                                at("act", p1_marks.get(("kvE", tt - 2), 0)),
                                at(*wkvq_ready)]
                        if tt < 2:
                            kvw0.append(cur("act"))
                        for fb in range(2):
                            for c in range(NC8):
                                t(lambda e, fb=fb, c=c, tt=tt, kv_ps=kv_ps: e.matmul(
                                    kv_ps[:, 512 * fb:512 * (fb + 1)],
                                    q_fm[:, c, 128 * tt:128 * (tt + 1)],
                                    slotB[:, c, 512 * fb:512 * (fb + 1)],
                                    start=(c == 0), stop=(c == NC8 - 1)),
                                  waits=(kvw0 if (fb == 0 and c == 0) else ()))
                        pe_kv = w.ctr["pe"]
                        qw0 = [at("act", p1_marks.get(("qE", tt - 2), 0)),
                               at(*wkvq_ready)]
                        if tt < 2 and s > 0:
                            qw0 = [cur("act"), cur("dve"), at(*wkvq_ready)]
                        for fb in range(2):
                            q_ps = (psE[:, 512 * fb:512 * (fb + 1)] if q_odd
                                    else (psC[:] if fb == 0 else psD[:]))
                            for c in range(NC8):
                                t(lambda e, fb=fb, c=c, tt=tt, q_ps=q_ps: e.matmul(
                                    q_ps,
                                    q_fm[:, c, 128 * tt:128 * (tt + 1)],
                                    slotA[:, c, 512 * fb:512 * (fb + 1)],
                                    start=(c == 0), stop=(c == NC8 - 1)),
                                  waits=(qw0 if (fb == 0 and c == 0) else ()))
                        pe_q = w.ctr["pe"]
                        p1_marks[("peq", tt)] = pe_q
                        kvE = a(lambda e, kv_ps=kv_ps: e.activation(
                            out=kvbf, in_=kv_ps[:], func=AF.Identity),
                            waits=[at("pe", pe_kv),
                                   at("dve", p1_marks.get(("kvO2", tt - 1), 0))])
                        p1_marks[("kvE", tt)] = kvE
                        if q_odd:
                            qE = a(lambda e: e.activation(
                                out=qbf, in_=psE[:], func=AF.Identity),
                                waits=[at("pe", pe_q),
                                       at("dve", p1_marks.get(("qO2", tt - 1), 0))])
                        else:
                            a(lambda e: e.activation(
                                out=qbf[:, 0:512], in_=psC[:], func=AF.Identity),
                              waits=[at("pe", pe_q),
                                     at("dve", p1_marks.get(("qO2", tt - 1), 0))])
                            qE = a(lambda e: e.activation(
                                out=qbf[:, 512:1024], in_=psD[:], func=AF.Identity))
                        p1_marks[("qE", tt)] = qE

                    if tt < NB:
                        ro = (rope_o, rope_o2, rope_o3)[tt % 3]
                        p1_marks[("ro", tt)] = ro
                        v(lambda e, tt=tt: e.tensor_tensor(
                            out=ropeB[:].rearrange("p (h k) -> p h k", k=DH),
                            in0=kvbf.rearrange("p (h k) -> p h k", k=DH),
                            in1=cosk[:, tt, None, :].broadcast_to([128, H, DH]), op=ALU.mult),
                          waits=[at("act", kvE)])
                        kvO2 = v(lambda e, tt=tt: e.tensor_tensor(
                            out=ropeA[:].rearrange("p (h two k) -> p h two k", two=2, k=32),
                            in0=kvbf.rearrange("p (h two k) -> p h two k", two=2, k=32)[:, :, ::-1, :],
                            in1=sinswk[:, tt, :].rearrange("p (two k) -> p two k", two=2)[:, None, :, :]
                                .broadcast_to([128, H, 2, 32]),
                            op=ALU.mult))
                        p1_marks[("kvO2", tt)] = kvO2
                        p1_marks[("kvr", tt)] = v(lambda e, ro=ro: e.tensor_add(
                            out=ro, in0=ropeB[:], in1=ropeA[:]),
                            waits=[at("act", p1_marks.get(("hq", tt - 3), 0))])
                        v(lambda e, tt=tt: e.tensor_tensor(
                            out=qropeA.rearrange("p (h k) -> p h k", k=DH),
                            in0=qbf.rearrange("p (h k) -> p h k", k=DH),
                            in1=cosk[:, tt, None, :].broadcast_to([128, H, DH]), op=ALU.mult),
                          waits=[at("act", qE)])
                        qO2 = v(lambda e, tt=tt: e.tensor_tensor(
                            out=qropeB.rearrange("p (h two k) -> p h two k", two=2, k=32),
                            in0=qbf.rearrange("p (h two k) -> p h two k", two=2, k=32)[:, :, ::-1, :],
                            in1=sinswk[:, tt, :].rearrange("p (two k) -> p two k", two=2)[:, None, :, :]
                                .broadcast_to([128, H, 2, 32]),
                            op=ALU.mult))
                        p1_marks[("qO2", tt)] = qO2
                        qoro = (qrope_o, ln1tm[:, 0, :], ln1tm[:, 1, :])[tt % 3]
                        p1_marks[("qoro", tt)] = qoro
                        p1_marks[("qro", tt)] = v(lambda e, qoro=qoro: e.tensor_add(
                            out=qoro, in0=qropeA, in1=qropeB),
                            waits=([at("dma", p1_marks[("dq", tt - 3)])]
                                   if ("dq", tt - 3) in p1_marks else [])
                                  + ([at("act", p1_marks[("qsq", tt - 3)])]
                                     if ("qsq", tt - 3) in p1_marks else []))
                    if 2 <= tt <= NB + 1:
                        u = tt - 2
                        hsqr = v(lambda e: e.tensor_reduce(
                            out=hsq[:, :, None], in_=sq_scr.rearrange("p (h k) -> p h k", k=DH),
                            axis=AX.X, op=ALU.add),
                            waits=[at("act", p1_marks[("hq", u)])])
                        v(lambda e: e.tensor_scalar_mul(out=hm[:], in0=hsum[:], scalar1=1.0 / DH))
                        v(lambda e: e.tensor_scalar_mul(out=hv[:], in0=hsq[:], scalar1=1.0 / DH))
                        v(lambda e: e.tensor_tensor(out=hsq[:], in0=hm[:], in1=hm[:], op=ALU.mult))
                        v(lambda e: e.tensor_sub(out=hv[:], in0=hv[:], in1=hsq[:]))
                        a(lambda e: e.activation(out=hr[:], in_=hv[:], func=AF.Ln, bias=LN_EPS),
                          waits=[cur("dve")])
                        a(lambda e: e.activation(out=hr[:], in_=hr[:], func=AF.Exp, scale=-0.5))
                        v(lambda e: e.tensor_copy(out=hmb[:], in_=hm[:]), waits=[cur("act")])
                        v(lambda e: e.tensor_copy(out=hrb[:], in_=hr[:]))
                        v(lambda e, u=u: e.tensor_sub(
                            out=ropeA[:].rearrange("p (h k) -> p h k", k=DH),
                            in0=p1_marks[("ro", u)].rearrange("p (h k) -> p h k", k=DH),
                            in1=hmb[:, :, None].broadcast_to([128, H, DH])))
                        v(lambda e: e.tensor_tensor(
                            out=ropeA[:].rearrange("p (h k) -> p h k", k=DH),
                            in0=ropeA[:].rearrange("p (h k) -> p h k", k=DH),
                            in1=hrb[:, :, None].broadcast_to([128, H, DH]), op=ALU.mult))
                        v(lambda e: e.tensor_tensor(
                            out=ropeA[:].rearrange("p (h k) -> p h k", k=DH),
                            in0=ropeA[:].rearrange("p (h k) -> p h k", k=DH),
                            in1=grep[:, None, :].broadcast_to([128, H, DH]), op=ALU.mult))
                        lkvw = v(lambda e, u=u: e.tensor_tensor(
                            out=lkv_tm[:, u, :].rearrange("p (h k) -> p h k", k=DH),
                            in0=ropeA[:].rearrange("p (h k) -> p h k", k=DH),
                            in1=brep[:, None, :].broadcast_to([128, H, DH]), op=ALU.add),
                            waits=([cur("pe")] if (s > 0 and u == 0) else ()))
                        y(lambda e, u=u: e.dma_start(
                            out=lkv_s[128 * u:128 * (u + 1), :], in_=lkv_tm[:, u, :]),
                          waits=[at("dve", lkvw)])
                        ksq = a(lambda e, u=u: e.activation(out=sq_scr, in_=lkv_tm[:, u, :],
                                                            func=AF.Square),
                                waits=[at("dve", lkvw)])
                        knr = v(lambda e: e.tensor_reduce(
                            out=hsq[:, :, None], in_=sq_scr.rearrange("p (h k) -> p h k", k=DH),
                            axis=AX.X, op=ALU.add),
                            waits=[at("act", ksq)])
                        if u == 0:
                            v(lambda e: e.tensor_copy(out=mkn2[:], in_=hsq[:]))
                        else:
                            v(lambda e: e.tensor_max(out=mkn2[:], in0=mkn2[:], in1=hsq[:]))
                        qsq = a(lambda e, u=u: e.activation(
                            out=sq_scr[:], in_=p1_marks[("qoro", u)], func=AF.Square),
                            waits=[at("dve", p1_marks[("qro", u)]), at("dve", knr)])
                        p1_marks[("qsq", u)] = qsq
                        qnr = v(lambda e: e.tensor_reduce(
                            out=hsq[:, :, None], in_=sq_scr.rearrange("p (h k) -> p h k", k=DH),
                            axis=AX.X, op=ALU.add),
                            waits=[at("act", qsq)])
                        p1_marks[("qnr", u)] = qnr
                        if u == 0:
                            v(lambda e: e.tensor_copy(out=mqn2[:], in_=hsq[:]))
                        else:
                            v(lambda e: e.tensor_max(out=mqn2[:], in0=mqn2[:], in1=hsq[:]))
                        p1_marks[("dq", u)] = y(lambda e, u=u: e.dma_start(
                            out=q_s[128 * u:128 * (u + 1), :], in_=p1_marks[("qoro", u)]),
                            waits=[at("dve", p1_marks[("qro", u)])])

                    if 1 <= tt <= NB:
                        u1 = tt - 1
                        v(lambda e, u1=u1: e.tensor_reduce(
                            out=hsum[:, :, None],
                            in_=p1_marks[("ro", u1)].rearrange("p (h k) -> p h k", k=DH),
                            axis=AX.X, op=ALU.add))
                        p1_marks[("hq", u1)] = a(lambda e, u1=u1: e.activation(
                            out=sq_scr, in_=p1_marks[("ro", u1)], func=AF.Square),
                            waits=[at("dve", p1_marks[("kvr", u1)]),
                                   at("dve", p1_marks.get(("qnr", u1 - 1), 0))])

                peB = w.ctr["pe"]

                tp_pair = {}
                if _PHASE >= 3:
                    ybar()
                    for c in range(NC8):
                        psem = "dtq" if c >= 4 else "dtp"
                        w.op("sync", lambda e, c=c: e.dma_start(
                            out=lkv_fm[:, c, :], in_=lkv_s[:, 128 * c:128 * (c + 1)],
                            transpose=True),
                            ([at("pe", peB)] if c == 0 else ()), psem, 16)
                        mk = w.op("sync", lambda e, c=c: e.dma_start(
                            out=q_fm[:, c, :], in_=q_s[:, 128 * c:128 * (c + 1)],
                            transpose=True), (), psem, 16)
                        if c in (3, 7):
                            for c2 in range(c - 3, c + 1):
                                tp_pair[c2] = (psem, mk)

                ybar()

                if _PHASE < 4:
                    continue
                v(lambda e: e.tensor_reduce(out=mq_run[:], in_=mqn2[:], axis=AX.X, op=ALU.max))
                v(lambda e: e.tensor_reduce(out=mk_run[:], in_=mkn2[:], axis=AX.X, op=ALU.max))
                dmq = w.op("gpsimd", lambda e: e.dma_start(out=mq_s[:], in_=mq_run[:, 0]),
                           [cur("dve")], "dfq", 16)
                dmk = w.op("gpsimd", lambda e: e.dma_start(out=mk_s[:], in_=mk_run[:, 0]),
                           (), "dfq", 16)
                w.op("gpsimd", lambda e: e.dma_start(out=c_row[0:1, 0:128], in_=mq_s[None, :]),
                     [at("dfq", dmk)], "dfq", 16)
                dcr = w.op("gpsimd", lambda e: e.dma_start(out=c_row[0:1, 128:256], in_=mk_s[None, :]),
                           (), "dfq", 16)
                v(lambda e: e.tensor_reduce(out=c_sc[:, 0:1], in_=c_row[0:1, 0:128], axis=AX.X, op=ALU.max),
                  waits=[at("dfq", dcr)])
                v(lambda e: e.tensor_reduce(out=c_sc[:, 1:2], in_=c_row[0:1, 128:256], axis=AX.X, op=ALU.max))
                a(lambda e: e.activation(out=c_sc[:, 0:1], in_=c_sc[:, 0:1], func=AF.Ln),
                  waits=[cur("dve")])
                a(lambda e: e.activation(out=c_sc[:, 1:2], in_=c_sc[:, 1:2], func=AF.Ln))
                v(lambda e: e.tensor_add(out=c_sc[:, 2:3], in0=c_sc[:, 0:1], in1=c_sc[:, 1:2]),
                  waits=[cur("act")])
                a(lambda e: e.activation(out=c_sc[:, 2:3], in_=c_sc[:, 2:3], func=AF.Exp,
                                         scale=0.5, bias=-2.0794415416798357),
                  waits=[cur("dve")])
                cst = v(lambda e: e.tensor_scalar(out=c_sc[:, 3:4], in0=c_sc[:, 2:3],
                                                  scalar1=-1.0, scalar2=-C_CLAMP,
                                                  op0=ALU.mult, op1=ALU.max),
                        waits=[cur("act")])
                cbc = t(lambda e: e.matmul(psC[:, 0:1], ones1f[:], c_sc[:, 3:4],
                                           start=True, stop=True),
                        waits=[at("dve", cst)])
                stab = a(lambda e, s=s: e.activation(out=negc[:, s:s + 1], in_=psC[:, 0:1],
                                                     func=AF.Identity),
                         waits=[at("pe", cbc)])

                ybar()
                if s == 0:
                    for c in range(NC8):
                        for hf in range(2):
                            psem = "dw0" if hf == 0 else "dw1"
                            dw = w.op("sync", lambda e, c=c, hf=hf: e.dma_start(
                                out=wstage[:, hf, :],
                                in_=woT_d[128 * c:128 * (c + 1), 512 * hf:512 * (hf + 1)]),
                                [at("dve", wcast[-2] if len(wcast) >= 2 else 0)], psem, 16)
                            wcast.append(
                                v(lambda e, c=c, hf=hf: e.tensor_copy(
                                    out=slotA[:, c, 512 * hf:512 * (hf + 1)],
                                    in_=wstage[:, hf, :]),
                                  waits=[at(psem, dw), at("pe", pe_q),
                                         at("out", wbkv_mark)]))
                            go(lambda e, c=c, hf=hf: e.dma_start(
                                out=wo_b[c, :, 512 * hf:512 * (hf + 1)],
                                in_=slotA[:, c, 512 * hf:512 * (hf + 1)]),
                               waits=[at("dve", wcast[-1])])
                else:
                    w.op("sync", lambda e: None, [at("pe", pe_q), cur("out")], None)
                    for c in range(NC8):
                        y(lambda e, c=c: e.dma_start(out=slotA[:, c, :], in_=wo_b[c]))
                if s == 0:
                    for r in range(8):
                        for c in range(NC8):
                            y(lambda e, r=r, c=c: e.dma_start(
                                out=slotB[:, c, 128 * r:128 * (r + 1)],
                                in_=wdq_s[128 * r:128 * (r + 1), 128 * c:128 * (c + 1)],
                                transpose=True),
                              waits=([at("pe", pe_kv), at("out", wbkv_mark)]
                                     if (r == 0 and c == 0) else ()))
                    ybar()
                    wdm = w.ctr["dma"]
                    for c in range(NC8):
                        go(lambda e, c=c: e.dma_start(out=wd_b[c], in_=slotB[:, c, :]),
                           waits=([at("dma", wdm)] if c == 0 else ()))
                else:
                    w.op("sync", lambda e: None, [at("pe", pe_kv), cur("out")], None)
                    for c in range(NC8):
                        y(lambda e, c=c: e.dma_start(out=slotB[:, c, :], in_=wd_b[c]))
                    ybar()
                wready = (w.ctr["dve"], w.ctr["dma"])

                if _PHASE < 5:
                    continue
                exp_marks = {0: w.ctr["act"], 1: w.ctr["act"]}
                prev_ep_pe = w.ctr["pe"]
                pend_chunks = [[{"mms": [], "chain": (lambda: None)}
                                for _ in range(max(1, _EPDELAY // 4))]]
                eps_state = {"psE_dve": w.ctr["dve"], "psE_act": w.ctr["act"],
                             "ffn_pe": w.ctr["pe"], "xst_dve": w.ctr["dve"],
                             "xdw_dma": w.ctr["dxw"]}
                attn_state = {"norm": w.ctr["dve"], "div": w.ctr["act"], "pending_div": None}
                norm_done = {}
                S_pre = {}
                st_guard = [w.ctr["dve"]]
                pa_lndma = {}
                len_cp_done = [0]

                pa_dxm = {}

                def mk_pa_dx(tt, s2):
                    def thunk():
                        pa_dxm[tt] = w.op("sync", lambda e, tt=tt: e.dma_start(
                            out=xstage[:, 0, :], in_=x_d[s2, 128 * tt:128 * (tt + 1), :]),
                            [at("dve", eps_state["xst_dve"])], "dxr", 16)
                        eps_state["xst_dve"] = max(eps_state["xst_dve"], 0)
                    return thunk

                def mk_pa_cp(tt, s2):
                    def thunk():
                        sl = tt % 2
                        dx = pa_dxm[tt]
                        v(lambda e: e.tensor_reduce(
                            out=st_sum[:], in_=xstage[:, 0, :], axis=AX.X, op=ALU.add),
                          waits=[at("dxr", dx), at("dve", st_guard[0])])
                        sqm = a(lambda e: e.activation(
                            out=ep_sq, in_=xstage[:, 0, :], func=AF.Square, accum_out=st_sq[:]),
                            waits=[at("dxr", dx), at("dve", st_guard[0])])
                        v(lambda e: e.tensor_scalar_mul(out=st_m[:], in0=st_sum[:], scalar1=1.0 / D))
                        v(lambda e: e.tensor_scalar_mul(out=st_v[:], in0=st_sq[:], scalar1=1.0 / D),
                          waits=[at("act", sqm)])
                        v(lambda e: e.tensor_tensor(out=red_t[:], in0=st_m[:], in1=st_m[:],
                                                    op=ALU.mult))
                        v(lambda e: e.tensor_sub(out=st_v[:], in0=st_v[:], in1=red_t[:]))
                        v(lambda e: e.tensor_scalar_add(out=st_v[:], in0=st_v[:], scalar1=LN_EPS))
                        v(lambda e: e.tensor_scalar(out=st_t[:].bitcast(I32), in0=st_v[:].bitcast(I32),
                                                    scalar1=1, scalar2=None,
                                                    op0=ALU.logical_shift_right))
                        v(lambda e: e.tensor_scalar(out=st_t[:].bitcast(I32), in0=st_t[:].bitcast(I32),
                                                    scalar1=0, scalar2=None, op0=ALU.bitwise_not))
                        v(lambda e: e.tensor_scalar(out=st_r[:].bitcast(I32), in0=st_t[:].bitcast(I32),
                                                    scalar1=0x5F3759DF + 1, scalar2=None, op0=ALU.add))
                        for _ in range(3):
                            v(lambda e: e.tensor_tensor(out=red_t[:], in0=st_r[:], in1=st_r[:],
                                                        op=ALU.mult))
                            v(lambda e: e.tensor_tensor(out=red_t[:], in0=red_t[:], in1=st_v[:],
                                                        op=ALU.mult))
                            v(lambda e: e.tensor_scalar(out=red_t[:], in0=red_t[:],
                                                        scalar1=-0.5, scalar2=1.5,
                                                        op0=ALU.mult, op1=ALU.add))
                            v(lambda e: e.tensor_tensor(out=st_r[:], in0=st_r[:], in1=red_t[:],
                                                        op=ALU.mult))
                        lnw = v(lambda e, sl=sl: e.tensor_scalar(
                            out=ln1tm[:, sl, :], in0=xstage[:, 0, :],
                            scalar1=st_m[:, 0:1], scalar2=st_r[:, 0:1],
                            op0=ALU.subtract, op1=ALU.mult),
                            waits=([at("dln", pa_lndma[tt - 1])]
                                   if tt - 1 in pa_lndma else []))
                        eps_state["xst_dve"] = lnw
                        st_guard[0] = lnw
                        pa_lndma[tt] = w.op("sync", lambda e, tt=tt, sl=sl: e.dma_start(
                            out=ln1_s[128 * tt:128 * (tt + 1), :], in_=ln1tm[:, sl, :]),
                            [at("dve", lnw)], "dln", 16)
                    return thunk

                if s + 1 < min(B_CORE, _NSEQ) and _PHASE >= 3:
                    pa_dx_q = [mk_pa_dx(tt, s + 1) for tt in range(NT)]
                    pa_cp_q = [mk_pa_cp(tt, s + 1) for tt in range(NT)]
                else:
                    pa_dx_q, pa_cp_q = [], []
                for tqb in range(min(4, _NTQB)):
                    if tqb == 0:
                        tks = list(range(NT))
                        qc0 = CTX
                    else:
                        tks = list(range(12))
                        qc0 = (tqb - 1) * 512
                    for pair in range(min(8, _NPAIR)):
                        hA, hB = 2 * pair, 2 * pair + 1

                        def emit_S_for(pair2, qc2, kt, extra=()):
                            sl = kt % 2
                            s_ps = psA if sl == 0 else psB
                            ao = 128 * (kt - 12) if (tqb == 0 and kt >= 12) else 0
                            sw = [at("act", exp_marks[sl]), at(*tp_pair[pair2])] + list(extra)
                            t(lambda e, pair2=pair2, kt=kt, s_ps=s_ps, qc2=qc2, ao=ao: e.matmul(
                                s_ps[:, ao:512],
                                lkv_fm[0:64, pair2, 128 * kt:128 * (kt + 1)],
                                q_fm[0:64, pair2, qc2 + ao:qc2 + 512],
                                start=True, stop=True, tile_position=(0, 0)),
                              waits=sw)
                            pes = t(lambda e, pair2=pair2, kt=kt, s_ps=s_ps, qc2=qc2, ao=ao: e.matmul(
                                s_ps[:, 512 + ao:1024],
                                lkv_fm[64:128, pair2, 128 * kt:128 * (kt + 1)],
                                q_fm[64:128, pair2, qc2 + ao:qc2 + 512],
                                start=True, stop=True, tile_position=(64, 0)))
                            last_S[0] = pes
                            return pes

                        def emit_S(kt, extra=()):
                            if (pair, kt) in S_pre:
                                return S_pre.pop((pair, kt))
                            return emit_S_for(pair, qc0, kt, extra)

                        def emit_av(kt, psl, need, ao=0, hA=hA, hB=hB):
                            t(lambda e, kt=kt, hA=hA, psl=psl, ao=ao: e.matmul(
                                psC[0:64, ao:512], lkv_tm[:, kt, DH * hA:DH * (hA + 1)],
                                p_buf[:, psl, ao:512],
                                start=False, stop=False, tile_position=(0, 0)),
                              waits=need)
                            t(lambda e, kt=kt, hB=hB, psl=psl, ao=ao: e.matmul(
                                psC[64:128, ao:512], lkv_tm[:, kt, DH * hB:DH * (hB + 1)],
                                p_buf[:, psl, 512 + ao:1024],
                                start=False, stop=False, tile_position=(0, 64)))
                            t(lambda e, psl=psl, ao=ao: e.matmul(
                                psD[0:64, ao:512], ones64[:, 0:64], p_buf[:, psl, ao:512],
                                start=False, stop=False, tile_position=(0, 0)))
                            t(lambda e, psl=psl, ao=ao: e.matmul(
                                psD[64:128, ao:512], ones64[:, 0:64], p_buf[:, psl, 512 + ao:1024],
                                start=False, stop=False, tile_position=(0, 64)))

                        pes_next = emit_S(tks[0])
                        pend_av = []
                        for i, kt in enumerate(tks):
                            sl = kt % 2
                            psl = kt % 3
                            s_ps = psA if sl == 0 else psB
                            pes = pes_next
                            ao_e = 128 * (kt - 12) if (tqb == 0 and kt >= 12) else 0
                            em = a(lambda e, psl=psl, s_ps=s_ps, s=s, ao_e=ao_e: e.activation(
                                out=p_buf[:, psl, :].rearrange(
                                    "p (two c) -> p two c", two=2)[:, :, ao_e:],
                                in_=s_ps[:].rearrange(
                                    "p (two c) -> p two c", two=2)[:, :, ao_e:],
                                func=AF.Exp,
                                bias=negc[:, s:s + 1], scale=0.125),
                                waits=[at("pe", pes)])
                            exp_marks[sl] = em
                            need = [at("act", em)]
                            a_off = 128 * (kt - 12) if (tqb == 0 and kt >= 12) else 0
                            if tqb == 0 and kt >= 12:
                                mm = v(lambda e, psl=psl, a_off=a_off: e.tensor_tensor(
                                    out=p_buf[:, psl, :].rearrange("p (two c) -> p two c", two=2)[:, :, a_off:a_off + 128],
                                    in0=p_buf[:, psl, :].rearrange("p (two c) -> p two c", two=2)[:, :, a_off:a_off + 128],
                                    in1=masks[:, None, :].broadcast_to([128, 2, 128]),
                                    op=ALU.mult),
                                    waits=[at("act", em)])
                                need.append(at("dve", mm))
                            if i == 2:
                                if attn_state["pending_div"] is not None:
                                    attn_state["pending_div"]()
                                    attn_state["pending_div"] = None
                                zw = [at("dve", attn_state["norm"]), at("act", attn_state["div"])]
                                if pair == 0:
                                    zw.append(at("pe", prev_ep_pe))
                                t(lambda e: e.matmul(psC[:], zeros1b[:], bo_bf[0:1, 0:512],
                                                     start=True, stop=False),
                                  waits=zw)
                                t(lambda e: e.matmul(psD[:], zeros1b[:], bo_bf[0:1, 0:512],
                                                     start=True, stop=False))
                            pend_av.append((kt, psl, need, a_off))
                            if i >= 2:
                                emit_av(*pend_av.pop(0))
                            if i + 1 < len(tks):
                                pes_next = emit_S(tks[i + 1])
                            if i == 1 and pend_chain2[0] is not None:
                                pend_chain2[0]()
                                pend_chain2[0] = None
                            if (_PHASE >= 6 and pend_chunks and pend_chunks[0]
                                    and i >= (6 if pend_chunks[0][0].get("late") else 1)):
                                mm_q = pend_chunks[0][0]["mms"]
                                for _ in range(2):
                                    if mm_q:
                                        mm_q.pop(0)()
                            if i == 5 and pa_cp_q and len(pa_dxm) > len_cp_done[0]:
                                pa_cp_q.pop(0)()
                                len_cp_done[0] += 1
                        for args in pend_av:
                            emit_av(*args)
                        pend_av = []
                        t(lambda e: e.matmul(psC[:], zeros1b[:], bo_bf[0:1, 0:512],
                                             start=False, stop=True))
                        avr = t(lambda e: e.matmul(psD[:], zeros1b[:], bo_bf[0:1, 0:512],
                                                   start=False, stop=True))

                        def mk_div(pair=pair, tqb=tqb, avr=avr):
                            def doit():
                                a(lambda e: e.activation(out=ep_sq[:, 0:512], in_=psD[:], func=AF.Ln),
                                  waits=[at("pe", avr), at("dve", attn_state["norm"])])
                                dvm = a(lambda e: e.activation(out=div_t[:], in_=ep_sq[:, 0:512],
                                                               func=AF.Exp, scale=-1.0))
                                attn_state["div"] = dvm
                                nm = v(lambda e, pair=pair, tqb=tqb: e.tensor_tensor(
                                    out=out_all[:, tqb % 2, pair, :], in0=psC[:], in1=div_t[:],
                                    op=ALU.mult),
                                    waits=[at("act", dvm)])
                                attn_state["norm"] = nm
                                norm_done[tqb] = nm
                            return doit

                        attn_state["pending_div"] = mk_div()
                        if _PHASE >= 6 and pend_chunks and pend_chunks[0]:
                            ch = pend_chunks[0].pop(0)
                            for f in ch["mms"]:
                                f()
                            pend_chain2[0] = ch["chain"]()
                            if not pend_chunks[0]:
                                pend_chunks.pop(0)
                        if pair < 7:
                            for kt2 in range(2):
                                S_pre[(pair + 1, kt2)] = emit_S_for(pair + 1, qc0, kt2)
                        elif tqb < min(4, _NTQB) - 1:
                            for kt2 in range(2):
                                S_pre[(0, kt2)] = emit_S_for(0, tqb * 512, kt2)
                        if _dbg and s == 0 and tqb == 0 and pair == 0:
                            attn_state["pending_div"]()
                            attn_state["pending_div"] = None
                            y(lambda e: e.dma_start(out=dbgf_d[0, :, 0:512], in_=div_t[:]),
                              waits=[at("dve", attn_state["norm"])])
                            y(lambda e: e.dma_start(out=dbgb_d[1, :, 0:512], in_=out_all[:, 0, :]))
                            y(lambda e: e.dma_start(out=dbgf_d[1, :, 0:2], in_=negc[:]))
                        if pa_dx_q:
                            pa_dx_q.pop(0)()

                    def make_ep_chunks(s=s, tqb=tqb, alt=False):
                        par = tqb % 2
                        eps = {}
                        cdst = {"dve": 0, "act": 0}

                        def tgt(k, fb):
                            if alt and k % 2 == 1:
                                return psC[:] if fb == 0 else psD[:]
                            return psE[:, 512 * fb:512 * (fb + 1)]

                        def partA(ts4):
                            row0 = tqb * 512 + ts4 * 128
                            pe_wo = w.ctr["pe"]
                            isalt = alt and ts4 % 2 == 1
                            pslo = psC[:] if isalt else psE[:, 0:512]
                            pshi = psD[:] if isalt else psE[:, 512:1024]
                            dxr = w.op("sync", lambda e, s=s, row0=row0: e.dma_start(
                                out=xstage[:, 0, :], in_=x_d[s, row0:row0 + 128, :]),
                                [at("dve", eps_state["xst_dve"])], "dxr", 16)
                            if isalt:
                                a(lambda e: e.activation(out=ep_sq[:, 0:512], in_=psC[:],
                                                         func=AF.Square, accum_out=st_sq[:]),
                                  waits=[at("pe", pe_wo), at("dve", st_guard[0])])
                                sqa = a(lambda e: e.activation(out=ep_sq[:, 512:1024], in_=psD[:],
                                                               func=AF.Square, accum_out=st_t[:]))
                                cdst["act"] = sqa
                            else:
                                sqa = a(lambda e: e.activation(out=ep_sq, in_=psE[:],
                                                               func=AF.Square,
                                                               accum_out=st_sq[:]),
                                        waits=[at("pe", pe_wo), at("dve", st_guard[0])])
                                eps_state["psE_act"] = sqa
                            v(lambda e, pslo=pslo: e.tensor_reduce(out=st_sum[:], in_=pslo,
                                                                   axis=AX.X, op=ALU.add),
                              waits=[at("pe", pe_wo), at("act", sqa)])
                            v(lambda e, pshi=pshi: e.tensor_reduce(out=st_sum2[:], in_=pshi,
                                                                   axis=AX.X, op=ALU.add))
                            if isalt:
                                v(lambda e: e.tensor_add(out=st_sq[:], in0=st_sq[:], in1=st_t[:]))
                            v(lambda e: e.tensor_add(out=st_sum[:], in0=st_sum[:], in1=st_sum2[:]))
                            v(lambda e: e.tensor_scalar_mul(out=st_m[:], in0=st_sum[:], scalar1=1.0 / D))
                            v(lambda e: e.tensor_scalar_mul(out=st_v[:], in0=st_sq[:], scalar1=1.0 / D),
                              waits=[cur("act")])
                            v(lambda e: e.tensor_tensor(out=red_t[:], in0=st_m[:], in1=st_m[:], op=ALU.mult))
                            v(lambda e: e.tensor_sub(out=st_v[:], in0=st_v[:], in1=red_t[:]))
                            v(lambda e: e.tensor_scalar_add(out=st_v[:], in0=st_v[:], scalar1=LN_EPS))
                            newton_rstd()
                            v(lambda e: e.tensor_tensor(out=st_beta[:], in0=st_m[:], in1=st_r[:], op=ALU.mult),
                              waits=[cur("act")])
                            v(lambda e: e.tensor_scalar_mul(out=st_beta[:], in0=st_beta[:], scalar1=-1.0))
                            xpw = v(lambda e: e.tensor_scalar_add(out=xprime[:], in0=xstage[:, 0, :],
                                                                  scalar1=st_beta[:, 0:1]),
                                    waits=[at("dxr", dxr), at("dxw", eps_state["xdw_dma"])])
                            eps_state["xst_dve"] = xpw
                            if isalt:
                                v(lambda e, ts4=ts4: e.scalar_tensor_tensor(
                                    out=y_t[:, ts4, 0:512], in0=psC[:], scalar=st_r[:, 0:1],
                                    in1=xprime[:, 0:512], op0=ALU.mult, op1=ALU.add))
                                yw = v(lambda e, ts4=ts4: e.scalar_tensor_tensor(
                                    out=y_t[:, ts4, 512:1024], in0=psD[:], scalar=st_r[:, 0:1],
                                    in1=xprime[:, 512:1024], op0=ALU.mult, op1=ALU.add))
                                cdst["dve"] = yw
                            else:
                                yw = v(lambda e, ts4=ts4: e.scalar_tensor_tensor(
                                    out=y_t[:, ts4, :], in0=psE[:], scalar=st_r[:, 0:1],
                                    in1=xprime[:],
                                    op0=ALU.mult, op1=ALU.add))
                                eps_state["psE_dve"] = yw
                            if ts4 == 3:
                                partA_post(ts4, row0, yw)
                                return None
                            return lambda: partA_post(ts4, row0, yw)

                        def partA_post(ts4, row0, yw):
                            v(lambda e, ts4=ts4: e.tensor_reduce(out=st_sum[:], in_=y_t[:, ts4, :],
                                                                 axis=AX.X, op=ALU.add))
                            sqb = a(lambda e, ts4=ts4: e.activation(out=ep_sq, in_=y_t[:, ts4, :],
                                                                    func=AF.Square, accum_out=st_sq[:]),
                                    waits=[at("dve", yw)])
                            v(lambda e: e.tensor_scalar_mul(out=st_m[:], in0=st_sum[:], scalar1=1.0 / D))
                            v(lambda e: e.tensor_scalar_mul(out=st_v[:], in0=st_sq[:], scalar1=1.0 / D),
                              waits=[at("act", sqb)])
                            v(lambda e: e.tensor_tensor(out=red_t[:], in0=st_m[:], in1=st_m[:], op=ALU.mult))
                            v(lambda e: e.tensor_sub(out=st_v[:], in0=st_v[:], in1=red_t[:]))
                            v(lambda e: e.tensor_scalar_add(out=st_v[:], in0=st_v[:], scalar1=LN_EPS))
                            newton_rstd()
                            v(lambda e, ts4=ts4: e.tensor_scalar(out=ln2_t[:], in0=y_t[:, ts4, :],
                                                                 scalar1=st_m[:, 0:1], scalar2=st_r[:, 0:1],
                                                                 op0=ALU.subtract, op1=ALU.mult),
                              waits=[cur("act"), at("out", ep_out_mark[0])])
                            st_guard[0] = w.ctr["dve"]
                            v(lambda e: e.tensor_reduce(
                                out=bs_max[:, :, None], in_=ln2_t[:].rearrange("p (b k) -> p b k", k=32),
                                axis=AX.X, op=ALU.max, apply_absolute_value=True))
                            v(lambda e: e.tensor_scalar(out=bs_s[:], in0=bs_max[:], scalar1=1.0 / 7.5,
                                                        scalar2=1e-30, op0=ALU.mult, op1=ALU.max))
                            v(lambda e: e.tensor_scalar_mul(out=bs_s64[:], in0=bs_s[:], scalar1=64.0))
                            v(lambda e: e.reciprocal(out=bs_max[:], in_=bs_s64[:]))
                            v(lambda e: e.tensor_tensor(
                                out=xprime[:].rearrange("p (b k) -> p b k", k=32),
                                in0=ln2_t[:].rearrange("p (b k) -> p b k", k=32),
                                in1=bs_max[:, :, None].broadcast_to([128, 32, 32]), op=ALU.mult))
                            v(lambda e: e.tensor_copy(out=y8_t, in_=xprime[:]))
                            xdw = v(lambda e: e.tensor_tensor(
                                out=xd_tm.rearrange("p (b k) -> p b k", k=32),
                                in0=y8_t.rearrange("p (b k) -> p b k", k=32),
                                in1=bs_s64[:, :, None].broadcast_to([128, 32, 32]), op=ALU.mult))
                            eps_state["xdw_dma"] = w.op("sync", lambda e, row0=row0: e.dma_start(
                                out=xd_s[row0:row0 + 128, :], in_=xd_tm),
                                [at("dve", xdw)], "dxw", 16)

                        def partB(ts4):
                            row0 = tqb * 512 + ts4 * 128
                            pe_ffn = w.ctr["pe"]
                            isalt = alt and ts4 % 2 == 1
                            pslo = psC[:] if isalt else psE[:, 0:512]
                            pshi = psD[:] if isalt else psE[:, 512:1024]
                            if isalt:
                                a(lambda e: e.activation(out=ep_sq[:, 0:512], in_=psC[:],
                                                         func=AF.Square, accum_out=st_sq[:]),
                                  waits=[at("pe", pe_ffn), at("dve", st_guard[0])])
                                sqc = a(lambda e: e.activation(out=ep_sq[:, 512:1024], in_=psD[:],
                                                               func=AF.Square, accum_out=st_t[:]))
                                cdst["act"] = sqc
                            else:
                                sqc = a(lambda e: e.activation(out=ep_sq, in_=psE[:],
                                                               func=AF.Square,
                                                               accum_out=st_sq[:]),
                                        waits=[at("pe", pe_ffn), at("dve", st_guard[0])])
                                eps_state["psE_act"] = sqc
                            v(lambda e, pslo=pslo: e.tensor_reduce(out=st_sum[:], in_=pslo,
                                                                   axis=AX.X, op=ALU.add),
                              waits=[at("pe", pe_ffn), at("act", sqc)])
                            v(lambda e, pshi=pshi: e.tensor_reduce(out=st_sum2[:], in_=pshi,
                                                                   axis=AX.X, op=ALU.add))
                            if isalt:
                                v(lambda e: e.tensor_add(out=st_sq[:], in0=st_sq[:], in1=st_t[:]))
                            v(lambda e: e.tensor_add(out=st_sum[:], in0=st_sum[:], in1=st_sum2[:]))
                            v(lambda e: e.tensor_scalar_mul(out=st_m[:], in0=st_sum[:], scalar1=1.0 / D))
                            v(lambda e: e.tensor_scalar_mul(out=st_v[:], in0=st_sq[:], scalar1=1.0 / D),
                              waits=[cur("act")])
                            v(lambda e: e.tensor_tensor(out=red_t[:], in0=st_m[:], in1=st_m[:], op=ALU.mult))
                            v(lambda e: e.tensor_sub(out=st_v[:], in0=st_v[:], in1=red_t[:]))
                            v(lambda e: e.tensor_scalar_add(out=st_v[:], in0=st_v[:], scalar1=LN_EPS))
                            return lambda: partB_post(ts4, row0)

                        def partB_post(ts4, row0):
                            isalt = alt and ts4 % 2 == 1
                            newton_rstd()
                            v(lambda e: e.tensor_tensor(out=st_beta[:], in0=st_m[:], in1=st_r[:], op=ALU.mult),
                              waits=[cur("act")])
                            v(lambda e: e.tensor_scalar_mul(out=st_beta[:], in0=st_beta[:], scalar1=-1.0))
                            v(lambda e, ts4=ts4: e.tensor_scalar_add(out=xprime[:], in0=y_t[:, ts4, :],
                                                                     scalar1=st_beta[:, 0:1]))
                            if isalt:
                                v(lambda e: e.scalar_tensor_tensor(
                                    out=ln2_t[:, 0:512], in0=psC[:], scalar=st_r[:, 0:1],
                                    in1=xprime[:, 0:512], op0=ALU.mult, op1=ALU.add),
                                    waits=[at("out", ep_out_mark[0])])
                                ow = v(lambda e: e.scalar_tensor_tensor(
                                    out=ln2_t[:, 512:1024], in0=psD[:], scalar=st_r[:, 0:1],
                                    in1=xprime[:, 512:1024], op0=ALU.mult, op1=ALU.add))
                                cdst["dve"] = ow
                            else:
                                ow = v(lambda e: e.scalar_tensor_tensor(
                                    out=ln2_t[:], in0=psE[:], scalar=st_r[:, 0:1], in1=xprime[:],
                                    op0=ALU.mult, op1=ALU.add),
                                    waits=[at("out", ep_out_mark[0])])
                                eps_state["psE_dve"] = ow
                            st_guard[0] = w.ctr["dve"]
                            ep_out_mark[0] = go(lambda e, s=s, row0=row0: e.dma_start(
                                out=out_d[s, row0:row0 + 128, :], in_=ln2_t[:]),
                                waits=[at("dve", ow)])

                        def mm_waits(k):
                            if alt and k % 2 == 1:
                                if cdst["dve"] == 0:
                                    return [at("dve", max(attn_state["norm"],
                                                          norm_done.get(tqb, 0))),
                                            at("dma", wready[1]),
                                            at("act", attn_state["div"])]
                                return [at("dve", max(cdst["dve"], norm_done.get(tqb, 0))),
                                        at("dma", wready[1]), at("act", cdst["act"])]
                            return [at("dve", max(eps_state["psE_dve"],
                                                  norm_done.get(tqb, 0))),
                                    at("dma", wready[1]),
                                    at("act", eps_state["psE_act"])]

                        def a_mms(k, fb):
                            out = []
                            for c in range(NC8):
                                out.append(lambda c=c, k=k, fb=fb: t(
                                    lambda e, fb=fb, c=c, ts4=k: e.matmul(
                                        tgt(k, fb),
                                        out_all[:, par, c, 128 * ts4:128 * (ts4 + 1)],
                                        slotA[:, c, 512 * fb:512 * (fb + 1)],
                                        start=(c == 0), stop=False),
                                    waits=(mm_waits(k) if (fb == 0 and c == 0) else ())))
                            out.append(lambda k=k, fb=fb: t(lambda e, k=k, fb=fb: e.matmul(
                                tgt(k, fb),
                                ones1b[:], bo_bf[0:1, 512 * fb:512 * (fb + 1)],
                                start=False, stop=True)))
                            return out

                        def b_mms(k, fb):
                            out = []

                            def first(k=k, fb=fb):
                                if k == 0 and fb == 0:
                                    w.op("sync", lambda e: None,
                                         [cur("dma"), cur("dxw"),
                                          at("pe", eps_state["ffn_pe"])], None)
                                    for c2 in range(NC8):
                                        w.op("sync", lambda e, c2=c2: e.dma_start(
                                            out=xd_fm[:, c2, :],
                                            in_=xd_s[tqb * 512:(tqb + 1) * 512,
                                                     128 * c2:128 * (c2 + 1)],
                                            transpose=True), (), "dtx", 16)
                                    eps["tpxd"] = w.ctr["dtx"]
                                t(lambda e, fb=fb, ts4=k, k=k: e.matmul(
                                    tgt(k, fb),
                                    xd_fm[:, 0, 128 * ts4:128 * (ts4 + 1)],
                                    slotB[:, 0, 512 * fb:512 * (fb + 1)],
                                    start=True, stop=False),
                                  waits=(([at("dtx", eps["tpxd"])] + mm_waits(k))
                                         if fb == 0 else ()))
                            out.append(first)
                            for c in range(1, NC8):
                                out.append(lambda c=c, k=k, fb=fb: t(
                                    lambda e, fb=fb, c=c, ts4=k, k=k: e.matmul(
                                        tgt(k, fb),
                                        xd_fm[:, c, 128 * ts4:128 * (ts4 + 1)],
                                        slotB[:, c, 512 * fb:512 * (fb + 1)],
                                        start=False, stop=False)))
                            out.append(lambda k=k, fb=fb: t(lambda e, k=k, fb=fb: e.matmul(
                                tgt(k, fb),
                                ones1b[:], fb_bf[0:1, 512 * fb:512 * (fb + 1)],
                                start=False, stop=True)))
                            return out

                        ckl = []
                        for k in range(4):
                            ckl.append({"mms": a_mms(k, 0) + a_mms(k, 1),
                                        "chain": (lambda k=k: partA(k))})
                        for k in range(4):
                            ckl.append({"mms": b_mms(k, 0) + b_mms(k, 1),
                                        "chain": (lambda k=k: partB(k)),
                                        "late": k == 0})
                        return ckl

                    if _PHASE >= 6:
                        chunks = make_ep_chunks(alt=(tqb == 3))
                        if tqb == 3:
                            if attn_state["pending_div"] is not None:
                                attn_state["pending_div"]()
                                attn_state["pending_div"] = None
                            if pend_chain2[0] is not None:
                                pend_chain2[0]()
                                pend_chain2[0] = None
                            for lst in pend_chunks:
                                for ch in lst:
                                    for f in ch["mms"]:
                                        f()
                                    c2 = ch["chain"]()
                                    if c2 is not None:
                                        c2()
                            pend_chunks.clear()
                            for ch in chunks:
                                for f in ch["mms"]:
                                    f()
                                c2 = ch["chain"]()
                                if c2 is not None:
                                    c2()
                        else:
                            pend_chunks.append(chunks)
                    prev_ep_pe = w.ctr["pe"]
                while pa_cp_q:
                    pa_cp_q.pop(0)()
                if attn_state["pending_div"] is not None:
                    attn_state["pending_div"]()
                    attn_state["pending_div"] = None

            w.final_wait("gpsimd", "out")

        for en in ENGINES:
            def make_body(en):
                def body(eng):
                    wk = Walk(en, eng, sems)
                    schedule(wk)
                    _CTR_FOR_DEBUG.update(wk.ctr)
                return body
            getattr(block, en)(make_body(en))
        _SEMS_FOR_DEBUG.update(sems)

    return nc


def _host_constants():
    import ml_dtypes
    inv = 1.0 / (10000.0 ** (np.arange(0, DH, 2, dtype=np.float64) / DH))
    pos = np.arange(SEQ, dtype=np.float64)
    fr = pos[:, None] * inv[None, :]
    emb = np.concatenate([fr, fr], axis=1)
    cos = np.cos(emb)
    sin = np.sin(emb)
    sinsw = np.concatenate([-sin[:, :32], sin[:, 32:]], axis=1)

    def to_tiles(tab, scale):
        t = (tab * scale).reshape(NT, 128, DH).transpose(1, 0, 2).reshape(128, NT * DH)
        return np.ascontiguousarray(t).astype(ml_dtypes.bfloat16)

    cosk = to_tiles(cos, 1.0)
    sinswk = to_tiles(sinsw, 1.0)

    p = np.arange(128)[:, None]
    c = np.arange(128)[None, :]
    mask = np.ascontiguousarray((c >= p).astype(np.float32)).astype(ml_dtypes.bfloat16)
    return cosk, sinswk, mask


_NC_CACHE = None


def kernel(L0_attn_x, L0_attn_Wq, L0_attn_Wkv, L0_attn_Wo, L0_attn_bo,
           L0_attn_ln_g, L0_attn_ln_b, L0_ffn_weight, L0_ffn_bias):
    global _NC_CACHE
    x = np.ascontiguousarray(np.asarray(L0_attn_x, dtype=np.float32))
    wqT = np.ascontiguousarray(np.asarray(L0_attn_Wq, dtype=np.float32).T)
    wkvT = np.ascontiguousarray(np.asarray(L0_attn_Wkv, dtype=np.float32).T)
    woT = np.ascontiguousarray(np.asarray(L0_attn_Wo, dtype=np.float32).T)
    wf = np.ascontiguousarray(np.asarray(L0_ffn_weight, dtype=np.float32))
    bo = np.ascontiguousarray(np.asarray(L0_attn_bo, dtype=np.float32))
    fb = np.ascontiguousarray(np.asarray(L0_ffn_bias, dtype=np.float32))
    g = np.ascontiguousarray(np.asarray(L0_attn_ln_g, dtype=np.float32))
    bb = np.ascontiguousarray(np.asarray(L0_attn_ln_b, dtype=np.float32))

    cosk, sinswk, mask = _host_constants()

    if _NC_CACHE is None:
        _NC_CACHE = build_nc()
    nc = _NC_CACHE

    in_maps = []
    for i in range(NCORES):
        in_maps.append({
            "x": np.ascontiguousarray(x[B_CORE * i:B_CORE * (i + 1)]),
            "wqT": wqT, "wkvT": wkvT, "woT": woT, "wf": wf,
            "bo": bo, "fb": fb, "lng": g, "lnb": bb,
            "cosk": cosk, "sinswk": sinswk, "mask": mask,
        })
    res = run_bass_kernel_spmd(nc, in_maps, core_ids=list(range(NCORES)))
    out = np.concatenate([np.asarray(res.results[i]["out"]) for i in range(NCORES)], axis=0)
    return out.astype(np.float32)


# version 45
# speedup vs baseline: 1.1874x; 1.0078x over previous
import os
import numpy as np
import contextlib
import concourse.bass as bass
import concourse.mybir as mybir
from concourse.bass_utils import run_bass_kernel_spmd

F32 = mybir.dt.float32
BF16 = mybir.dt.bfloat16
F8 = mybir.dt.float8e4
I32 = mybir.dt.int32
AF = mybir.ActivationFunctionType
ALU = mybir.AluOpType
AX = mybir.AxisListType

NCORES = 8
B_CORE = 2
SEQ = 2048
D = 1024
H = 16
DH = 64
CTX = 1536
LAT = 512
NT = SEQ // 128
NC8 = D // 128
LN_EPS = 1e-5
C_CLAMP = 80.0

ENGINES = ("sync", "tensor", "scalar", "vector", "gpsimd")
_SEMS_FOR_DEBUG = {}
_CTR_FOR_DEBUG = {}
_PHASE = int(os.environ.get("KERN_PHASE", "99"))
_NSEQ = int(os.environ.get("KERN_NSEQ", "2"))
_NTQB = int(os.environ.get("KERN_NTQB", "4"))
_NPAIR = int(os.environ.get("KERN_NPAIR", "8"))
_EP = int(os.environ.get("KERN_EP", "9"))
_EPDELAY = int(os.environ.get("KERN_EPDELAY", "4"))
_EP2 = int(os.environ.get("KERN_EP2", "9"))


class Walk:

    def __init__(self, active, eng, sems):
        self.active = active
        self.eng = eng
        self.sems = sems
        self.ctr = {k: 0 for k in sems}

    AUTO_CHAIN = {"vector": "dve", "scalar": "act"}

    def op(self, ename, fn, waits=(), inc=None, n=1):
        rw = [(s, (self.ctr[s] if v is None else v)) for (s, v) in waits]
        ac = self.AUTO_CHAIN.get(ename)
        if ac is not None and inc == ac:
            rw.append((ac, self.ctr[ac]))
        if self.active == ename:
            for s, v in rw:
                if v > 0:
                    self.eng.wait_ge(self.sems[s], v)
            inst = fn(self.eng)
            if inc is not None and inst is not None:
                inst.then_inc(self.sems[inc], n)
        if inc is not None:
            self.ctr[inc] += n
        return self.ctr[inc] if inc is not None else None

    def final_wait(self, ename, sem):
        v = self.ctr[sem]
        if self.active == ename:
            self.eng.wait_ge(self.sems[sem], v)


def cur(s):
    return (s, None)


def at(s, v):
    return (s, v)


def build_nc():
    nc = bass.Bass("TRN2", target_bir_lowering=False)

    x_d = nc.dram_tensor("x", [B_CORE, SEQ, D], F32, kind="ExternalInput").ap()
    wqT_d = nc.dram_tensor("wqT", [D, D], F32, kind="ExternalInput").ap()
    wkvT_d = nc.dram_tensor("wkvT", [D, D], F32, kind="ExternalInput").ap()
    woT_d = nc.dram_tensor("woT", [D, D], F32, kind="ExternalInput").ap()
    wf_d = nc.dram_tensor("wf", [D, D], F32, kind="ExternalInput").ap()
    bo_d = nc.dram_tensor("bo", [D], F32, kind="ExternalInput").ap()
    fb_d = nc.dram_tensor("fb", [D], F32, kind="ExternalInput").ap()
    g_d = nc.dram_tensor("lng", [DH], F32, kind="ExternalInput").ap()
    bb_d = nc.dram_tensor("lnb", [DH], F32, kind="ExternalInput").ap()
    cosk_d = nc.dram_tensor("cosk", [128, NT * DH], BF16, kind="ExternalInput").ap()
    sinswk_d = nc.dram_tensor("sinswk", [128, NT * DH], BF16, kind="ExternalInput").ap()
    mask_d = nc.dram_tensor("mask", [128, 128], BF16, kind="ExternalInput").ap()
    out_d = nc.dram_tensor("out", [B_CORE, SEQ, D], F32, kind="ExternalOutput").ap()

    _dbg = os.environ.get("KERN_DBG") == "1"
    _ik = "ExternalOutput" if _dbg else "Internal"
    ln1_s = nc.dram_tensor("ln1_s", [SEQ, D], BF16, kind=_ik).ap()
    q_s = nc.dram_tensor("q_s", [SEQ, D], BF16, kind=_ik).ap()
    lkv_s = nc.dram_tensor("lkv_s", [SEQ, D], BF16, kind=_ik).ap()
    xd_s = nc.dram_tensor("xd_s", [SEQ, D], BF16, kind=_ik).ap()
    wdq_s = nc.dram_tensor("wdq_s", [D, D], BF16, kind=_ik).ap()
    wq_b = nc.dram_tensor("wq_b", [NC8, 128, D], BF16, kind=_ik).ap()
    wkv_b = nc.dram_tensor("wkv_b", [NC8, 128, D], BF16, kind=_ik).ap()
    wo_b = nc.dram_tensor("wo_b", [NC8, 128, D], BF16, kind=_ik).ap()
    if _dbg:
        dbgb_d = nc.dram_tensor("dbgb", [4, 128, D], BF16, kind="ExternalOutput").ap()
        dbgf_d = nc.dram_tensor("dbgf", [4, 128, D], F32, kind="ExternalOutput").ap()
    mq_s = nc.dram_tensor("mq_s", [128], F32, kind="Internal").ap()
    mk_s = nc.dram_tensor("mk_s", [128], F32, kind="Internal").ap()

    with contextlib.ExitStack() as stack:
        E = stack.enter_context
        slotA = E(nc.sbuf_tensor("slotA", [128, NC8, D], BF16))
        slotB = E(nc.sbuf_tensor("slotB", [128, NC8, D], BF16))
        wstage = E(nc.sbuf_tensor("wstage", [128, 2, 512], F32))
        cosk = E(nc.sbuf_tensor("cosk_t", [128, NT, DH], BF16))
        sinswk = E(nc.sbuf_tensor("sinswk_t", [128, NT, DH], BF16))
        masks = E(nc.sbuf_tensor("masks", [128, 128], BF16))
        grep = E(nc.sbuf_tensor("grep", [128, DH], BF16))
        brep = E(nc.sbuf_tensor("brep", [128, DH], BF16))
        gbf32 = E(nc.sbuf_tensor("gbf32", [128, 2, DH], F32))
        bo_bf = E(nc.sbuf_tensor("bo_bf", [1, D], BF16))
        fb_bf = E(nc.sbuf_tensor("fb_bf", [1, D], BF16))
        ones64 = E(nc.sbuf_tensor("ones64", [128, DH], BF16))
        ones1b = E(nc.sbuf_tensor("ones1b", [1, 128], BF16))
        zeros1b = E(nc.sbuf_tensor("zeros1b", [1, 128], BF16))
        ones1f = E(nc.sbuf_tensor("ones1f", [1, 128], F32))

        xstage = E(nc.sbuf_tensor("xstage", [128, 2, D], F32))
        ln1tm = E(nc.sbuf_tensor("ln1tm", [128, 2, D], BF16))
        ropeA = E(nc.sbuf_tensor("ropeA", [128, D], BF16))
        ropeB = E(nc.sbuf_tensor("ropeB", [128, D], BF16))
        lkv_tm = E(nc.sbuf_tensor("lkv_tm", [128, NT, D], BF16))
        lkv_fm = E(nc.sbuf_tensor("lkv_fm", [128, NC8, SEQ], BF16))
        q_fm = E(nc.sbuf_tensor("q_fm", [128, NC8, SEQ], BF16))
        p_buf = E(nc.sbuf_tensor("p_buf", [128, 3, D], BF16))
        out_all = E(nc.sbuf_tensor("out_all", [128, 2, NC8, LAT], BF16))
        div_t = E(nc.sbuf_tensor("div_t", [128, LAT], F32))
        y_t = E(nc.sbuf_tensor("y_t", [128, 4, D], BF16))
        ln2_t = E(nc.sbuf_tensor("ln2_t", [128, D], F32))
        xprime = E(nc.sbuf_tensor("xprime", [128, D], F32))
        xd_fm = E(nc.sbuf_tensor("xd_fm", [128, NC8, LAT], BF16))
        st_sum = E(nc.sbuf_tensor("st_sum", [128, 1], F32))
        st_sum2 = E(nc.sbuf_tensor("st_sum2", [128, 1], F32))
        st_sq = E(nc.sbuf_tensor("st_sq", [128, 1], F32))
        st_m = E(nc.sbuf_tensor("st_m", [128, 1], F32))
        st_v = E(nc.sbuf_tensor("st_v", [128, 1], F32))
        st_r = E(nc.sbuf_tensor("st_r", [128, 1], F32))
        st_beta = E(nc.sbuf_tensor("st_beta", [128, 1], F32))
        st_t = E(nc.sbuf_tensor("st_t", [128, 1], F32))
        hsum = E(nc.sbuf_tensor("hsum", [128, H], F32))
        hsq = E(nc.sbuf_tensor("hsq", [128, H], F32))
        hm = E(nc.sbuf_tensor("hm", [128, H], F32))
        hv = E(nc.sbuf_tensor("hv", [128, H], F32))
        hr = E(nc.sbuf_tensor("hr", [128, H], F32))
        hmb = E(nc.sbuf_tensor("hmb", [128, H], BF16))
        hrb = E(nc.sbuf_tensor("hrb", [128, H], BF16))
        mq_run = E(nc.sbuf_tensor("mq_run", [128, 1], F32))
        mk_run = E(nc.sbuf_tensor("mk_run", [128, 1], F32))
        red_t = E(nc.sbuf_tensor("red_t", [128, 1], F32))
        c_sc = E(nc.sbuf_tensor("c_sc", [1, 4], F32))
        negc = E(nc.sbuf_tensor("negc", [128, 2], F32))
        bs_max = E(nc.sbuf_tensor("bs_max", [128, 32], F32))
        bs_s = E(nc.sbuf_tensor("bs_s", [128, 32], F32))
        bs_s64 = E(nc.sbuf_tensor("bs_s64", [128, 32], F32))
        constv = E(nc.sbuf_tensor("constv", [128, 8], F32))
        mqn2 = E(nc.sbuf_tensor("mqn2", [128, H], F32))
        mkn2 = E(nc.sbuf_tensor("mkn2", [128, H], F32))

        psA = E(nc.psum_tensor("psA", [128, D], F32))
        psB = E(nc.psum_tensor("psB", [128, D], F32))
        psC = E(nc.psum_tensor("psC", [128, 512], F32))
        psD = E(nc.psum_tensor("psD", [128, 512], F32))
        psE = E(nc.psum_tensor("psE", [128, D], F32))

        _cvals = [LN_EPS, -0.5, -1.0, 0.125, 1.0, 0.0, -2.0794415416798357, 0.5]
        for _i, _cv in enumerate(_cvals):
            nc.const_aps.register(F32, _cv, constv[:, _i:_i + 1])

        sem_names = ["dma", "dw0", "dw1", "out", "pe", "act", "dve", "gp", "dtx", "dxr", "dxw",
                     "dln", "dtp", "dfq", "dtq"]
        sems = {k: E(nc.semaphore("s_" + k)) for k in sem_names}
        block = E(nc.Block())

        sq_scr = y_t[:, 0:2, :].bitcast(F32).rearrange("p a b -> p (a b)")
        kvbf = p_buf[:, 0, :]
        c_row = div_t[0:1, 0:256]
        rope_o = xstage[:, 1, 0:512].bitcast(BF16)
        xd_tm = xprime[:, 0:512].bitcast(BF16)
        y8_t = ln2_t[:, 0:256].bitcast(F8)
        ep_sq = xstage[:, 1, :]
        rope_o2 = div_t[:].bitcast(BF16)
        rope_o3 = xstage[:, 1, 512:1024].bitcast(BF16)
        pasq = ropeA[:].bitcast(F32)

        def schedule(w: Walk):
            t = lambda fn, waits=(), inc="pe", n=1: w.op("tensor", fn, waits, inc, n)
            a = lambda fn, waits=(), inc="act", n=1: w.op("scalar", fn, waits, inc, n)
            v = lambda fn, waits=(), inc="dve", n=1: w.op("vector", fn, waits, inc, n)
            y = lambda fn, waits=(), inc="dma", n=16: w.op("sync", fn, waits, inc, n)
            go = lambda fn, waits=(), inc="out", n=16: w.op("gpsimd", fn, waits, inc, n)
            ybar = lambda: w.op("sync", lambda e: None, [cur("dma")], None)

            def newton_rstd(first_waits=()):
                a(lambda e: e.activation(out=st_r[:], in_=st_v[:], func=AF.Ln),
                  waits=[cur("dve")] + list(first_waits))
                a(lambda e: e.activation(out=st_r[:], in_=st_r[:], func=AF.Exp, scale=-0.5))

            wq_done = []
            wcast = []
            last_S = [0]
            pend_chain2 = [None]
            ffn_mark = {0: 0, 1: 0}
            ep_out_mark = [0]

            y(lambda e: e.dma_start(out=cosk[:], in_=cosk_d.rearrange("p (t k) -> p t k", k=DH)))
            y(lambda e: e.dma_start(out=sinswk[:], in_=sinswk_d.rearrange("p (t k) -> p t k", k=DH)))
            y(lambda e: e.dma_start(out=masks[:], in_=mask_d[:]))
            y(lambda e: e.dma_start(out=gbf32[:, 0, :], in_=g_d[None, :].broadcast_to([128, DH])))
            y(lambda e: e.dma_start(out=gbf32[:, 1, :], in_=bb_d[None, :].broadcast_to([128, DH])))
            y(lambda e: e.dma_start(out=wstage[0:1, 0, :], in_=bo_d[None, 0:512]))
            y(lambda e: e.dma_start(out=wstage[0:1, 1, :], in_=bo_d[None, 512:1024]))
            ld0 = y(lambda e: e.dma_start(out=c_row[:], in_=fb_d[None, 0:256]))
            v(lambda e: e.tensor_copy(out=grep[:], in_=gbf32[:, 0, :]), waits=[at("dma", ld0)])
            v(lambda e: e.tensor_copy(out=brep[:], in_=gbf32[:, 1, :]))
            v(lambda e: e.tensor_copy(out=bo_bf[0:1, 0:512], in_=wstage[0:1, 0, :]))
            bcst = v(lambda e: e.tensor_copy(out=bo_bf[0:1, 512:1024], in_=wstage[0:1, 1, :]))
            y(lambda e: e.dma_start(out=wstage[0:1, 0, :], in_=fb_d[None, 0:512]),
              waits=[at("dve", bcst)])
            ldf = y(lambda e: e.dma_start(out=wstage[0:1, 1, :], in_=fb_d[None, 512:1024]))
            v(lambda e: e.tensor_copy(out=fb_bf[0:1, 0:512], in_=wstage[0:1, 0, :]),
              waits=[at("dma", ldf)])
            v(lambda e: e.tensor_copy(out=fb_bf[0:1, 512:1024], in_=wstage[0:1, 1, :]))
            for _i, _cv in enumerate([LN_EPS, -0.5, -1.0, 0.125, 1.0, 0.0, -2.0794415416798357, 0.5]):
                v(lambda e, _i=_i, _cv=_cv: e.memset(constv[:, _i:_i + 1], _cv))
            v(lambda e: e.memset(ones64[:], 1.0))
            v(lambda e: e.memset(ones1b[:], 1.0))
            v(lambda e: e.memset(zeros1b[:], 0.0))
            v(lambda e: e.memset(ones1f[:], 1.0))

            for r in range(8):
                wsfull = wstage[:].rearrange("p a b -> p (a b)")
                dw = y(lambda e, r=r, wsfull=wsfull: e.dma_start(
                    out=wsfull, in_=wf_d[128 * r:128 * (r + 1), :]),
                    waits=[cur("dve") if r == 0 else at("dve", wq_done[r - 1])])
                v(lambda e, wsfull=wsfull: e.tensor_reduce(
                    out=bs_max[:, :, None], in_=wsfull.rearrange("p (b k) -> p b k", k=32),
                    axis=AX.X, op=ALU.max, apply_absolute_value=True),
                  waits=[at("dma", dw)])
                v(lambda e: e.tensor_scalar(out=bs_s[:], in0=bs_max[:], scalar1=1.0 / 7.5, scalar2=1e-30,
                                            op0=ALU.mult, op1=ALU.max))
                v(lambda e: e.tensor_scalar_mul(out=bs_s64[:], in0=bs_s[:], scalar1=64.0))
                v(lambda e: e.reciprocal(out=bs_max[:], in_=bs_s64[:]))
                v(lambda e, wsfull=wsfull: e.tensor_tensor(
                    out=xprime[:].rearrange("p (b k) -> p b k", k=32),
                    in0=wsfull.rearrange("p (b k) -> p b k", k=32),
                    in1=bs_max[:, :, None].broadcast_to([128, 32, 32]), op=ALU.mult))
                v(lambda e: e.tensor_copy(out=y8_t, in_=xprime[:]))
                wq_done.append(
                    v(lambda e: e.tensor_tensor(
                        out=xd_tm.rearrange("p (b k) -> p b k", k=32),
                        in0=y8_t.rearrange("p (b k) -> p b k", k=32),
                        in1=bs_s64[:, :, None].broadcast_to([128, 32, 32]), op=ALU.mult)))
                y(lambda e, r=r: e.dma_start(out=wdq_s[128 * r:128 * (r + 1), :], in_=xd_tm),
                  waits=[at("dve", wq_done[r])])

            for s in range(min(B_CORE, _NSEQ)):
                if _PHASE < 2:
                    break
                def emit_ln1_tp():
                    ybar()
                    w.op("sync", lambda e: None, [cur("dln")], None)
                    for c in range(NC8):
                        w.op("sync", lambda e, c=c: e.dma_start(
                            out=q_fm[:, c, :], in_=ln1_s[:, 128 * c:128 * (c + 1)],
                            transpose=True),
                            ([at("pe", last_S[0])] if (s > 0 and c == 0) else ()),
                            "dtp", 16)
                    return w.ctr["dtp"]

                tp_ln1 = 0
                if s > 0 and _PHASE >= 3:
                    tp_ln1 = emit_ln1_tp()

                if s == 0:
                    wload_q = []
                    for wd_ap, slot, wb in [(wqT_d, slotA, wq_b), (wkvT_d, slotB, wkv_b)]:
                        for c in range(NC8):
                            for hf in range(2):
                                def wload(wd_ap=wd_ap, slot=slot, wb=wb, c=c, hf=hf):
                                    psem = "dw0" if hf == 0 else "dw1"
                                    dw = w.op("sync", lambda e, wd_ap=wd_ap, c=c, hf=hf: e.dma_start(
                                        out=wstage[:, hf, :],
                                        in_=wd_ap[128 * c:128 * (c + 1), 512 * hf:512 * (hf + 1)]),
                                        [at("dve", wcast[-2] if len(wcast) >= 2 else 0)], psem, 16)
                                    wcast.append(
                                        v(lambda e, slot=slot, c=c, hf=hf: e.tensor_copy(
                                            out=slot[:, c, 512 * hf:512 * (hf + 1)],
                                            in_=wstage[:, hf, :]),
                                          waits=[at(psem, dw)]))
                                    go(lambda e, slot=slot, wb=wb, c=c, hf=hf: e.dma_start(
                                        out=wb[c, :, 512 * hf:512 * (hf + 1)],
                                        in_=slot[:, c, 512 * hf:512 * (hf + 1)]),
                                       waits=[at("dve", wcast[-1])])
                                wload_q.append(wload)
                else:
                    wbkv_mark = 0
                    w.op("sync", lambda e: None, [cur("pe"), cur("out")], None)
                    for c in range(NC8):
                        y(lambda e, c=c: e.dma_start(out=slotA[:, c, :], in_=wq_b[c]))
                        y(lambda e, c=c: e.dma_start(out=slotB[:, c, :], in_=wkv_b[c]))
                    ybar()
                    wkvq_ready = ("dma", w.ctr["dma"])

                p1_marks = {}
                for tt in range(NT if (_PHASE >= 3 and s == 0) else 0):
                    sl = tt % 2
                    if tt < 2:
                        xw = [cur("dve"), cur("act")] if s > 0 else []
                    else:
                        wv, wa = p1_marks[("stageA", tt - 2)]
                        xw = [at("dve", wv), at("act", wa)]
                    dx = y(lambda e, s=s, tt=tt, sl=sl: e.dma_start(
                        out=xstage[:, sl, :], in_=x_d[s, 128 * tt:128 * (tt + 1), :]),
                        waits=xw)
                    v(lambda e, sl=sl: e.tensor_reduce(
                        out=st_sum[:], in_=xstage[:, sl, :], axis=AX.X, op=ALU.add),
                      waits=[at("dma", dx)])
                    sqm = a(lambda e, sl=sl: e.activation(
                        out=sq_scr, in_=xstage[:, sl, :], func=AF.Square, accum_out=st_sq[:]),
                        waits=[at("dma", dx)] + ([cur("dve")] if (tt == 0 and s > 0) else []))
                    v(lambda e: e.tensor_scalar_mul(out=st_m[:], in0=st_sum[:], scalar1=1.0 / D))
                    v(lambda e: e.tensor_scalar_mul(out=st_v[:], in0=st_sq[:], scalar1=1.0 / D),
                      waits=[cur("act")])
                    v(lambda e: e.tensor_tensor(out=red_t[:], in0=st_m[:], in1=st_m[:], op=ALU.mult))
                    v(lambda e: e.tensor_sub(out=st_v[:], in0=st_v[:], in1=red_t[:]))
                    a(lambda e: e.activation(out=st_r[:], in_=st_v[:], func=AF.Ln, bias=LN_EPS),
                      waits=[cur("dve")])
                    a(lambda e: e.activation(out=st_r[:], in_=st_r[:], func=AF.Exp, scale=-0.5))
                    lnw = v(lambda e, sl=sl: e.tensor_scalar(
                        out=ln1tm[:, sl, :], in0=xstage[:, sl, :],
                        scalar1=st_m[:, 0:1], scalar2=st_r[:, 0:1],
                        op0=ALU.subtract, op1=ALU.mult),
                        waits=[cur("act"), cur("dma")])
                    y(lambda e, tt=tt, sl=sl: e.dma_start(
                        out=ln1_s[128 * tt:128 * (tt + 1), :], in_=ln1tm[:, sl, :]),
                      waits=[at("dve", lnw)])
                    p1_marks[("stageA", tt)] = (lnw, sqm)
                    for _ in range(2):
                        if wload_q:
                            wload_q.pop(0)()

                if s == 0:
                    while wload_q:
                        wload_q.pop(0)()
                    wkvq_ready = ("dve", wcast[-1])
                    wbkv_mark = w.ctr["out"]
                if _PHASE >= 3 and s == 0:
                    tp_ln1 = emit_ln1_tp()

                qbf = out_all[:, 0, 0:2, :].rearrange("p a b -> p (a b)")
                qropeA = out_all[:, 0, 2:4, :].rearrange("p a b -> p (a b)")
                qropeB = out_all[:, 0, 4:6, :].rearrange("p a b -> p (a b)")
                qrope_o = out_all[:, 0, 6:8, :].rearrange("p a b -> p (a b)")
                NB = NT if _PHASE >= 3 else 0
                for tt in range(NB + 2):
                    if tt < NB:
                        sl = tt % 2
                        kv_ps = psA if sl == 0 else psB
                        q_odd = tt % 2 == 1
                        kvw0 = [at("dtp", tp_ln1),
                                at("act", p1_marks.get(("kvE", tt - 2), 0)),
                                at(*wkvq_ready)]
                        if tt < 2:
                            kvw0.append(cur("act"))
                        for fb in range(2):
                            for c in range(NC8):
                                t(lambda e, fb=fb, c=c, tt=tt, kv_ps=kv_ps: e.matmul(
                                    kv_ps[:, 512 * fb:512 * (fb + 1)],
                                    q_fm[:, c, 128 * tt:128 * (tt + 1)],
                                    slotB[:, c, 512 * fb:512 * (fb + 1)],
                                    start=(c == 0), stop=(c == NC8 - 1)),
                                  waits=(kvw0 if (fb == 0 and c == 0) else ()))
                        pe_kv = w.ctr["pe"]
                        qw0 = [at("act", p1_marks.get(("qE", tt - 2), 0)),
                               at(*wkvq_ready)]
                        if tt < 2 and s > 0:
                            qw0 = [cur("act"), cur("dve"), at(*wkvq_ready)]
                        for fb in range(2):
                            q_ps = (psE[:, 512 * fb:512 * (fb + 1)] if q_odd
                                    else (psC[:] if fb == 0 else psD[:]))
                            for c in range(NC8):
                                t(lambda e, fb=fb, c=c, tt=tt, q_ps=q_ps: e.matmul(
                                    q_ps,
                                    q_fm[:, c, 128 * tt:128 * (tt + 1)],
                                    slotA[:, c, 512 * fb:512 * (fb + 1)],
                                    start=(c == 0), stop=(c == NC8 - 1)),
                                  waits=(qw0 if (fb == 0 and c == 0) else ()))
                        pe_q = w.ctr["pe"]
                        p1_marks[("peq", tt)] = pe_q
                        kvE = a(lambda e, kv_ps=kv_ps: e.activation(
                            out=kvbf, in_=kv_ps[:], func=AF.Identity),
                            waits=[at("pe", pe_kv),
                                   at("dve", p1_marks.get(("kvO2", tt - 1), 0))])
                        p1_marks[("kvE", tt)] = kvE
                        if q_odd:
                            qE = a(lambda e: e.activation(
                                out=qbf, in_=psE[:], func=AF.Identity),
                                waits=[at("pe", pe_q),
                                       at("dve", p1_marks.get(("qO2", tt - 1), 0))])
                        else:
                            a(lambda e: e.activation(
                                out=qbf[:, 0:512], in_=psC[:], func=AF.Identity),
                              waits=[at("pe", pe_q),
                                     at("dve", p1_marks.get(("qO2", tt - 1), 0))])
                            qE = a(lambda e: e.activation(
                                out=qbf[:, 512:1024], in_=psD[:], func=AF.Identity))
                        p1_marks[("qE", tt)] = qE

                    if tt < NB:
                        ro = (rope_o, rope_o2, rope_o3)[tt % 3]
                        p1_marks[("ro", tt)] = ro
                        v(lambda e, tt=tt: e.tensor_tensor(
                            out=ropeB[:].rearrange("p (h k) -> p h k", k=DH),
                            in0=kvbf.rearrange("p (h k) -> p h k", k=DH),
                            in1=cosk[:, tt, None, :].broadcast_to([128, H, DH]), op=ALU.mult),
                          waits=[at("act", kvE)])
                        kvO2 = v(lambda e, tt=tt: e.tensor_tensor(
                            out=ropeA[:].rearrange("p (h two k) -> p h two k", two=2, k=32),
                            in0=kvbf.rearrange("p (h two k) -> p h two k", two=2, k=32)[:, :, ::-1, :],
                            in1=sinswk[:, tt, :].rearrange("p (two k) -> p two k", two=2)[:, None, :, :]
                                .broadcast_to([128, H, 2, 32]),
                            op=ALU.mult))
                        p1_marks[("kvO2", tt)] = kvO2
                        p1_marks[("kvr", tt)] = v(lambda e, ro=ro: e.tensor_add(
                            out=ro, in0=ropeB[:], in1=ropeA[:]),
                            waits=[at("act", p1_marks.get(("hq", tt - 3), 0))])
                        v(lambda e, tt=tt: e.tensor_tensor(
                            out=qropeA.rearrange("p (h k) -> p h k", k=DH),
                            in0=qbf.rearrange("p (h k) -> p h k", k=DH),
                            in1=cosk[:, tt, None, :].broadcast_to([128, H, DH]), op=ALU.mult),
                          waits=[at("act", qE)])
                        qO2 = v(lambda e, tt=tt: e.tensor_tensor(
                            out=qropeB.rearrange("p (h two k) -> p h two k", two=2, k=32),
                            in0=qbf.rearrange("p (h two k) -> p h two k", two=2, k=32)[:, :, ::-1, :],
                            in1=sinswk[:, tt, :].rearrange("p (two k) -> p two k", two=2)[:, None, :, :]
                                .broadcast_to([128, H, 2, 32]),
                            op=ALU.mult))
                        p1_marks[("qO2", tt)] = qO2
                        qoro = (qrope_o, ln1tm[:, 0, :], ln1tm[:, 1, :])[tt % 3]
                        p1_marks[("qoro", tt)] = qoro
                        p1_marks[("qro", tt)] = v(lambda e, qoro=qoro: e.tensor_add(
                            out=qoro, in0=qropeA, in1=qropeB),
                            waits=([at("dma", p1_marks[("dq", tt - 3)])]
                                   if ("dq", tt - 3) in p1_marks else [])
                                  + ([at("act", p1_marks[("qsq", tt - 3)])]
                                     if ("qsq", tt - 3) in p1_marks else []))
                    if 2 <= tt <= NB + 1:
                        u = tt - 2
                        hsqr = v(lambda e: e.tensor_reduce(
                            out=hsq[:, :, None], in_=sq_scr.rearrange("p (h k) -> p h k", k=DH),
                            axis=AX.X, op=ALU.add),
                            waits=[at("act", p1_marks[("hq", u)])])
                        v(lambda e: e.tensor_scalar_mul(out=hm[:], in0=hsum[:], scalar1=1.0 / DH))
                        v(lambda e: e.tensor_scalar_mul(out=hv[:], in0=hsq[:], scalar1=1.0 / DH))
                        v(lambda e: e.tensor_tensor(out=hsq[:], in0=hm[:], in1=hm[:], op=ALU.mult))
                        v(lambda e: e.tensor_sub(out=hv[:], in0=hv[:], in1=hsq[:]))
                        a(lambda e: e.activation(out=hr[:], in_=hv[:], func=AF.Ln, bias=LN_EPS),
                          waits=[cur("dve")])
                        a(lambda e: e.activation(out=hr[:], in_=hr[:], func=AF.Exp, scale=-0.5))
                        v(lambda e: e.tensor_copy(out=hmb[:], in_=hm[:]), waits=[cur("act")])
                        v(lambda e: e.tensor_copy(out=hrb[:], in_=hr[:]))
                        v(lambda e, u=u: e.tensor_sub(
                            out=ropeA[:].rearrange("p (h k) -> p h k", k=DH),
                            in0=p1_marks[("ro", u)].rearrange("p (h k) -> p h k", k=DH),
                            in1=hmb[:, :, None].broadcast_to([128, H, DH])))
                        v(lambda e: e.tensor_tensor(
                            out=ropeA[:].rearrange("p (h k) -> p h k", k=DH),
                            in0=ropeA[:].rearrange("p (h k) -> p h k", k=DH),
                            in1=hrb[:, :, None].broadcast_to([128, H, DH]), op=ALU.mult))
                        v(lambda e: e.tensor_tensor(
                            out=ropeA[:].rearrange("p (h k) -> p h k", k=DH),
                            in0=ropeA[:].rearrange("p (h k) -> p h k", k=DH),
                            in1=grep[:, None, :].broadcast_to([128, H, DH]), op=ALU.mult))
                        lkvw = v(lambda e, u=u: e.tensor_tensor(
                            out=lkv_tm[:, u, :].rearrange("p (h k) -> p h k", k=DH),
                            in0=ropeA[:].rearrange("p (h k) -> p h k", k=DH),
                            in1=brep[:, None, :].broadcast_to([128, H, DH]), op=ALU.add),
                            waits=([cur("pe")] if (s > 0 and u == 0) else ()))
                        y(lambda e, u=u: e.dma_start(
                            out=lkv_s[128 * u:128 * (u + 1), :], in_=lkv_tm[:, u, :]),
                          waits=[at("dve", lkvw)])
                        ksq = a(lambda e, u=u: e.activation(out=sq_scr, in_=lkv_tm[:, u, :],
                                                            func=AF.Square),
                                waits=[at("dve", lkvw)])
                        knr = v(lambda e: e.tensor_reduce(
                            out=hsq[:, :, None], in_=sq_scr.rearrange("p (h k) -> p h k", k=DH),
                            axis=AX.X, op=ALU.add),
                            waits=[at("act", ksq)])
                        if u == 0:
                            v(lambda e: e.tensor_copy(out=mkn2[:], in_=hsq[:]))
                        else:
                            v(lambda e: e.tensor_max(out=mkn2[:], in0=mkn2[:], in1=hsq[:]))
                        qsq = a(lambda e, u=u: e.activation(
                            out=sq_scr[:], in_=p1_marks[("qoro", u)], func=AF.Square),
                            waits=[at("dve", p1_marks[("qro", u)]), at("dve", knr)])
                        p1_marks[("qsq", u)] = qsq
                        qnr = v(lambda e: e.tensor_reduce(
                            out=hsq[:, :, None], in_=sq_scr.rearrange("p (h k) -> p h k", k=DH),
                            axis=AX.X, op=ALU.add),
                            waits=[at("act", qsq)])
                        p1_marks[("qnr", u)] = qnr
                        if u == 0:
                            v(lambda e: e.tensor_copy(out=mqn2[:], in_=hsq[:]))
                        else:
                            v(lambda e: e.tensor_max(out=mqn2[:], in0=mqn2[:], in1=hsq[:]))
                        p1_marks[("dq", u)] = y(lambda e, u=u: e.dma_start(
                            out=q_s[128 * u:128 * (u + 1), :], in_=p1_marks[("qoro", u)]),
                            waits=[at("dve", p1_marks[("qro", u)])])

                    if 1 <= tt <= NB:
                        u1 = tt - 1
                        v(lambda e, u1=u1: e.tensor_reduce(
                            out=hsum[:, :, None],
                            in_=p1_marks[("ro", u1)].rearrange("p (h k) -> p h k", k=DH),
                            axis=AX.X, op=ALU.add))
                        p1_marks[("hq", u1)] = a(lambda e, u1=u1: e.activation(
                            out=sq_scr, in_=p1_marks[("ro", u1)], func=AF.Square),
                            waits=[at("dve", p1_marks[("kvr", u1)]),
                                   at("dve", p1_marks.get(("qnr", u1 - 1), 0))])

                peB = w.ctr["pe"]

                tp_pair = {}
                if _PHASE >= 3:
                    ybar()
                    for c in range(NC8):
                        psem = "dtq" if c >= 4 else "dtp"
                        w.op("sync", lambda e, c=c: e.dma_start(
                            out=lkv_fm[:, c, :], in_=lkv_s[:, 128 * c:128 * (c + 1)],
                            transpose=True),
                            ([at("pe", peB)] if c == 0 else ()), psem, 16)
                        mk = w.op("sync", lambda e, c=c: e.dma_start(
                            out=q_fm[:, c, :], in_=q_s[:, 128 * c:128 * (c + 1)],
                            transpose=True), (), psem, 16)
                        if c in (3, 7):
                            for c2 in range(c - 3, c + 1):
                                tp_pair[c2] = (psem, mk)

                ybar()

                if _PHASE < 4:
                    continue
                v(lambda e: e.tensor_reduce(out=mq_run[:], in_=mqn2[:], axis=AX.X, op=ALU.max))
                v(lambda e: e.tensor_reduce(out=mk_run[:], in_=mkn2[:], axis=AX.X, op=ALU.max))
                dmq = w.op("gpsimd", lambda e: e.dma_start(out=mq_s[:], in_=mq_run[:, 0]),
                           [cur("dve")], "dfq", 16)
                dmk = w.op("gpsimd", lambda e: e.dma_start(out=mk_s[:], in_=mk_run[:, 0]),
                           (), "dfq", 16)
                w.op("gpsimd", lambda e: e.dma_start(out=c_row[0:1, 0:128], in_=mq_s[None, :]),
                     [at("dfq", dmk)], "dfq", 16)
                dcr = w.op("gpsimd", lambda e: e.dma_start(out=c_row[0:1, 128:256], in_=mk_s[None, :]),
                           (), "dfq", 16)
                v(lambda e: e.tensor_reduce(out=c_sc[:, 0:1], in_=c_row[0:1, 0:128], axis=AX.X, op=ALU.max),
                  waits=[at("dfq", dcr)])
                v(lambda e: e.tensor_reduce(out=c_sc[:, 1:2], in_=c_row[0:1, 128:256], axis=AX.X, op=ALU.max))
                a(lambda e: e.activation(out=c_sc[:, 0:1], in_=c_sc[:, 0:1], func=AF.Ln),
                  waits=[cur("dve")])
                a(lambda e: e.activation(out=c_sc[:, 1:2], in_=c_sc[:, 1:2], func=AF.Ln))
                v(lambda e: e.tensor_add(out=c_sc[:, 2:3], in0=c_sc[:, 0:1], in1=c_sc[:, 1:2]),
                  waits=[cur("act")])
                a(lambda e: e.activation(out=c_sc[:, 2:3], in_=c_sc[:, 2:3], func=AF.Exp,
                                         scale=0.5, bias=-2.0794415416798357),
                  waits=[cur("dve")])
                cst = v(lambda e: e.tensor_scalar(out=c_sc[:, 3:4], in0=c_sc[:, 2:3],
                                                  scalar1=-1.0, scalar2=-C_CLAMP,
                                                  op0=ALU.mult, op1=ALU.max),
                        waits=[cur("act")])
                cbc = t(lambda e: e.matmul(psC[:, 0:1], ones1f[:], c_sc[:, 3:4],
                                           start=True, stop=True),
                        waits=[at("dve", cst)])
                stab = a(lambda e, s=s: e.activation(out=negc[:, s:s + 1], in_=psC[:, 0:1],
                                                     func=AF.Identity),
                         waits=[at("pe", cbc)])

                ybar()
                if s == 0:
                    for c in range(NC8):
                        for hf in range(2):
                            psem = "dw0" if hf == 0 else "dw1"
                            dw = w.op("sync", lambda e, c=c, hf=hf: e.dma_start(
                                out=wstage[:, hf, :],
                                in_=woT_d[128 * c:128 * (c + 1), 512 * hf:512 * (hf + 1)]),
                                [at("dve", wcast[-2] if len(wcast) >= 2 else 0)], psem, 16)
                            wcast.append(
                                v(lambda e, c=c, hf=hf: e.tensor_copy(
                                    out=slotA[:, c, 512 * hf:512 * (hf + 1)],
                                    in_=wstage[:, hf, :]),
                                  waits=[at(psem, dw), at("pe", pe_q),
                                         at("out", wbkv_mark)]))
                            go(lambda e, c=c, hf=hf: e.dma_start(
                                out=wo_b[c, :, 512 * hf:512 * (hf + 1)],
                                in_=slotA[:, c, 512 * hf:512 * (hf + 1)]),
                               waits=[at("dve", wcast[-1])])
                else:
                    w.op("sync", lambda e: None, [at("pe", pe_q), cur("out")], None)
                    for c in range(NC8):
                        y(lambda e, c=c: e.dma_start(out=slotA[:, c, :], in_=wo_b[c]))
                for r in range(8):
                    for c in range(NC8):
                        y(lambda e, r=r, c=c: e.dma_start(
                            out=slotB[:, c, 128 * r:128 * (r + 1)],
                            in_=wdq_s[128 * r:128 * (r + 1), 128 * c:128 * (c + 1)],
                            transpose=True),
                          waits=([at("pe", pe_kv), at("out", wbkv_mark)]
                                 if (r == 0 and c == 0) else ()))
                ybar()
                wready = (w.ctr["dve"], w.ctr["dma"])

                if _PHASE < 5:
                    continue
                exp_marks = {0: w.ctr["act"], 1: w.ctr["act"]}
                prev_ep_pe = w.ctr["pe"]
                pend_chunks = [[{"mms": [], "chain": (lambda: None)}
                                for _ in range(max(1, _EPDELAY // 4))]]
                eps_state = {"psE_dve": w.ctr["dve"], "psE_act": w.ctr["act"],
                             "ffn_pe": w.ctr["pe"], "xst_dve": w.ctr["dve"],
                             "xdw_dma": w.ctr["dxw"]}
                attn_state = {"norm": w.ctr["dve"], "div": w.ctr["act"], "pending_div": None}
                norm_done = {}
                S_pre = {}
                st_guard = [w.ctr["dve"]]
                pa_lndma = {}
                len_cp_done = [0]

                pa_dxm = {}

                def mk_pa_dx(tt, s2):
                    def thunk():
                        pa_dxm[tt] = w.op("sync", lambda e, tt=tt: e.dma_start(
                            out=xstage[:, 0, :], in_=x_d[s2, 128 * tt:128 * (tt + 1), :]),
                            [at("dve", eps_state["xst_dve"])], "dxr", 16)
                        eps_state["xst_dve"] = max(eps_state["xst_dve"], 0)
                    return thunk

                def mk_pa_cp(tt, s2):
                    def thunk():
                        sl = tt % 2
                        dx = pa_dxm[tt]
                        v(lambda e: e.tensor_reduce(
                            out=st_sum[:], in_=xstage[:, 0, :], axis=AX.X, op=ALU.add),
                          waits=[at("dxr", dx), at("dve", st_guard[0])])
                        sqm = a(lambda e: e.activation(
                            out=ep_sq, in_=xstage[:, 0, :], func=AF.Square, accum_out=st_sq[:]),
                            waits=[at("dxr", dx), at("dve", st_guard[0])])
                        v(lambda e: e.tensor_scalar_mul(out=st_m[:], in0=st_sum[:], scalar1=1.0 / D))
                        v(lambda e: e.tensor_scalar_mul(out=st_v[:], in0=st_sq[:], scalar1=1.0 / D),
                          waits=[at("act", sqm)])
                        v(lambda e: e.tensor_tensor(out=red_t[:], in0=st_m[:], in1=st_m[:],
                                                    op=ALU.mult))
                        v(lambda e: e.tensor_sub(out=st_v[:], in0=st_v[:], in1=red_t[:]))
                        v(lambda e: e.tensor_scalar_add(out=st_v[:], in0=st_v[:], scalar1=LN_EPS))
                        v(lambda e: e.tensor_scalar(out=st_t[:].bitcast(I32), in0=st_v[:].bitcast(I32),
                                                    scalar1=1, scalar2=None,
                                                    op0=ALU.logical_shift_right))
                        v(lambda e: e.tensor_scalar(out=st_t[:].bitcast(I32), in0=st_t[:].bitcast(I32),
                                                    scalar1=0, scalar2=None, op0=ALU.bitwise_not))
                        v(lambda e: e.tensor_scalar(out=st_r[:].bitcast(I32), in0=st_t[:].bitcast(I32),
                                                    scalar1=0x5F3759DF + 1, scalar2=None, op0=ALU.add))
                        for _ in range(3):
                            v(lambda e: e.tensor_tensor(out=red_t[:], in0=st_r[:], in1=st_r[:],
                                                        op=ALU.mult))
                            v(lambda e: e.tensor_tensor(out=red_t[:], in0=red_t[:], in1=st_v[:],
                                                        op=ALU.mult))
                            v(lambda e: e.tensor_scalar(out=red_t[:], in0=red_t[:],
                                                        scalar1=-0.5, scalar2=1.5,
                                                        op0=ALU.mult, op1=ALU.add))
                            v(lambda e: e.tensor_tensor(out=st_r[:], in0=st_r[:], in1=red_t[:],
                                                        op=ALU.mult))
                        lnw = v(lambda e, sl=sl: e.tensor_scalar(
                            out=ln1tm[:, sl, :], in0=xstage[:, 0, :],
                            scalar1=st_m[:, 0:1], scalar2=st_r[:, 0:1],
                            op0=ALU.subtract, op1=ALU.mult),
                            waits=([at("dln", pa_lndma[tt - 1])]
                                   if tt - 1 in pa_lndma else []))
                        eps_state["xst_dve"] = lnw
                        st_guard[0] = lnw
                        pa_lndma[tt] = w.op("sync", lambda e, tt=tt, sl=sl: e.dma_start(
                            out=ln1_s[128 * tt:128 * (tt + 1), :], in_=ln1tm[:, sl, :]),
                            [at("dve", lnw)], "dln", 16)
                    return thunk

                if s + 1 < min(B_CORE, _NSEQ) and _PHASE >= 3:
                    pa_dx_q = [mk_pa_dx(tt, s + 1) for tt in range(NT)]
                    pa_cp_q = [mk_pa_cp(tt, s + 1) for tt in range(NT)]
                else:
                    pa_dx_q, pa_cp_q = [], []
                for tqb in range(min(4, _NTQB)):
                    if tqb == 0:
                        tks = list(range(NT))
                        qc0 = CTX
                    else:
                        tks = list(range(12))
                        qc0 = (tqb - 1) * 512
                    for pair in range(min(8, _NPAIR)):
                        hA, hB = 2 * pair, 2 * pair + 1

                        def emit_S_for(pair2, qc2, kt, extra=()):
                            sl = kt % 2
                            s_ps = psA if sl == 0 else psB
                            ao = 128 * (kt - 12) if (tqb == 0 and kt >= 12) else 0
                            sw = [at("act", exp_marks[sl]), at(*tp_pair[pair2])] + list(extra)
                            t(lambda e, pair2=pair2, kt=kt, s_ps=s_ps, qc2=qc2, ao=ao: e.matmul(
                                s_ps[:, ao:512],
                                lkv_fm[0:64, pair2, 128 * kt:128 * (kt + 1)],
                                q_fm[0:64, pair2, qc2 + ao:qc2 + 512],
                                start=True, stop=True, tile_position=(0, 0)),
                              waits=sw)
                            pes = t(lambda e, pair2=pair2, kt=kt, s_ps=s_ps, qc2=qc2, ao=ao: e.matmul(
                                s_ps[:, 512 + ao:1024],
                                lkv_fm[64:128, pair2, 128 * kt:128 * (kt + 1)],
                                q_fm[64:128, pair2, qc2 + ao:qc2 + 512],
                                start=True, stop=True, tile_position=(64, 0)))
                            last_S[0] = pes
                            return pes

                        def emit_S(kt, extra=()):
                            if (pair, kt) in S_pre:
                                return S_pre.pop((pair, kt))
                            return emit_S_for(pair, qc0, kt, extra)

                        def emit_av(kt, psl, need, ao=0, hA=hA, hB=hB):
                            t(lambda e, kt=kt, hA=hA, psl=psl, ao=ao: e.matmul(
                                psC[0:64, ao:512], lkv_tm[:, kt, DH * hA:DH * (hA + 1)],
                                p_buf[:, psl, ao:512],
                                start=False, stop=False, tile_position=(0, 0)),
                              waits=need)
                            t(lambda e, kt=kt, hB=hB, psl=psl, ao=ao: e.matmul(
                                psC[64:128, ao:512], lkv_tm[:, kt, DH * hB:DH * (hB + 1)],
                                p_buf[:, psl, 512 + ao:1024],
                                start=False, stop=False, tile_position=(0, 64)))
                            t(lambda e, psl=psl, ao=ao: e.matmul(
                                psD[0:64, ao:512], ones64[:, 0:64], p_buf[:, psl, ao:512],
                                start=False, stop=False, tile_position=(0, 0)))
                            t(lambda e, psl=psl, ao=ao: e.matmul(
                                psD[64:128, ao:512], ones64[:, 0:64], p_buf[:, psl, 512 + ao:1024],
                                start=False, stop=False, tile_position=(0, 64)))

                        pes_next = emit_S(tks[0])
                        pend_av = []
                        for i, kt in enumerate(tks):
                            sl = kt % 2
                            psl = kt % 3
                            s_ps = psA if sl == 0 else psB
                            pes = pes_next
                            ao_e = 128 * (kt - 12) if (tqb == 0 and kt >= 12) else 0
                            em = a(lambda e, psl=psl, s_ps=s_ps, s=s, ao_e=ao_e: e.activation(
                                out=p_buf[:, psl, :].rearrange(
                                    "p (two c) -> p two c", two=2)[:, :, ao_e:],
                                in_=s_ps[:].rearrange(
                                    "p (two c) -> p two c", two=2)[:, :, ao_e:],
                                func=AF.Exp,
                                bias=negc[:, s:s + 1], scale=0.125),
                                waits=[at("pe", pes)])
                            exp_marks[sl] = em
                            need = [at("act", em)]
                            a_off = 128 * (kt - 12) if (tqb == 0 and kt >= 12) else 0
                            if tqb == 0 and kt >= 12:
                                mm = v(lambda e, psl=psl, a_off=a_off: e.tensor_tensor(
                                    out=p_buf[:, psl, :].rearrange("p (two c) -> p two c", two=2)[:, :, a_off:a_off + 128],
                                    in0=p_buf[:, psl, :].rearrange("p (two c) -> p two c", two=2)[:, :, a_off:a_off + 128],
                                    in1=masks[:, None, :].broadcast_to([128, 2, 128]),
                                    op=ALU.mult),
                                    waits=[at("act", em)])
                                need.append(at("dve", mm))
                            if i == 2:
                                if attn_state["pending_div"] is not None:
                                    attn_state["pending_div"]()
                                    attn_state["pending_div"] = None
                                zw = [at("dve", attn_state["norm"]), at("act", attn_state["div"])]
                                if pair == 0:
                                    zw.append(at("pe", prev_ep_pe))
                                t(lambda e: e.matmul(psC[:], zeros1b[:], bo_bf[0:1, 0:512],
                                                     start=True, stop=False),
                                  waits=zw)
                                t(lambda e: e.matmul(psD[:], zeros1b[:], bo_bf[0:1, 0:512],
                                                     start=True, stop=False))
                            pend_av.append((kt, psl, need, a_off))
                            if i >= 2:
                                emit_av(*pend_av.pop(0))
                            if i + 1 < len(tks):
                                pes_next = emit_S(tks[i + 1])
                            if i == 1 and pend_chain2[0] is not None:
                                pend_chain2[0]()
                                pend_chain2[0] = None
                            if (_PHASE >= 6 and pend_chunks and pend_chunks[0]
                                    and i >= (6 if pend_chunks[0][0].get("late") else 1)):
                                mm_q = pend_chunks[0][0]["mms"]
                                for _ in range(2):
                                    if mm_q:
                                        mm_q.pop(0)()
                            if i == 5 and pa_cp_q and len(pa_dxm) > len_cp_done[0]:
                                pa_cp_q.pop(0)()
                                len_cp_done[0] += 1
                        for args in pend_av:
                            emit_av(*args)
                        pend_av = []
                        t(lambda e: e.matmul(psC[:], zeros1b[:], bo_bf[0:1, 0:512],
                                             start=False, stop=True))
                        avr = t(lambda e: e.matmul(psD[:], zeros1b[:], bo_bf[0:1, 0:512],
                                                   start=False, stop=True))

                        def mk_div(pair=pair, tqb=tqb, avr=avr):
                            def doit():
                                a(lambda e: e.activation(out=ep_sq[:, 0:512], in_=psD[:], func=AF.Ln),
                                  waits=[at("pe", avr), at("dve", attn_state["norm"])])
                                dvm = a(lambda e: e.activation(out=div_t[:], in_=ep_sq[:, 0:512],
                                                               func=AF.Exp, scale=-1.0))
                                attn_state["div"] = dvm
                                nm = v(lambda e, pair=pair, tqb=tqb: e.tensor_tensor(
                                    out=out_all[:, tqb % 2, pair, :], in0=psC[:], in1=div_t[:],
                                    op=ALU.mult),
                                    waits=[at("act", dvm)])
                                attn_state["norm"] = nm
                                norm_done[tqb] = nm
                            return doit

                        attn_state["pending_div"] = mk_div()
                        if _PHASE >= 6 and pend_chunks and pend_chunks[0]:
                            ch = pend_chunks[0].pop(0)
                            for f in ch["mms"]:
                                f()
                            pend_chain2[0] = ch["chain"]()
                            if not pend_chunks[0]:
                                pend_chunks.pop(0)
                        if pair < 7:
                            for kt2 in range(2):
                                S_pre[(pair + 1, kt2)] = emit_S_for(pair + 1, qc0, kt2)
                        elif tqb < min(4, _NTQB) - 1:
                            for kt2 in range(2):
                                S_pre[(0, kt2)] = emit_S_for(0, tqb * 512, kt2)
                        if _dbg and s == 0 and tqb == 0 and pair == 0:
                            attn_state["pending_div"]()
                            attn_state["pending_div"] = None
                            y(lambda e: e.dma_start(out=dbgf_d[0, :, 0:512], in_=div_t[:]),
                              waits=[at("dve", attn_state["norm"])])
                            y(lambda e: e.dma_start(out=dbgb_d[1, :, 0:512], in_=out_all[:, 0, :]))
                            y(lambda e: e.dma_start(out=dbgf_d[1, :, 0:2], in_=negc[:]))
                        if pa_dx_q:
                            pa_dx_q.pop(0)()

                    def make_ep_chunks(s=s, tqb=tqb, alt=False):
                        par = tqb % 2
                        eps = {}
                        cdst = {"dve": 0, "act": 0}

                        def tgt(k, fb):
                            if alt and k % 2 == 1:
                                return psC[:] if fb == 0 else psD[:]
                            return psE[:, 512 * fb:512 * (fb + 1)]

                        def partA(ts4):
                            row0 = tqb * 512 + ts4 * 128
                            pe_wo = w.ctr["pe"]
                            isalt = alt and ts4 % 2 == 1
                            pslo = psC[:] if isalt else psE[:, 0:512]
                            pshi = psD[:] if isalt else psE[:, 512:1024]
                            dxr = w.op("sync", lambda e, s=s, row0=row0: e.dma_start(
                                out=xstage[:, 0, :], in_=x_d[s, row0:row0 + 128, :]),
                                [at("dve", eps_state["xst_dve"])], "dxr", 16)
                            if isalt:
                                a(lambda e: e.activation(out=ep_sq[:, 0:512], in_=psC[:],
                                                         func=AF.Square, accum_out=st_sq[:]),
                                  waits=[at("pe", pe_wo), at("dve", st_guard[0])])
                                sqa = a(lambda e: e.activation(out=ep_sq[:, 512:1024], in_=psD[:],
                                                               func=AF.Square, accum_out=st_t[:]))
                                cdst["act"] = sqa
                            else:
                                sqa = a(lambda e: e.activation(out=ep_sq, in_=psE[:],
                                                               func=AF.Square,
                                                               accum_out=st_sq[:]),
                                        waits=[at("pe", pe_wo), at("dve", st_guard[0])])
                                eps_state["psE_act"] = sqa
                            if isalt:
                                a(lambda e: e.activation(out=ep_sq[:, 0:512], in_=psC[:],
                                                         func=AF.Identity, accum_out=st_sum[:]))
                                sid = a(lambda e: e.activation(out=ep_sq[:, 512:1024], in_=psD[:],
                                                               func=AF.Identity,
                                                               accum_out=st_sum2[:]))
                                v(lambda e: e.tensor_add(out=st_sq[:], in0=st_sq[:], in1=st_t[:]),
                                  waits=[at("act", sid)])
                                v(lambda e: e.tensor_add(out=st_sum[:], in0=st_sum[:],
                                                         in1=st_sum2[:]))
                            else:
                                sid = a(lambda e: e.activation(out=ep_sq, in_=psE[:],
                                                               func=AF.Identity,
                                                               accum_out=st_sum[:]))
                                eps_state["psE_act"] = sid
                            if isalt:
                                cdst["act"] = sid
                            v(lambda e: e.tensor_scalar_mul(out=st_m[:], in0=st_sum[:],
                                                            scalar1=1.0 / D),
                              waits=[at("act", sid)])
                            v(lambda e: e.tensor_scalar_mul(out=st_v[:], in0=st_sq[:], scalar1=1.0 / D),
                              waits=[cur("act")])
                            v(lambda e: e.tensor_tensor(out=red_t[:], in0=st_m[:], in1=st_m[:], op=ALU.mult))
                            v(lambda e: e.tensor_sub(out=st_v[:], in0=st_v[:], in1=red_t[:]))
                            v(lambda e: e.tensor_scalar_add(out=st_v[:], in0=st_v[:], scalar1=LN_EPS))
                            newton_rstd()
                            v(lambda e: e.tensor_tensor(out=st_beta[:], in0=st_m[:], in1=st_r[:], op=ALU.mult),
                              waits=[cur("act")])
                            v(lambda e: e.tensor_scalar_mul(out=st_beta[:], in0=st_beta[:], scalar1=-1.0))
                            xpw = v(lambda e: e.tensor_scalar_add(out=xprime[:], in0=xstage[:, 0, :],
                                                                  scalar1=st_beta[:, 0:1]),
                                    waits=[at("dxr", dxr), at("dxw", eps_state["xdw_dma"])])
                            eps_state["xst_dve"] = xpw
                            if isalt:
                                v(lambda e, ts4=ts4: e.scalar_tensor_tensor(
                                    out=y_t[:, ts4, 0:512], in0=psC[:], scalar=st_r[:, 0:1],
                                    in1=xprime[:, 0:512], op0=ALU.mult, op1=ALU.add))
                                yw = v(lambda e, ts4=ts4: e.scalar_tensor_tensor(
                                    out=y_t[:, ts4, 512:1024], in0=psD[:], scalar=st_r[:, 0:1],
                                    in1=xprime[:, 512:1024], op0=ALU.mult, op1=ALU.add))
                                cdst["dve"] = yw
                            else:
                                yw = v(lambda e, ts4=ts4: e.scalar_tensor_tensor(
                                    out=y_t[:, ts4, :], in0=psE[:], scalar=st_r[:, 0:1],
                                    in1=xprime[:],
                                    op0=ALU.mult, op1=ALU.add))
                                eps_state["psE_dve"] = yw
                            if ts4 == 3:
                                partA_post(ts4, row0, yw)
                                return None
                            return lambda: partA_post(ts4, row0, yw)

                        def partA_post(ts4, row0, yw):
                            v(lambda e, ts4=ts4: e.tensor_reduce(out=st_sum[:], in_=y_t[:, ts4, :],
                                                                 axis=AX.X, op=ALU.add))
                            sqb = a(lambda e, ts4=ts4: e.activation(out=ep_sq, in_=y_t[:, ts4, :],
                                                                    func=AF.Square, accum_out=st_sq[:]),
                                    waits=[at("dve", yw)])
                            v(lambda e: e.tensor_scalar_mul(out=st_m[:], in0=st_sum[:], scalar1=1.0 / D))
                            v(lambda e: e.tensor_scalar_mul(out=st_v[:], in0=st_sq[:], scalar1=1.0 / D),
                              waits=[at("act", sqb)])
                            v(lambda e: e.tensor_tensor(out=red_t[:], in0=st_m[:], in1=st_m[:], op=ALU.mult))
                            v(lambda e: e.tensor_sub(out=st_v[:], in0=st_v[:], in1=red_t[:]))
                            v(lambda e: e.tensor_scalar_add(out=st_v[:], in0=st_v[:], scalar1=LN_EPS))
                            newton_rstd()
                            v(lambda e, ts4=ts4: e.tensor_scalar(out=ln2_t[:], in0=y_t[:, ts4, :],
                                                                 scalar1=st_m[:, 0:1], scalar2=st_r[:, 0:1],
                                                                 op0=ALU.subtract, op1=ALU.mult),
                              waits=[cur("act"), at("out", ep_out_mark[0])])
                            st_guard[0] = w.ctr["dve"]
                            v(lambda e: e.tensor_reduce(
                                out=bs_max[:, :, None], in_=ln2_t[:].rearrange("p (b k) -> p b k", k=32),
                                axis=AX.X, op=ALU.max, apply_absolute_value=True))
                            v(lambda e: e.tensor_scalar(out=bs_s[:], in0=bs_max[:], scalar1=1.0 / 7.5,
                                                        scalar2=1e-30, op0=ALU.mult, op1=ALU.max))
                            v(lambda e: e.tensor_scalar_mul(out=bs_s64[:], in0=bs_s[:], scalar1=64.0))
                            v(lambda e: e.reciprocal(out=bs_max[:], in_=bs_s64[:]))
                            v(lambda e: e.tensor_tensor(
                                out=xprime[:].rearrange("p (b k) -> p b k", k=32),
                                in0=ln2_t[:].rearrange("p (b k) -> p b k", k=32),
                                in1=bs_max[:, :, None].broadcast_to([128, 32, 32]), op=ALU.mult))
                            v(lambda e: e.tensor_copy(out=y8_t, in_=xprime[:]))
                            xdw = v(lambda e: e.tensor_tensor(
                                out=xd_tm.rearrange("p (b k) -> p b k", k=32),
                                in0=y8_t.rearrange("p (b k) -> p b k", k=32),
                                in1=bs_s64[:, :, None].broadcast_to([128, 32, 32]), op=ALU.mult))
                            eps_state["xdw_dma"] = w.op("sync", lambda e, row0=row0: e.dma_start(
                                out=xd_s[row0:row0 + 128, :], in_=xd_tm),
                                [at("dve", xdw)], "dxw", 16)

                        def partB(ts4):
                            row0 = tqb * 512 + ts4 * 128
                            pe_ffn = w.ctr["pe"]
                            isalt = alt and ts4 % 2 == 1
                            pslo = psC[:] if isalt else psE[:, 0:512]
                            pshi = psD[:] if isalt else psE[:, 512:1024]
                            if isalt:
                                a(lambda e: e.activation(out=ep_sq[:, 0:512], in_=psC[:],
                                                         func=AF.Square, accum_out=st_sq[:]),
                                  waits=[at("pe", pe_ffn), at("dve", st_guard[0])])
                                sqc = a(lambda e: e.activation(out=ep_sq[:, 512:1024], in_=psD[:],
                                                               func=AF.Square, accum_out=st_t[:]))
                                cdst["act"] = sqc
                            else:
                                sqc = a(lambda e: e.activation(out=ep_sq, in_=psE[:],
                                                               func=AF.Square,
                                                               accum_out=st_sq[:]),
                                        waits=[at("pe", pe_ffn), at("dve", st_guard[0])])
                                eps_state["psE_act"] = sqc
                            if isalt:
                                a(lambda e: e.activation(out=ep_sq[:, 0:512], in_=psC[:],
                                                         func=AF.Identity, accum_out=st_sum[:]))
                                sid = a(lambda e: e.activation(out=ep_sq[:, 512:1024], in_=psD[:],
                                                               func=AF.Identity,
                                                               accum_out=st_sum2[:]))
                                v(lambda e: e.tensor_add(out=st_sq[:], in0=st_sq[:], in1=st_t[:]),
                                  waits=[at("act", sid)])
                                v(lambda e: e.tensor_add(out=st_sum[:], in0=st_sum[:],
                                                         in1=st_sum2[:]))
                            else:
                                sid = a(lambda e: e.activation(out=ep_sq, in_=psE[:],
                                                               func=AF.Identity,
                                                               accum_out=st_sum[:]))
                                eps_state["psE_act"] = sid
                            if isalt:
                                cdst["act"] = sid
                            v(lambda e: e.tensor_scalar_mul(out=st_m[:], in0=st_sum[:],
                                                            scalar1=1.0 / D),
                              waits=[at("act", sid)])
                            v(lambda e: e.tensor_scalar_mul(out=st_v[:], in0=st_sq[:], scalar1=1.0 / D),
                              waits=[cur("act")])
                            v(lambda e: e.tensor_tensor(out=red_t[:], in0=st_m[:], in1=st_m[:], op=ALU.mult))
                            v(lambda e: e.tensor_sub(out=st_v[:], in0=st_v[:], in1=red_t[:]))
                            v(lambda e: e.tensor_scalar_add(out=st_v[:], in0=st_v[:], scalar1=LN_EPS))
                            return lambda: partB_post(ts4, row0)

                        def partB_post(ts4, row0):
                            isalt = alt and ts4 % 2 == 1
                            newton_rstd()
                            v(lambda e: e.tensor_tensor(out=st_beta[:], in0=st_m[:], in1=st_r[:], op=ALU.mult),
                              waits=[cur("act")])
                            v(lambda e: e.tensor_scalar_mul(out=st_beta[:], in0=st_beta[:], scalar1=-1.0))
                            v(lambda e, ts4=ts4: e.tensor_scalar_add(out=xprime[:], in0=y_t[:, ts4, :],
                                                                     scalar1=st_beta[:, 0:1]))
                            if isalt:
                                v(lambda e: e.scalar_tensor_tensor(
                                    out=ln2_t[:, 0:512], in0=psC[:], scalar=st_r[:, 0:1],
                                    in1=xprime[:, 0:512], op0=ALU.mult, op1=ALU.add),
                                    waits=[at("out", ep_out_mark[0])])
                                ow = v(lambda e: e.scalar_tensor_tensor(
                                    out=ln2_t[:, 512:1024], in0=psD[:], scalar=st_r[:, 0:1],
                                    in1=xprime[:, 512:1024], op0=ALU.mult, op1=ALU.add))
                                cdst["dve"] = ow
                            else:
                                ow = v(lambda e: e.scalar_tensor_tensor(
                                    out=ln2_t[:], in0=psE[:], scalar=st_r[:, 0:1], in1=xprime[:],
                                    op0=ALU.mult, op1=ALU.add),
                                    waits=[at("out", ep_out_mark[0])])
                                eps_state["psE_dve"] = ow
                            st_guard[0] = w.ctr["dve"]
                            ep_out_mark[0] = go(lambda e, s=s, row0=row0: e.dma_start(
                                out=out_d[s, row0:row0 + 128, :], in_=ln2_t[:]),
                                waits=[at("dve", ow)])

                        def mm_waits(k):
                            if alt and k % 2 == 1:
                                if cdst["dve"] == 0:
                                    return [at("dve", max(attn_state["norm"],
                                                          norm_done.get(tqb, 0))),
                                            at("dma", wready[1]),
                                            at("act", attn_state["div"])]
                                return [at("dve", max(cdst["dve"], norm_done.get(tqb, 0))),
                                        at("dma", wready[1]), at("act", cdst["act"])]
                            return [at("dve", max(eps_state["psE_dve"],
                                                  norm_done.get(tqb, 0))),
                                    at("dma", wready[1]),
                                    at("act", eps_state["psE_act"])]

                        def a_mms(k, fb):
                            out = []
                            for c in range(NC8):
                                out.append(lambda c=c, k=k, fb=fb: t(
                                    lambda e, fb=fb, c=c, ts4=k: e.matmul(
                                        tgt(k, fb),
                                        out_all[:, par, c, 128 * ts4:128 * (ts4 + 1)],
                                        slotA[:, c, 512 * fb:512 * (fb + 1)],
                                        start=(c == 0), stop=False),
                                    waits=(mm_waits(k) if (fb == 0 and c == 0) else ())))
                            out.append(lambda k=k, fb=fb: t(lambda e, k=k, fb=fb: e.matmul(
                                tgt(k, fb),
                                ones1b[:], bo_bf[0:1, 512 * fb:512 * (fb + 1)],
                                start=False, stop=True)))
                            return out

                        def b_mms(k, fb):
                            out = []

                            def first(k=k, fb=fb):
                                if k == 0 and fb == 0:
                                    w.op("sync", lambda e: None,
                                         [cur("dma"), cur("dxw"),
                                          at("pe", eps_state["ffn_pe"])], None)
                                    for c2 in range(NC8):
                                        w.op("sync", lambda e, c2=c2: e.dma_start(
                                            out=xd_fm[:, c2, :],
                                            in_=xd_s[tqb * 512:(tqb + 1) * 512,
                                                     128 * c2:128 * (c2 + 1)],
                                            transpose=True), (), "dtx", 16)
                                    eps["tpxd"] = w.ctr["dtx"]
                                t(lambda e, fb=fb, ts4=k, k=k: e.matmul(
                                    tgt(k, fb),
                                    xd_fm[:, 0, 128 * ts4:128 * (ts4 + 1)],
                                    slotB[:, 0, 512 * fb:512 * (fb + 1)],
                                    start=True, stop=False),
                                  waits=(([at("dtx", eps["tpxd"])] + mm_waits(k))
                                         if fb == 0 else ()))
                            out.append(first)
                            for c in range(1, NC8):
                                out.append(lambda c=c, k=k, fb=fb: t(
                                    lambda e, fb=fb, c=c, ts4=k, k=k: e.matmul(
                                        tgt(k, fb),
                                        xd_fm[:, c, 128 * ts4:128 * (ts4 + 1)],
                                        slotB[:, c, 512 * fb:512 * (fb + 1)],
                                        start=False, stop=False)))
                            out.append(lambda k=k, fb=fb: t(lambda e, k=k, fb=fb: e.matmul(
                                tgt(k, fb),
                                ones1b[:], fb_bf[0:1, 512 * fb:512 * (fb + 1)],
                                start=False, stop=True)))
                            return out

                        ckl = []
                        for k in range(4):
                            ckl.append({"mms": a_mms(k, 0) + a_mms(k, 1),
                                        "chain": (lambda k=k: partA(k))})
                        for k in range(4):
                            ckl.append({"mms": b_mms(k, 0) + b_mms(k, 1),
                                        "chain": (lambda k=k: partB(k)),
                                        "late": k == 0})
                        return ckl

                    if _PHASE >= 6:
                        chunks = make_ep_chunks(alt=(tqb == 3))
                        if tqb == 3:
                            if attn_state["pending_div"] is not None:
                                attn_state["pending_div"]()
                                attn_state["pending_div"] = None
                            if pend_chain2[0] is not None:
                                pend_chain2[0]()
                                pend_chain2[0] = None
                            for lst in pend_chunks:
                                for ch in lst:
                                    for f in ch["mms"]:
                                        f()
                                    c2 = ch["chain"]()
                                    if c2 is not None:
                                        c2()
                            pend_chunks.clear()
                            for ch in chunks:
                                for f in ch["mms"]:
                                    f()
                                c2 = ch["chain"]()
                                if c2 is not None:
                                    c2()
                        else:
                            pend_chunks.append(chunks)
                    prev_ep_pe = w.ctr["pe"]
                while pa_cp_q:
                    pa_cp_q.pop(0)()
                if attn_state["pending_div"] is not None:
                    attn_state["pending_div"]()
                    attn_state["pending_div"] = None

            w.final_wait("gpsimd", "out")

        for en in ENGINES:
            def make_body(en):
                def body(eng):
                    wk = Walk(en, eng, sems)
                    schedule(wk)
                    _CTR_FOR_DEBUG.update(wk.ctr)
                return body
            getattr(block, en)(make_body(en))
        _SEMS_FOR_DEBUG.update(sems)

    return nc


def _host_constants():
    import ml_dtypes
    inv = 1.0 / (10000.0 ** (np.arange(0, DH, 2, dtype=np.float64) / DH))
    pos = np.arange(SEQ, dtype=np.float64)
    fr = pos[:, None] * inv[None, :]
    emb = np.concatenate([fr, fr], axis=1)
    cos = np.cos(emb)
    sin = np.sin(emb)
    sinsw = np.concatenate([-sin[:, :32], sin[:, 32:]], axis=1)

    def to_tiles(tab, scale):
        t = (tab * scale).reshape(NT, 128, DH).transpose(1, 0, 2).reshape(128, NT * DH)
        return np.ascontiguousarray(t).astype(ml_dtypes.bfloat16)

    cosk = to_tiles(cos, 1.0)
    sinswk = to_tiles(sinsw, 1.0)

    p = np.arange(128)[:, None]
    c = np.arange(128)[None, :]
    mask = np.ascontiguousarray((c >= p).astype(np.float32)).astype(ml_dtypes.bfloat16)
    return cosk, sinswk, mask


_NC_CACHE = None


def kernel(L0_attn_x, L0_attn_Wq, L0_attn_Wkv, L0_attn_Wo, L0_attn_bo,
           L0_attn_ln_g, L0_attn_ln_b, L0_ffn_weight, L0_ffn_bias):
    global _NC_CACHE
    x = np.ascontiguousarray(np.asarray(L0_attn_x, dtype=np.float32))
    wqT = np.ascontiguousarray(np.asarray(L0_attn_Wq, dtype=np.float32).T)
    wkvT = np.ascontiguousarray(np.asarray(L0_attn_Wkv, dtype=np.float32).T)
    woT = np.ascontiguousarray(np.asarray(L0_attn_Wo, dtype=np.float32).T)
    wf = np.ascontiguousarray(np.asarray(L0_ffn_weight, dtype=np.float32))
    bo = np.ascontiguousarray(np.asarray(L0_attn_bo, dtype=np.float32))
    fb = np.ascontiguousarray(np.asarray(L0_ffn_bias, dtype=np.float32))
    g = np.ascontiguousarray(np.asarray(L0_attn_ln_g, dtype=np.float32))
    bb = np.ascontiguousarray(np.asarray(L0_attn_ln_b, dtype=np.float32))

    cosk, sinswk, mask = _host_constants()

    if _NC_CACHE is None:
        _NC_CACHE = build_nc()
    nc = _NC_CACHE

    in_maps = []
    for i in range(NCORES):
        in_maps.append({
            "x": np.ascontiguousarray(x[B_CORE * i:B_CORE * (i + 1)]),
            "wqT": wqT, "wkvT": wkvT, "woT": woT, "wf": wf,
            "bo": bo, "fb": fb, "lng": g, "lnb": bb,
            "cosk": cosk, "sinswk": sinswk, "mask": mask,
        })
    res = run_bass_kernel_spmd(nc, in_maps, core_ids=list(range(NCORES)))
    out = np.concatenate([np.asarray(res.results[i]["out"]) for i in range(NCORES)], axis=0)
    return out.astype(np.float32)


# version 46 (final)
# speedup vs baseline: 1.1899x; 1.0021x over previous
import os
import numpy as np
import contextlib
import concourse.bass as bass
import concourse.mybir as mybir
from concourse.bass_utils import run_bass_kernel_spmd

F32 = mybir.dt.float32
BF16 = mybir.dt.bfloat16
F8 = mybir.dt.float8e4
I32 = mybir.dt.int32
AF = mybir.ActivationFunctionType
ALU = mybir.AluOpType
AX = mybir.AxisListType

NCORES = 8
B_CORE = 2
SEQ = 2048
D = 1024
H = 16
DH = 64
CTX = 1536
LAT = 512
NT = SEQ // 128
NC8 = D // 128
LN_EPS = 1e-5
C_CLAMP = 80.0

ENGINES = ("sync", "tensor", "scalar", "vector", "gpsimd")
_SEMS_FOR_DEBUG = {}
_CTR_FOR_DEBUG = {}
_PHASE = int(os.environ.get("KERN_PHASE", "99"))
_NSEQ = int(os.environ.get("KERN_NSEQ", "2"))
_NTQB = int(os.environ.get("KERN_NTQB", "4"))
_NPAIR = int(os.environ.get("KERN_NPAIR", "8"))
_EP = int(os.environ.get("KERN_EP", "9"))
_EPDELAY = int(os.environ.get("KERN_EPDELAY", "4"))
_EP2 = int(os.environ.get("KERN_EP2", "9"))


class Walk:

    def __init__(self, active, eng, sems):
        self.active = active
        self.eng = eng
        self.sems = sems
        self.ctr = {k: 0 for k in sems}

    AUTO_CHAIN = {"vector": "dve", "scalar": "act"}

    def op(self, ename, fn, waits=(), inc=None, n=1):
        rw = [(s, (self.ctr[s] if v is None else v)) for (s, v) in waits]
        ac = self.AUTO_CHAIN.get(ename)
        if ac is not None and inc == ac:
            rw.append((ac, self.ctr[ac]))
        if self.active == ename:
            for s, v in rw:
                if v > 0:
                    self.eng.wait_ge(self.sems[s], v)
            inst = fn(self.eng)
            if inc is not None and inst is not None:
                inst.then_inc(self.sems[inc], n)
        if inc is not None:
            self.ctr[inc] += n
        return self.ctr[inc] if inc is not None else None

    def final_wait(self, ename, sem):
        v = self.ctr[sem]
        if self.active == ename:
            self.eng.wait_ge(self.sems[sem], v)


def cur(s):
    return (s, None)


def at(s, v):
    return (s, v)


def build_nc():
    nc = bass.Bass("TRN2", target_bir_lowering=False)

    x_d = nc.dram_tensor("x", [B_CORE, SEQ, D], F32, kind="ExternalInput").ap()
    wqT_d = nc.dram_tensor("wqT", [D, D], F32, kind="ExternalInput").ap()
    wkvT_d = nc.dram_tensor("wkvT", [D, D], F32, kind="ExternalInput").ap()
    woT_d = nc.dram_tensor("woT", [D, D], F32, kind="ExternalInput").ap()
    wf_d = nc.dram_tensor("wf", [D, D], F32, kind="ExternalInput").ap()
    bo_d = nc.dram_tensor("bo", [D], F32, kind="ExternalInput").ap()
    fb_d = nc.dram_tensor("fb", [D], F32, kind="ExternalInput").ap()
    g_d = nc.dram_tensor("lng", [DH], F32, kind="ExternalInput").ap()
    bb_d = nc.dram_tensor("lnb", [DH], F32, kind="ExternalInput").ap()
    cosk_d = nc.dram_tensor("cosk", [128, NT * DH], BF16, kind="ExternalInput").ap()
    sinswk_d = nc.dram_tensor("sinswk", [128, NT * DH], BF16, kind="ExternalInput").ap()
    mask_d = nc.dram_tensor("mask", [128, 128], BF16, kind="ExternalInput").ap()
    out_d = nc.dram_tensor("out", [B_CORE, SEQ, D], F32, kind="ExternalOutput").ap()

    _dbg = os.environ.get("KERN_DBG") == "1"
    _ik = "ExternalOutput" if _dbg else "Internal"
    ln1_s = nc.dram_tensor("ln1_s", [SEQ, D], BF16, kind=_ik).ap()
    q_s = nc.dram_tensor("q_s", [SEQ, D], BF16, kind=_ik).ap()
    lkv_s = nc.dram_tensor("lkv_s", [SEQ, D], BF16, kind=_ik).ap()
    xd_s = nc.dram_tensor("xd_s", [SEQ, D], BF16, kind=_ik).ap()
    wdq_s = nc.dram_tensor("wdq_s", [D, D], BF16, kind=_ik).ap()
    wq_b = nc.dram_tensor("wq_b", [NC8, 128, D], BF16, kind=_ik).ap()
    wkv_b = nc.dram_tensor("wkv_b", [NC8, 128, D], BF16, kind=_ik).ap()
    wo_b = nc.dram_tensor("wo_b", [NC8, 128, D], BF16, kind=_ik).ap()
    wd_b = nc.dram_tensor("wd_b", [NC8, 128, D], BF16, kind=_ik).ap()
    if _dbg:
        dbgb_d = nc.dram_tensor("dbgb", [4, 128, D], BF16, kind="ExternalOutput").ap()
        dbgf_d = nc.dram_tensor("dbgf", [4, 128, D], F32, kind="ExternalOutput").ap()
    mq_s = nc.dram_tensor("mq_s", [128], F32, kind="Internal").ap()
    mk_s = nc.dram_tensor("mk_s", [128], F32, kind="Internal").ap()

    with contextlib.ExitStack() as stack:
        E = stack.enter_context
        slotA = E(nc.sbuf_tensor("slotA", [128, NC8, D], BF16))
        slotB = E(nc.sbuf_tensor("slotB", [128, NC8, D], BF16))
        wstage = E(nc.sbuf_tensor("wstage", [128, 2, 512], F32))
        cosk = E(nc.sbuf_tensor("cosk_t", [128, NT, DH], BF16))
        sinswk = E(nc.sbuf_tensor("sinswk_t", [128, NT, DH], BF16))
        masks = E(nc.sbuf_tensor("masks", [128, 128], BF16))
        grep = E(nc.sbuf_tensor("grep", [128, DH], BF16))
        brep = E(nc.sbuf_tensor("brep", [128, DH], BF16))
        gbf32 = E(nc.sbuf_tensor("gbf32", [128, 2, DH], F32))
        bo_bf = E(nc.sbuf_tensor("bo_bf", [1, D], BF16))
        fb_bf = E(nc.sbuf_tensor("fb_bf", [1, D], BF16))
        ones64 = E(nc.sbuf_tensor("ones64", [128, DH], BF16))
        ones1b = E(nc.sbuf_tensor("ones1b", [1, 128], BF16))
        zeros1b = E(nc.sbuf_tensor("zeros1b", [1, 128], BF16))
        ones1f = E(nc.sbuf_tensor("ones1f", [1, 128], F32))

        xstage = E(nc.sbuf_tensor("xstage", [128, 2, D], F32))
        ln1tm = E(nc.sbuf_tensor("ln1tm", [128, 2, D], BF16))
        ropeA = E(nc.sbuf_tensor("ropeA", [128, D], BF16))
        ropeB = E(nc.sbuf_tensor("ropeB", [128, D], BF16))
        lkv_tm = E(nc.sbuf_tensor("lkv_tm", [128, NT, D], BF16))
        lkv_fm = E(nc.sbuf_tensor("lkv_fm", [128, NC8, SEQ], BF16))
        q_fm = E(nc.sbuf_tensor("q_fm", [128, NC8, SEQ], BF16))
        p_buf = E(nc.sbuf_tensor("p_buf", [128, 3, D], BF16))
        out_all = E(nc.sbuf_tensor("out_all", [128, 2, NC8, LAT], BF16))
        div_t = E(nc.sbuf_tensor("div_t", [128, LAT], F32))
        y_t = E(nc.sbuf_tensor("y_t", [128, 4, D], BF16))
        ln2_t = E(nc.sbuf_tensor("ln2_t", [128, D], F32))
        xprime = E(nc.sbuf_tensor("xprime", [128, D], F32))
        xd_fm = E(nc.sbuf_tensor("xd_fm", [128, NC8, LAT], BF16))
        st_sum = E(nc.sbuf_tensor("st_sum", [128, 1], F32))
        st_sum2 = E(nc.sbuf_tensor("st_sum2", [128, 1], F32))
        st_sq = E(nc.sbuf_tensor("st_sq", [128, 1], F32))
        st_m = E(nc.sbuf_tensor("st_m", [128, 1], F32))
        st_v = E(nc.sbuf_tensor("st_v", [128, 1], F32))
        st_r = E(nc.sbuf_tensor("st_r", [128, 1], F32))
        st_beta = E(nc.sbuf_tensor("st_beta", [128, 1], F32))
        st_t = E(nc.sbuf_tensor("st_t", [128, 1], F32))
        hsum = E(nc.sbuf_tensor("hsum", [128, H], F32))
        hsq = E(nc.sbuf_tensor("hsq", [128, H], F32))
        hm = E(nc.sbuf_tensor("hm", [128, H], F32))
        hv = E(nc.sbuf_tensor("hv", [128, H], F32))
        hr = E(nc.sbuf_tensor("hr", [128, H], F32))
        hmb = E(nc.sbuf_tensor("hmb", [128, H], BF16))
        hrb = E(nc.sbuf_tensor("hrb", [128, H], BF16))
        mq_run = E(nc.sbuf_tensor("mq_run", [128, 1], F32))
        mk_run = E(nc.sbuf_tensor("mk_run", [128, 1], F32))
        red_t = E(nc.sbuf_tensor("red_t", [128, 1], F32))
        c_sc = E(nc.sbuf_tensor("c_sc", [1, 4], F32))
        negc = E(nc.sbuf_tensor("negc", [128, 2], F32))
        bs_max = E(nc.sbuf_tensor("bs_max", [128, 32], F32))
        bs_s = E(nc.sbuf_tensor("bs_s", [128, 32], F32))
        bs_s64 = E(nc.sbuf_tensor("bs_s64", [128, 32], F32))
        constv = E(nc.sbuf_tensor("constv", [128, 8], F32))
        mqn2 = E(nc.sbuf_tensor("mqn2", [128, H], F32))
        mkn2 = E(nc.sbuf_tensor("mkn2", [128, H], F32))

        psA = E(nc.psum_tensor("psA", [128, D], F32))
        psB = E(nc.psum_tensor("psB", [128, D], F32))
        psC = E(nc.psum_tensor("psC", [128, 512], F32))
        psD = E(nc.psum_tensor("psD", [128, 512], F32))
        psE = E(nc.psum_tensor("psE", [128, D], F32))

        _cvals = [LN_EPS, -0.5, -1.0, 0.125, 1.0, 0.0, -2.0794415416798357, 0.5]
        for _i, _cv in enumerate(_cvals):
            nc.const_aps.register(F32, _cv, constv[:, _i:_i + 1])

        sem_names = ["dma", "dw0", "dw1", "out", "pe", "act", "dve", "gp", "dtx", "dxr", "dxw",
                     "dln", "dtp", "dfq", "dtq"]
        sems = {k: E(nc.semaphore("s_" + k)) for k in sem_names}
        block = E(nc.Block())

        sq_scr = y_t[:, 0:2, :].bitcast(F32).rearrange("p a b -> p (a b)")
        kvbf = p_buf[:, 0, :]
        c_row = div_t[0:1, 0:256]
        rope_o = xstage[:, 1, 0:512].bitcast(BF16)
        xd_tm = xprime[:, 0:512].bitcast(BF16)
        y8_t = ln2_t[:, 0:256].bitcast(F8)
        ep_sq = xstage[:, 1, :]
        rope_o2 = div_t[:].bitcast(BF16)
        rope_o3 = xstage[:, 1, 512:1024].bitcast(BF16)
        pasq = ropeA[:].bitcast(F32)

        def schedule(w: Walk):
            t = lambda fn, waits=(), inc="pe", n=1: w.op("tensor", fn, waits, inc, n)
            a = lambda fn, waits=(), inc="act", n=1: w.op("scalar", fn, waits, inc, n)
            v = lambda fn, waits=(), inc="dve", n=1: w.op("vector", fn, waits, inc, n)
            y = lambda fn, waits=(), inc="dma", n=16: w.op("sync", fn, waits, inc, n)
            go = lambda fn, waits=(), inc="out", n=16: w.op("gpsimd", fn, waits, inc, n)
            ybar = lambda: w.op("sync", lambda e: None, [cur("dma")], None)

            def newton_rstd(first_waits=()):
                a(lambda e: e.activation(out=st_r[:], in_=st_v[:], func=AF.Ln),
                  waits=[cur("dve")] + list(first_waits))
                a(lambda e: e.activation(out=st_r[:], in_=st_r[:], func=AF.Exp, scale=-0.5))

            wq_done = []
            wcast = []
            last_S = [0]
            pend_chain2 = [None]
            ffn_mark = {0: 0, 1: 0}
            ep_out_mark = [0]

            y(lambda e: e.dma_start(out=cosk[:], in_=cosk_d.rearrange("p (t k) -> p t k", k=DH)))
            y(lambda e: e.dma_start(out=sinswk[:], in_=sinswk_d.rearrange("p (t k) -> p t k", k=DH)))
            y(lambda e: e.dma_start(out=masks[:], in_=mask_d[:]))
            y(lambda e: e.dma_start(out=gbf32[:, 0, :], in_=g_d[None, :].broadcast_to([128, DH])))
            y(lambda e: e.dma_start(out=gbf32[:, 1, :], in_=bb_d[None, :].broadcast_to([128, DH])))
            y(lambda e: e.dma_start(out=wstage[0:1, 0, :], in_=bo_d[None, 0:512]))
            y(lambda e: e.dma_start(out=wstage[0:1, 1, :], in_=bo_d[None, 512:1024]))
            ld0 = y(lambda e: e.dma_start(out=c_row[:], in_=fb_d[None, 0:256]))
            v(lambda e: e.tensor_copy(out=grep[:], in_=gbf32[:, 0, :]), waits=[at("dma", ld0)])
            v(lambda e: e.tensor_copy(out=brep[:], in_=gbf32[:, 1, :]))
            v(lambda e: e.tensor_copy(out=bo_bf[0:1, 0:512], in_=wstage[0:1, 0, :]))
            bcst = v(lambda e: e.tensor_copy(out=bo_bf[0:1, 512:1024], in_=wstage[0:1, 1, :]))
            y(lambda e: e.dma_start(out=wstage[0:1, 0, :], in_=fb_d[None, 0:512]),
              waits=[at("dve", bcst)])
            ldf = y(lambda e: e.dma_start(out=wstage[0:1, 1, :], in_=fb_d[None, 512:1024]))
            v(lambda e: e.tensor_copy(out=fb_bf[0:1, 0:512], in_=wstage[0:1, 0, :]),
              waits=[at("dma", ldf)])
            v(lambda e: e.tensor_copy(out=fb_bf[0:1, 512:1024], in_=wstage[0:1, 1, :]))
            for _i, _cv in enumerate([LN_EPS, -0.5, -1.0, 0.125, 1.0, 0.0, -2.0794415416798357, 0.5]):
                v(lambda e, _i=_i, _cv=_cv: e.memset(constv[:, _i:_i + 1], _cv))
            v(lambda e: e.memset(ones64[:], 1.0))
            v(lambda e: e.memset(ones1b[:], 1.0))
            v(lambda e: e.memset(zeros1b[:], 0.0))
            v(lambda e: e.memset(ones1f[:], 1.0))

            for r in range(8):
                wsfull = wstage[:].rearrange("p a b -> p (a b)")
                dw = y(lambda e, r=r, wsfull=wsfull: e.dma_start(
                    out=wsfull, in_=wf_d[128 * r:128 * (r + 1), :]),
                    waits=[cur("dve") if r == 0 else at("dve", wq_done[r - 1])])
                v(lambda e, wsfull=wsfull: e.tensor_reduce(
                    out=bs_max[:, :, None], in_=wsfull.rearrange("p (b k) -> p b k", k=32),
                    axis=AX.X, op=ALU.max, apply_absolute_value=True),
                  waits=[at("dma", dw)])
                v(lambda e: e.tensor_scalar(out=bs_s[:], in0=bs_max[:], scalar1=1.0 / 7.5, scalar2=1e-30,
                                            op0=ALU.mult, op1=ALU.max))
                v(lambda e: e.tensor_scalar_mul(out=bs_s64[:], in0=bs_s[:], scalar1=64.0))
                v(lambda e: e.reciprocal(out=bs_max[:], in_=bs_s64[:]))
                v(lambda e, wsfull=wsfull: e.tensor_tensor(
                    out=xprime[:].rearrange("p (b k) -> p b k", k=32),
                    in0=wsfull.rearrange("p (b k) -> p b k", k=32),
                    in1=bs_max[:, :, None].broadcast_to([128, 32, 32]), op=ALU.mult))
                v(lambda e: e.tensor_copy(out=y8_t, in_=xprime[:]))
                wq_done.append(
                    v(lambda e: e.tensor_tensor(
                        out=xd_tm.rearrange("p (b k) -> p b k", k=32),
                        in0=y8_t.rearrange("p (b k) -> p b k", k=32),
                        in1=bs_s64[:, :, None].broadcast_to([128, 32, 32]), op=ALU.mult)))
                y(lambda e, r=r: e.dma_start(out=wdq_s[128 * r:128 * (r + 1), :], in_=xd_tm),
                  waits=[at("dve", wq_done[r])])

            for s in range(min(B_CORE, _NSEQ)):
                if _PHASE < 2:
                    break
                def emit_ln1_tp():
                    ybar()
                    w.op("sync", lambda e: None, [cur("dln")], None)
                    for c in range(NC8):
                        w.op("sync", lambda e, c=c: e.dma_start(
                            out=q_fm[:, c, :], in_=ln1_s[:, 128 * c:128 * (c + 1)],
                            transpose=True),
                            ([at("pe", last_S[0])] if (s > 0 and c == 0) else ()),
                            "dtp", 16)
                    return w.ctr["dtp"]

                tp_ln1 = 0
                if s > 0 and _PHASE >= 3:
                    tp_ln1 = emit_ln1_tp()

                if s == 0:
                    wload_q = []
                    for wd_ap, slot, wb in [(wqT_d, slotA, wq_b), (wkvT_d, slotB, wkv_b)]:
                        for c in range(NC8):
                            for hf in range(2):
                                def wload(wd_ap=wd_ap, slot=slot, wb=wb, c=c, hf=hf):
                                    psem = "dw0" if hf == 0 else "dw1"
                                    dw = w.op("sync", lambda e, wd_ap=wd_ap, c=c, hf=hf: e.dma_start(
                                        out=wstage[:, hf, :],
                                        in_=wd_ap[128 * c:128 * (c + 1), 512 * hf:512 * (hf + 1)]),
                                        [at("dve", wcast[-2] if len(wcast) >= 2 else 0)], psem, 16)
                                    wcast.append(
                                        v(lambda e, slot=slot, c=c, hf=hf: e.tensor_copy(
                                            out=slot[:, c, 512 * hf:512 * (hf + 1)],
                                            in_=wstage[:, hf, :]),
                                          waits=[at(psem, dw)]))
                                    go(lambda e, slot=slot, wb=wb, c=c, hf=hf: e.dma_start(
                                        out=wb[c, :, 512 * hf:512 * (hf + 1)],
                                        in_=slot[:, c, 512 * hf:512 * (hf + 1)]),
                                       waits=[at("dve", wcast[-1])])
                                wload_q.append(wload)
                else:
                    wbkv_mark = 0
                    w.op("sync", lambda e: None, [cur("pe"), cur("out")], None)
                    for c in range(NC8):
                        y(lambda e, c=c: e.dma_start(out=slotA[:, c, :], in_=wq_b[c]))
                        y(lambda e, c=c: e.dma_start(out=slotB[:, c, :], in_=wkv_b[c]))
                    ybar()
                    wkvq_ready = ("dma", w.ctr["dma"])

                p1_marks = {}
                for tt in range(NT if (_PHASE >= 3 and s == 0) else 0):
                    sl = tt % 2
                    if tt < 2:
                        xw = [cur("dve"), cur("act")] if s > 0 else []
                    else:
                        wv, wa = p1_marks[("stageA", tt - 2)]
                        xw = [at("dve", wv), at("act", wa)]
                    dx = y(lambda e, s=s, tt=tt, sl=sl: e.dma_start(
                        out=xstage[:, sl, :], in_=x_d[s, 128 * tt:128 * (tt + 1), :]),
                        waits=xw)
                    v(lambda e, sl=sl: e.tensor_reduce(
                        out=st_sum[:], in_=xstage[:, sl, :], axis=AX.X, op=ALU.add),
                      waits=[at("dma", dx)])
                    sqm = a(lambda e, sl=sl: e.activation(
                        out=sq_scr, in_=xstage[:, sl, :], func=AF.Square, accum_out=st_sq[:]),
                        waits=[at("dma", dx)] + ([cur("dve")] if (tt == 0 and s > 0) else []))
                    v(lambda e: e.tensor_scalar_mul(out=st_m[:], in0=st_sum[:], scalar1=1.0 / D))
                    v(lambda e: e.tensor_scalar_mul(out=st_v[:], in0=st_sq[:], scalar1=1.0 / D),
                      waits=[cur("act")])
                    v(lambda e: e.tensor_tensor(out=red_t[:], in0=st_m[:], in1=st_m[:], op=ALU.mult))
                    v(lambda e: e.tensor_sub(out=st_v[:], in0=st_v[:], in1=red_t[:]))
                    a(lambda e: e.activation(out=st_r[:], in_=st_v[:], func=AF.Ln, bias=LN_EPS),
                      waits=[cur("dve")])
                    a(lambda e: e.activation(out=st_r[:], in_=st_r[:], func=AF.Exp, scale=-0.5))
                    lnw = v(lambda e, sl=sl: e.tensor_scalar(
                        out=ln1tm[:, sl, :], in0=xstage[:, sl, :],
                        scalar1=st_m[:, 0:1], scalar2=st_r[:, 0:1],
                        op0=ALU.subtract, op1=ALU.mult),
                        waits=[cur("act"), cur("dma")])
                    y(lambda e, tt=tt, sl=sl: e.dma_start(
                        out=ln1_s[128 * tt:128 * (tt + 1), :], in_=ln1tm[:, sl, :]),
                      waits=[at("dve", lnw)])
                    p1_marks[("stageA", tt)] = (lnw, sqm)
                    for _ in range(2):
                        if wload_q:
                            wload_q.pop(0)()

                if s == 0:
                    while wload_q:
                        wload_q.pop(0)()
                    wkvq_ready = ("dve", wcast[-1])
                    wbkv_mark = w.ctr["out"]
                if _PHASE >= 3 and s == 0:
                    tp_ln1 = emit_ln1_tp()

                qbf = out_all[:, 0, 0:2, :].rearrange("p a b -> p (a b)")
                qropeA = out_all[:, 0, 2:4, :].rearrange("p a b -> p (a b)")
                qropeB = out_all[:, 0, 4:6, :].rearrange("p a b -> p (a b)")
                qrope_o = out_all[:, 0, 6:8, :].rearrange("p a b -> p (a b)")
                NB = NT if _PHASE >= 3 else 0
                for tt in range(NB + 2):
                    if tt < NB:
                        sl = tt % 2
                        kv_ps = psA if sl == 0 else psB
                        q_odd = tt % 2 == 1
                        kvw0 = [at("dtp", tp_ln1),
                                at("act", p1_marks.get(("kvE", tt - 2), 0)),
                                at(*wkvq_ready)]
                        if tt < 2:
                            kvw0.append(cur("act"))
                        for fb in range(2):
                            for c in range(NC8):
                                t(lambda e, fb=fb, c=c, tt=tt, kv_ps=kv_ps: e.matmul(
                                    kv_ps[:, 512 * fb:512 * (fb + 1)],
                                    q_fm[:, c, 128 * tt:128 * (tt + 1)],
                                    slotB[:, c, 512 * fb:512 * (fb + 1)],
                                    start=(c == 0), stop=(c == NC8 - 1)),
                                  waits=(kvw0 if (fb == 0 and c == 0) else ()))
                        pe_kv = w.ctr["pe"]
                        qw0 = [at("act", p1_marks.get(("qE", tt - 2), 0)),
                               at(*wkvq_ready)]
                        if tt < 2 and s > 0:
                            qw0 = [cur("act"), cur("dve"), at(*wkvq_ready)]
                        for fb in range(2):
                            q_ps = (psE[:, 512 * fb:512 * (fb + 1)] if q_odd
                                    else (psC[:] if fb == 0 else psD[:]))
                            for c in range(NC8):
                                t(lambda e, fb=fb, c=c, tt=tt, q_ps=q_ps: e.matmul(
                                    q_ps,
                                    q_fm[:, c, 128 * tt:128 * (tt + 1)],
                                    slotA[:, c, 512 * fb:512 * (fb + 1)],
                                    start=(c == 0), stop=(c == NC8 - 1)),
                                  waits=(qw0 if (fb == 0 and c == 0) else ()))
                        pe_q = w.ctr["pe"]
                        p1_marks[("peq", tt)] = pe_q
                        kvE = a(lambda e, kv_ps=kv_ps: e.activation(
                            out=kvbf, in_=kv_ps[:], func=AF.Identity),
                            waits=[at("pe", pe_kv),
                                   at("dve", p1_marks.get(("kvO2", tt - 1), 0))])
                        p1_marks[("kvE", tt)] = kvE
                        if q_odd:
                            qE = a(lambda e: e.activation(
                                out=qbf, in_=psE[:], func=AF.Identity),
                                waits=[at("pe", pe_q),
                                       at("dve", p1_marks.get(("qO2", tt - 1), 0))])
                        else:
                            a(lambda e: e.activation(
                                out=qbf[:, 0:512], in_=psC[:], func=AF.Identity),
                              waits=[at("pe", pe_q),
                                     at("dve", p1_marks.get(("qO2", tt - 1), 0))])
                            qE = a(lambda e: e.activation(
                                out=qbf[:, 512:1024], in_=psD[:], func=AF.Identity))
                        p1_marks[("qE", tt)] = qE

                    if tt < NB:
                        ro = (rope_o, rope_o2, rope_o3)[tt % 3]
                        p1_marks[("ro", tt)] = ro
                        v(lambda e, tt=tt: e.tensor_tensor(
                            out=ropeB[:].rearrange("p (h k) -> p h k", k=DH),
                            in0=kvbf.rearrange("p (h k) -> p h k", k=DH),
                            in1=cosk[:, tt, None, :].broadcast_to([128, H, DH]), op=ALU.mult),
                          waits=[at("act", kvE)])
                        kvO2 = v(lambda e, tt=tt: e.tensor_tensor(
                            out=ropeA[:].rearrange("p (h two k) -> p h two k", two=2, k=32),
                            in0=kvbf.rearrange("p (h two k) -> p h two k", two=2, k=32)[:, :, ::-1, :],
                            in1=sinswk[:, tt, :].rearrange("p (two k) -> p two k", two=2)[:, None, :, :]
                                .broadcast_to([128, H, 2, 32]),
                            op=ALU.mult))
                        p1_marks[("kvO2", tt)] = kvO2
                        p1_marks[("kvr", tt)] = v(lambda e, ro=ro: e.tensor_add(
                            out=ro, in0=ropeB[:], in1=ropeA[:]),
                            waits=[at("act", p1_marks.get(("hq", tt - 3), 0))])
                        v(lambda e, tt=tt: e.tensor_tensor(
                            out=qropeA.rearrange("p (h k) -> p h k", k=DH),
                            in0=qbf.rearrange("p (h k) -> p h k", k=DH),
                            in1=cosk[:, tt, None, :].broadcast_to([128, H, DH]), op=ALU.mult),
                          waits=[at("act", qE)])
                        qO2 = v(lambda e, tt=tt: e.tensor_tensor(
                            out=qropeB.rearrange("p (h two k) -> p h two k", two=2, k=32),
                            in0=qbf.rearrange("p (h two k) -> p h two k", two=2, k=32)[:, :, ::-1, :],
                            in1=sinswk[:, tt, :].rearrange("p (two k) -> p two k", two=2)[:, None, :, :]
                                .broadcast_to([128, H, 2, 32]),
                            op=ALU.mult))
                        p1_marks[("qO2", tt)] = qO2
                        qoro = (qrope_o, ln1tm[:, 0, :], ln1tm[:, 1, :])[tt % 3]
                        p1_marks[("qoro", tt)] = qoro
                        p1_marks[("qro", tt)] = v(lambda e, qoro=qoro: e.tensor_add(
                            out=qoro, in0=qropeA, in1=qropeB),
                            waits=([at("dma", p1_marks[("dq", tt - 3)])]
                                   if ("dq", tt - 3) in p1_marks else [])
                                  + ([at("act", p1_marks[("qsq", tt - 3)])]
                                     if ("qsq", tt - 3) in p1_marks else []))
                    if 2 <= tt <= NB + 1:
                        u = tt - 2
                        hsqr = v(lambda e: e.tensor_reduce(
                            out=hsq[:, :, None], in_=sq_scr.rearrange("p (h k) -> p h k", k=DH),
                            axis=AX.X, op=ALU.add),
                            waits=[at("act", p1_marks[("hq", u)])])
                        v(lambda e: e.tensor_scalar_mul(out=hm[:], in0=hsum[:], scalar1=1.0 / DH))
                        v(lambda e: e.tensor_scalar_mul(out=hv[:], in0=hsq[:], scalar1=1.0 / DH))
                        v(lambda e: e.tensor_tensor(out=hsq[:], in0=hm[:], in1=hm[:], op=ALU.mult))
                        v(lambda e: e.tensor_sub(out=hv[:], in0=hv[:], in1=hsq[:]))
                        a(lambda e: e.activation(out=hr[:], in_=hv[:], func=AF.Ln, bias=LN_EPS),
                          waits=[cur("dve")])
                        a(lambda e: e.activation(out=hr[:], in_=hr[:], func=AF.Exp, scale=-0.5))
                        v(lambda e: e.tensor_copy(out=hmb[:], in_=hm[:]), waits=[cur("act")])
                        v(lambda e: e.tensor_copy(out=hrb[:], in_=hr[:]))
                        v(lambda e, u=u: e.tensor_sub(
                            out=ropeA[:].rearrange("p (h k) -> p h k", k=DH),
                            in0=p1_marks[("ro", u)].rearrange("p (h k) -> p h k", k=DH),
                            in1=hmb[:, :, None].broadcast_to([128, H, DH])))
                        v(lambda e: e.tensor_tensor(
                            out=ropeA[:].rearrange("p (h k) -> p h k", k=DH),
                            in0=ropeA[:].rearrange("p (h k) -> p h k", k=DH),
                            in1=hrb[:, :, None].broadcast_to([128, H, DH]), op=ALU.mult))
                        v(lambda e: e.tensor_tensor(
                            out=ropeA[:].rearrange("p (h k) -> p h k", k=DH),
                            in0=ropeA[:].rearrange("p (h k) -> p h k", k=DH),
                            in1=grep[:, None, :].broadcast_to([128, H, DH]), op=ALU.mult))
                        lkvw = v(lambda e, u=u: e.tensor_tensor(
                            out=lkv_tm[:, u, :].rearrange("p (h k) -> p h k", k=DH),
                            in0=ropeA[:].rearrange("p (h k) -> p h k", k=DH),
                            in1=brep[:, None, :].broadcast_to([128, H, DH]), op=ALU.add),
                            waits=([cur("pe")] if (s > 0 and u == 0) else ()))
                        y(lambda e, u=u: e.dma_start(
                            out=lkv_s[128 * u:128 * (u + 1), :], in_=lkv_tm[:, u, :]),
                          waits=[at("dve", lkvw)])
                        ksq = a(lambda e, u=u: e.activation(out=sq_scr, in_=lkv_tm[:, u, :],
                                                            func=AF.Square),
                                waits=[at("dve", lkvw)])
                        knr = v(lambda e: e.tensor_reduce(
                            out=hsq[:, :, None], in_=sq_scr.rearrange("p (h k) -> p h k", k=DH),
                            axis=AX.X, op=ALU.add),
                            waits=[at("act", ksq)])
                        if u == 0:
                            v(lambda e: e.tensor_copy(out=mkn2[:], in_=hsq[:]))
                        else:
                            v(lambda e: e.tensor_max(out=mkn2[:], in0=mkn2[:], in1=hsq[:]))
                        qsq = a(lambda e, u=u: e.activation(
                            out=sq_scr[:], in_=p1_marks[("qoro", u)], func=AF.Square),
                            waits=[at("dve", p1_marks[("qro", u)]), at("dve", knr)])
                        p1_marks[("qsq", u)] = qsq
                        qnr = v(lambda e: e.tensor_reduce(
                            out=hsq[:, :, None], in_=sq_scr.rearrange("p (h k) -> p h k", k=DH),
                            axis=AX.X, op=ALU.add),
                            waits=[at("act", qsq)])
                        p1_marks[("qnr", u)] = qnr
                        if u == 0:
                            v(lambda e: e.tensor_copy(out=mqn2[:], in_=hsq[:]))
                        else:
                            v(lambda e: e.tensor_max(out=mqn2[:], in0=mqn2[:], in1=hsq[:]))
                        p1_marks[("dq", u)] = y(lambda e, u=u: e.dma_start(
                            out=q_s[128 * u:128 * (u + 1), :], in_=p1_marks[("qoro", u)]),
                            waits=[at("dve", p1_marks[("qro", u)])])

                    if 1 <= tt <= NB:
                        u1 = tt - 1
                        v(lambda e, u1=u1: e.tensor_reduce(
                            out=hsum[:, :, None],
                            in_=p1_marks[("ro", u1)].rearrange("p (h k) -> p h k", k=DH),
                            axis=AX.X, op=ALU.add))
                        p1_marks[("hq", u1)] = a(lambda e, u1=u1: e.activation(
                            out=sq_scr, in_=p1_marks[("ro", u1)], func=AF.Square),
                            waits=[at("dve", p1_marks[("kvr", u1)]),
                                   at("dve", p1_marks.get(("qnr", u1 - 1), 0))])

                peB = w.ctr["pe"]

                tp_pair = {}
                if _PHASE >= 3:
                    ybar()
                    for c in range(NC8):
                        psem = "dtq" if c >= 4 else "dtp"
                        w.op("sync", lambda e, c=c: e.dma_start(
                            out=lkv_fm[:, c, :], in_=lkv_s[:, 128 * c:128 * (c + 1)],
                            transpose=True),
                            ([at("pe", peB)] if c == 0 else ()), psem, 16)
                        mk = w.op("sync", lambda e, c=c: e.dma_start(
                            out=q_fm[:, c, :], in_=q_s[:, 128 * c:128 * (c + 1)],
                            transpose=True), (), psem, 16)
                        if c in (3, 7):
                            for c2 in range(c - 3, c + 1):
                                tp_pair[c2] = (psem, mk)

                ybar()

                if _PHASE < 4:
                    continue
                v(lambda e: e.tensor_reduce(out=mq_run[:], in_=mqn2[:], axis=AX.X, op=ALU.max))
                v(lambda e: e.tensor_reduce(out=mk_run[:], in_=mkn2[:], axis=AX.X, op=ALU.max))
                dmq = w.op("gpsimd", lambda e: e.dma_start(out=mq_s[:], in_=mq_run[:, 0]),
                           [cur("dve")], "dfq", 16)
                dmk = w.op("gpsimd", lambda e: e.dma_start(out=mk_s[:], in_=mk_run[:, 0]),
                           (), "dfq", 16)
                w.op("gpsimd", lambda e: e.dma_start(out=c_row[0:1, 0:128], in_=mq_s[None, :]),
                     [at("dfq", dmk)], "dfq", 16)
                dcr = w.op("gpsimd", lambda e: e.dma_start(out=c_row[0:1, 128:256], in_=mk_s[None, :]),
                           (), "dfq", 16)
                v(lambda e: e.tensor_reduce(out=c_sc[:, 0:1], in_=c_row[0:1, 0:128], axis=AX.X, op=ALU.max),
                  waits=[at("dfq", dcr)])
                v(lambda e: e.tensor_reduce(out=c_sc[:, 1:2], in_=c_row[0:1, 128:256], axis=AX.X, op=ALU.max))
                a(lambda e: e.activation(out=c_sc[:, 0:1], in_=c_sc[:, 0:1], func=AF.Ln),
                  waits=[cur("dve")])
                a(lambda e: e.activation(out=c_sc[:, 1:2], in_=c_sc[:, 1:2], func=AF.Ln))
                v(lambda e: e.tensor_add(out=c_sc[:, 2:3], in0=c_sc[:, 0:1], in1=c_sc[:, 1:2]),
                  waits=[cur("act")])
                a(lambda e: e.activation(out=c_sc[:, 2:3], in_=c_sc[:, 2:3], func=AF.Exp,
                                         scale=0.5, bias=-2.0794415416798357),
                  waits=[cur("dve")])
                cst = v(lambda e: e.tensor_scalar(out=c_sc[:, 3:4], in0=c_sc[:, 2:3],
                                                  scalar1=-1.0, scalar2=-C_CLAMP,
                                                  op0=ALU.mult, op1=ALU.max),
                        waits=[cur("act")])
                cbc = t(lambda e: e.matmul(psC[:, 0:1], ones1f[:], c_sc[:, 3:4],
                                           start=True, stop=True),
                        waits=[at("dve", cst)])
                stab = a(lambda e, s=s: e.activation(out=negc[:, s:s + 1], in_=psC[:, 0:1],
                                                     func=AF.Identity),
                         waits=[at("pe", cbc)])

                ybar()
                if s == 0:
                    for c in range(NC8):
                        for hf in range(2):
                            psem = "dw0" if hf == 0 else "dw1"
                            dw = w.op("sync", lambda e, c=c, hf=hf: e.dma_start(
                                out=wstage[:, hf, :],
                                in_=woT_d[128 * c:128 * (c + 1), 512 * hf:512 * (hf + 1)]),
                                [at("dve", wcast[-2] if len(wcast) >= 2 else 0)], psem, 16)
                            wcast.append(
                                v(lambda e, c=c, hf=hf: e.tensor_copy(
                                    out=slotA[:, c, 512 * hf:512 * (hf + 1)],
                                    in_=wstage[:, hf, :]),
                                  waits=[at(psem, dw), at("pe", pe_q),
                                         at("out", wbkv_mark)]))
                            go(lambda e, c=c, hf=hf: e.dma_start(
                                out=wo_b[c, :, 512 * hf:512 * (hf + 1)],
                                in_=slotA[:, c, 512 * hf:512 * (hf + 1)]),
                               waits=[at("dve", wcast[-1])])
                else:
                    w.op("sync", lambda e: None, [at("pe", pe_q), cur("out")], None)
                    for c in range(NC8):
                        y(lambda e, c=c: e.dma_start(out=slotA[:, c, :], in_=wo_b[c]))
                if s == 0:
                    for r in range(8):
                        for c in range(NC8):
                            y(lambda e, r=r, c=c: e.dma_start(
                                out=slotB[:, c, 128 * r:128 * (r + 1)],
                                in_=wdq_s[128 * r:128 * (r + 1), 128 * c:128 * (c + 1)],
                                transpose=True),
                              waits=([at("pe", pe_kv), at("out", wbkv_mark)]
                                     if (r == 0 and c == 0) else ()))
                    ybar()
                    wdm = w.ctr["dma"]
                    for c in range(NC8):
                        go(lambda e, c=c: e.dma_start(out=wd_b[c], in_=slotB[:, c, :]),
                           waits=([at("dma", wdm)] if c == 0 else ()))
                else:
                    w.op("sync", lambda e: None, [at("pe", pe_kv), cur("out")], None)
                    for c in range(NC8):
                        y(lambda e, c=c: e.dma_start(out=slotB[:, c, :], in_=wd_b[c]))
                    ybar()
                wready = (w.ctr["dve"], w.ctr["dma"])

                if _PHASE < 5:
                    continue
                exp_marks = {0: w.ctr["act"], 1: w.ctr["act"]}
                prev_ep_pe = w.ctr["pe"]
                pend_chunks = [[{"mms": [], "chain": (lambda: None)}
                                for _ in range(max(1, _EPDELAY // 4))]]
                eps_state = {"psE_dve": w.ctr["dve"], "psE_act": w.ctr["act"],
                             "ffn_pe": w.ctr["pe"], "xst_dve": w.ctr["dve"],
                             "xdw_dma": w.ctr["dxw"]}
                attn_state = {"norm": w.ctr["dve"], "div": w.ctr["act"], "pending_div": None}
                norm_done = {}
                S_pre = {}
                st_guard = [w.ctr["dve"]]
                pa_lndma = {}
                len_cp_done = [0]

                pa_dxm = {}

                def mk_pa_dx(tt, s2):
                    def thunk():
                        pa_dxm[tt] = w.op("sync", lambda e, tt=tt: e.dma_start(
                            out=xstage[:, 0, :], in_=x_d[s2, 128 * tt:128 * (tt + 1), :]),
                            [at("dve", eps_state["xst_dve"])], "dxr", 16)
                        eps_state["xst_dve"] = max(eps_state["xst_dve"], 0)
                    return thunk

                def mk_pa_cp(tt, s2):
                    def thunk():
                        sl = tt % 2
                        dx = pa_dxm[tt]
                        v(lambda e: e.tensor_reduce(
                            out=st_sum[:], in_=xstage[:, 0, :], axis=AX.X, op=ALU.add),
                          waits=[at("dxr", dx), at("dve", st_guard[0])])
                        sqm = a(lambda e: e.activation(
                            out=ep_sq, in_=xstage[:, 0, :], func=AF.Square, accum_out=st_sq[:]),
                            waits=[at("dxr", dx), at("dve", st_guard[0])])
                        v(lambda e: e.tensor_scalar_mul(out=st_m[:], in0=st_sum[:], scalar1=1.0 / D))
                        v(lambda e: e.tensor_scalar_mul(out=st_v[:], in0=st_sq[:], scalar1=1.0 / D),
                          waits=[at("act", sqm)])
                        v(lambda e: e.tensor_tensor(out=red_t[:], in0=st_m[:], in1=st_m[:],
                                                    op=ALU.mult))
                        v(lambda e: e.tensor_sub(out=st_v[:], in0=st_v[:], in1=red_t[:]))
                        v(lambda e: e.tensor_scalar_add(out=st_v[:], in0=st_v[:], scalar1=LN_EPS))
                        v(lambda e: e.tensor_scalar(out=st_t[:].bitcast(I32), in0=st_v[:].bitcast(I32),
                                                    scalar1=1, scalar2=None,
                                                    op0=ALU.logical_shift_right))
                        v(lambda e: e.tensor_scalar(out=st_t[:].bitcast(I32), in0=st_t[:].bitcast(I32),
                                                    scalar1=0, scalar2=None, op0=ALU.bitwise_not))
                        v(lambda e: e.tensor_scalar(out=st_r[:].bitcast(I32), in0=st_t[:].bitcast(I32),
                                                    scalar1=0x5F3759DF + 1, scalar2=None, op0=ALU.add))
                        for _ in range(3):
                            v(lambda e: e.tensor_tensor(out=red_t[:], in0=st_r[:], in1=st_r[:],
                                                        op=ALU.mult))
                            v(lambda e: e.tensor_tensor(out=red_t[:], in0=red_t[:], in1=st_v[:],
                                                        op=ALU.mult))
                            v(lambda e: e.tensor_scalar(out=red_t[:], in0=red_t[:],
                                                        scalar1=-0.5, scalar2=1.5,
                                                        op0=ALU.mult, op1=ALU.add))
                            v(lambda e: e.tensor_tensor(out=st_r[:], in0=st_r[:], in1=red_t[:],
                                                        op=ALU.mult))
                        lnw = v(lambda e, sl=sl: e.tensor_scalar(
                            out=ln1tm[:, sl, :], in0=xstage[:, 0, :],
                            scalar1=st_m[:, 0:1], scalar2=st_r[:, 0:1],
                            op0=ALU.subtract, op1=ALU.mult),
                            waits=([at("dln", pa_lndma[tt - 1])]
                                   if tt - 1 in pa_lndma else []))
                        eps_state["xst_dve"] = lnw
                        st_guard[0] = lnw
                        pa_lndma[tt] = w.op("sync", lambda e, tt=tt, sl=sl: e.dma_start(
                            out=ln1_s[128 * tt:128 * (tt + 1), :], in_=ln1tm[:, sl, :]),
                            [at("dve", lnw)], "dln", 16)
                    return thunk

                if s + 1 < min(B_CORE, _NSEQ) and _PHASE >= 3:
                    pa_dx_q = [mk_pa_dx(tt, s + 1) for tt in range(NT)]
                    pa_cp_q = [mk_pa_cp(tt, s + 1) for tt in range(NT)]
                else:
                    pa_dx_q, pa_cp_q = [], []
                for tqb in range(min(4, _NTQB)):
                    if tqb == 0:
                        tks = list(range(NT))
                        qc0 = CTX
                    else:
                        tks = list(range(12))
                        qc0 = (tqb - 1) * 512
                    for pair in range(min(8, _NPAIR)):
                        hA, hB = 2 * pair, 2 * pair + 1

                        def emit_S_for(pair2, qc2, kt, extra=()):
                            sl = kt % 2
                            s_ps = psA if sl == 0 else psB
                            ao = 128 * (kt - 12) if (tqb == 0 and kt >= 12) else 0
                            sw = [at("act", exp_marks[sl]), at(*tp_pair[pair2])] + list(extra)
                            t(lambda e, pair2=pair2, kt=kt, s_ps=s_ps, qc2=qc2, ao=ao: e.matmul(
                                s_ps[:, ao:512],
                                lkv_fm[0:64, pair2, 128 * kt:128 * (kt + 1)],
                                q_fm[0:64, pair2, qc2 + ao:qc2 + 512],
                                start=True, stop=True, tile_position=(0, 0)),
                              waits=sw)
                            pes = t(lambda e, pair2=pair2, kt=kt, s_ps=s_ps, qc2=qc2, ao=ao: e.matmul(
                                s_ps[:, 512 + ao:1024],
                                lkv_fm[64:128, pair2, 128 * kt:128 * (kt + 1)],
                                q_fm[64:128, pair2, qc2 + ao:qc2 + 512],
                                start=True, stop=True, tile_position=(64, 0)))
                            last_S[0] = pes
                            return pes

                        def emit_S(kt, extra=()):
                            if (pair, kt) in S_pre:
                                return S_pre.pop((pair, kt))
                            return emit_S_for(pair, qc0, kt, extra)

                        def emit_av(kt, psl, need, ao=0, hA=hA, hB=hB):
                            t(lambda e, kt=kt, hA=hA, psl=psl, ao=ao: e.matmul(
                                psC[0:64, ao:512], lkv_tm[:, kt, DH * hA:DH * (hA + 1)],
                                p_buf[:, psl, ao:512],
                                start=False, stop=False, tile_position=(0, 0)),
                              waits=need)
                            t(lambda e, kt=kt, hB=hB, psl=psl, ao=ao: e.matmul(
                                psC[64:128, ao:512], lkv_tm[:, kt, DH * hB:DH * (hB + 1)],
                                p_buf[:, psl, 512 + ao:1024],
                                start=False, stop=False, tile_position=(0, 64)))
                            t(lambda e, psl=psl, ao=ao: e.matmul(
                                psD[0:64, ao:512], ones64[:, 0:64], p_buf[:, psl, ao:512],
                                start=False, stop=False, tile_position=(0, 0)))
                            t(lambda e, psl=psl, ao=ao: e.matmul(
                                psD[64:128, ao:512], ones64[:, 0:64], p_buf[:, psl, 512 + ao:1024],
                                start=False, stop=False, tile_position=(0, 64)))

                        pes_next = emit_S(tks[0])
                        pend_av = []
                        for i, kt in enumerate(tks):
                            sl = kt % 2
                            psl = kt % 3
                            s_ps = psA if sl == 0 else psB
                            pes = pes_next
                            ao_e = 128 * (kt - 12) if (tqb == 0 and kt >= 12) else 0
                            em = a(lambda e, psl=psl, s_ps=s_ps, s=s, ao_e=ao_e: e.activation(
                                out=p_buf[:, psl, :].rearrange(
                                    "p (two c) -> p two c", two=2)[:, :, ao_e:],
                                in_=s_ps[:].rearrange(
                                    "p (two c) -> p two c", two=2)[:, :, ao_e:],
                                func=AF.Exp,
                                bias=negc[:, s:s + 1], scale=0.125),
                                waits=[at("pe", pes)])
                            exp_marks[sl] = em
                            need = [at("act", em)]
                            a_off = 128 * (kt - 12) if (tqb == 0 and kt >= 12) else 0
                            if tqb == 0 and kt >= 12:
                                mm = v(lambda e, psl=psl, a_off=a_off: e.tensor_tensor(
                                    out=p_buf[:, psl, :].rearrange("p (two c) -> p two c", two=2)[:, :, a_off:a_off + 128],
                                    in0=p_buf[:, psl, :].rearrange("p (two c) -> p two c", two=2)[:, :, a_off:a_off + 128],
                                    in1=masks[:, None, :].broadcast_to([128, 2, 128]),
                                    op=ALU.mult),
                                    waits=[at("act", em)])
                                need.append(at("dve", mm))
                            if i == 2:
                                if attn_state["pending_div"] is not None:
                                    attn_state["pending_div"]()
                                    attn_state["pending_div"] = None
                                zw = [at("dve", attn_state["norm"]), at("act", attn_state["div"])]
                                if pair == 0:
                                    zw.append(at("pe", prev_ep_pe))
                                t(lambda e: e.matmul(psC[:], zeros1b[:], bo_bf[0:1, 0:512],
                                                     start=True, stop=False),
                                  waits=zw)
                                t(lambda e: e.matmul(psD[:], zeros1b[:], bo_bf[0:1, 0:512],
                                                     start=True, stop=False))
                            pend_av.append((kt, psl, need, a_off))
                            if i >= 2:
                                emit_av(*pend_av.pop(0))
                            if i + 1 < len(tks):
                                pes_next = emit_S(tks[i + 1])
                            if i == 1 and pend_chain2[0] is not None:
                                pend_chain2[0]()
                                pend_chain2[0] = None
                            if (_PHASE >= 6 and pend_chunks and pend_chunks[0]
                                    and i >= (6 if pend_chunks[0][0].get("late") else 1)):
                                mm_q = pend_chunks[0][0]["mms"]
                                for _ in range(2):
                                    if mm_q:
                                        mm_q.pop(0)()
                            if i == 5 and pa_cp_q and len(pa_dxm) > len_cp_done[0]:
                                pa_cp_q.pop(0)()
                                len_cp_done[0] += 1
                        for args in pend_av:
                            emit_av(*args)
                        pend_av = []
                        t(lambda e: e.matmul(psC[:], zeros1b[:], bo_bf[0:1, 0:512],
                                             start=False, stop=True))
                        avr = t(lambda e: e.matmul(psD[:], zeros1b[:], bo_bf[0:1, 0:512],
                                                   start=False, stop=True))

                        def mk_div(pair=pair, tqb=tqb, avr=avr):
                            def doit():
                                a(lambda e: e.activation(out=ep_sq[:, 0:512], in_=psD[:], func=AF.Ln),
                                  waits=[at("pe", avr), at("dve", attn_state["norm"])])
                                dvm = a(lambda e: e.activation(out=div_t[:], in_=ep_sq[:, 0:512],
                                                               func=AF.Exp, scale=-1.0))
                                attn_state["div"] = dvm
                                nm = v(lambda e, pair=pair, tqb=tqb: e.tensor_tensor(
                                    out=out_all[:, tqb % 2, pair, :], in0=psC[:], in1=div_t[:],
                                    op=ALU.mult),
                                    waits=[at("act", dvm)])
                                attn_state["norm"] = nm
                                norm_done[tqb] = nm
                            return doit

                        attn_state["pending_div"] = mk_div()
                        if _PHASE >= 6 and pend_chunks and pend_chunks[0]:
                            ch = pend_chunks[0].pop(0)
                            for f in ch["mms"]:
                                f()
                            pend_chain2[0] = ch["chain"]()
                            if not pend_chunks[0]:
                                pend_chunks.pop(0)
                        if pair < 7:
                            for kt2 in range(2):
                                S_pre[(pair + 1, kt2)] = emit_S_for(pair + 1, qc0, kt2)
                        elif tqb < min(4, _NTQB) - 1:
                            for kt2 in range(2):
                                S_pre[(0, kt2)] = emit_S_for(0, tqb * 512, kt2)
                        if _dbg and s == 0 and tqb == 0 and pair == 0:
                            attn_state["pending_div"]()
                            attn_state["pending_div"] = None
                            y(lambda e: e.dma_start(out=dbgf_d[0, :, 0:512], in_=div_t[:]),
                              waits=[at("dve", attn_state["norm"])])
                            y(lambda e: e.dma_start(out=dbgb_d[1, :, 0:512], in_=out_all[:, 0, :]))
                            y(lambda e: e.dma_start(out=dbgf_d[1, :, 0:2], in_=negc[:]))
                        if pa_dx_q:
                            pa_dx_q.pop(0)()

                    def make_ep_chunks(s=s, tqb=tqb, alt=False):
                        par = tqb % 2
                        eps = {}
                        cdst = {"dve": 0, "act": 0}

                        def tgt(k, fb):
                            if alt and k % 2 == 1:
                                return psC[:] if fb == 0 else psD[:]
                            return psE[:, 512 * fb:512 * (fb + 1)]

                        def partA(ts4):
                            row0 = tqb * 512 + ts4 * 128
                            pe_wo = w.ctr["pe"]
                            isalt = alt and ts4 % 2 == 1
                            pslo = psC[:] if isalt else psE[:, 0:512]
                            pshi = psD[:] if isalt else psE[:, 512:1024]
                            dxr = w.op("sync", lambda e, s=s, row0=row0: e.dma_start(
                                out=xstage[:, 0, :], in_=x_d[s, row0:row0 + 128, :]),
                                [at("dve", eps_state["xst_dve"])], "dxr", 16)
                            if isalt:
                                a(lambda e: e.activation(out=ep_sq[:, 0:512], in_=psC[:],
                                                         func=AF.Square, accum_out=st_sq[:]),
                                  waits=[at("pe", pe_wo), at("dve", st_guard[0])])
                                sqa = a(lambda e: e.activation(out=ep_sq[:, 512:1024], in_=psD[:],
                                                               func=AF.Square, accum_out=st_t[:]))
                                cdst["act"] = sqa
                            else:
                                sqa = a(lambda e: e.activation(out=ep_sq, in_=psE[:],
                                                               func=AF.Square,
                                                               accum_out=st_sq[:]),
                                        waits=[at("pe", pe_wo), at("dve", st_guard[0])])
                                eps_state["psE_act"] = sqa
                            if isalt:
                                a(lambda e: e.activation(out=ep_sq[:, 0:512], in_=psC[:],
                                                         func=AF.Identity, accum_out=st_sum[:]))
                                sid = a(lambda e: e.activation(out=ep_sq[:, 512:1024], in_=psD[:],
                                                               func=AF.Identity,
                                                               accum_out=st_sum2[:]))
                                v(lambda e: e.tensor_add(out=st_sq[:], in0=st_sq[:], in1=st_t[:]),
                                  waits=[at("act", sid)])
                                v(lambda e: e.tensor_add(out=st_sum[:], in0=st_sum[:],
                                                         in1=st_sum2[:]))
                            else:
                                sid = a(lambda e: e.activation(out=ep_sq, in_=psE[:],
                                                               func=AF.Identity,
                                                               accum_out=st_sum[:]))
                                eps_state["psE_act"] = sid
                            if isalt:
                                cdst["act"] = sid
                            v(lambda e: e.tensor_scalar_mul(out=st_m[:], in0=st_sum[:],
                                                            scalar1=1.0 / D),
                              waits=[at("act", sid)])
                            v(lambda e: e.tensor_scalar_mul(out=st_v[:], in0=st_sq[:], scalar1=1.0 / D),
                              waits=[cur("act")])
                            v(lambda e: e.tensor_tensor(out=red_t[:], in0=st_m[:], in1=st_m[:], op=ALU.mult))
                            v(lambda e: e.tensor_sub(out=st_v[:], in0=st_v[:], in1=red_t[:]))
                            v(lambda e: e.tensor_scalar_add(out=st_v[:], in0=st_v[:], scalar1=LN_EPS))
                            newton_rstd()
                            v(lambda e: e.tensor_tensor(out=st_beta[:], in0=st_m[:], in1=st_r[:], op=ALU.mult),
                              waits=[cur("act")])
                            v(lambda e: e.tensor_scalar_mul(out=st_beta[:], in0=st_beta[:], scalar1=-1.0))
                            xpw = v(lambda e: e.tensor_scalar_add(out=xprime[:], in0=xstage[:, 0, :],
                                                                  scalar1=st_beta[:, 0:1]),
                                    waits=[at("dxr", dxr), at("dxw", eps_state["xdw_dma"])])
                            eps_state["xst_dve"] = xpw
                            if isalt:
                                v(lambda e, ts4=ts4: e.scalar_tensor_tensor(
                                    out=y_t[:, ts4, 0:512], in0=psC[:], scalar=st_r[:, 0:1],
                                    in1=xprime[:, 0:512], op0=ALU.mult, op1=ALU.add))
                                yw = v(lambda e, ts4=ts4: e.scalar_tensor_tensor(
                                    out=y_t[:, ts4, 512:1024], in0=psD[:], scalar=st_r[:, 0:1],
                                    in1=xprime[:, 512:1024], op0=ALU.mult, op1=ALU.add))
                                cdst["dve"] = yw
                            else:
                                yw = v(lambda e, ts4=ts4: e.scalar_tensor_tensor(
                                    out=y_t[:, ts4, :], in0=psE[:], scalar=st_r[:, 0:1],
                                    in1=xprime[:],
                                    op0=ALU.mult, op1=ALU.add))
                                eps_state["psE_dve"] = yw
                            if ts4 == 3:
                                partA_post(ts4, row0, yw)
                                return None
                            return lambda: partA_post(ts4, row0, yw)

                        def partA_post(ts4, row0, yw):
                            v(lambda e, ts4=ts4: e.tensor_reduce(out=st_sum[:], in_=y_t[:, ts4, :],
                                                                 axis=AX.X, op=ALU.add))
                            sqb = a(lambda e, ts4=ts4: e.activation(out=ep_sq, in_=y_t[:, ts4, :],
                                                                    func=AF.Square, accum_out=st_sq[:]),
                                    waits=[at("dve", yw)])
                            v(lambda e: e.tensor_scalar_mul(out=st_m[:], in0=st_sum[:], scalar1=1.0 / D))
                            v(lambda e: e.tensor_scalar_mul(out=st_v[:], in0=st_sq[:], scalar1=1.0 / D),
                              waits=[at("act", sqb)])
                            v(lambda e: e.tensor_tensor(out=red_t[:], in0=st_m[:], in1=st_m[:], op=ALU.mult))
                            v(lambda e: e.tensor_sub(out=st_v[:], in0=st_v[:], in1=red_t[:]))
                            v(lambda e: e.tensor_scalar_add(out=st_v[:], in0=st_v[:], scalar1=LN_EPS))
                            newton_rstd()
                            v(lambda e, ts4=ts4: e.tensor_scalar(out=ln2_t[:], in0=y_t[:, ts4, :],
                                                                 scalar1=st_m[:, 0:1], scalar2=st_r[:, 0:1],
                                                                 op0=ALU.subtract, op1=ALU.mult),
                              waits=[cur("act"), at("out", ep_out_mark[0])])
                            st_guard[0] = w.ctr["dve"]
                            v(lambda e: e.tensor_reduce(
                                out=bs_max[:, :, None], in_=ln2_t[:].rearrange("p (b k) -> p b k", k=32),
                                axis=AX.X, op=ALU.max, apply_absolute_value=True))
                            v(lambda e: e.tensor_scalar(out=bs_s[:], in0=bs_max[:], scalar1=1.0 / 7.5,
                                                        scalar2=1e-30, op0=ALU.mult, op1=ALU.max))
                            v(lambda e: e.tensor_scalar_mul(out=bs_s64[:], in0=bs_s[:], scalar1=64.0))
                            v(lambda e: e.reciprocal(out=bs_max[:], in_=bs_s64[:]))
                            v(lambda e: e.tensor_tensor(
                                out=xprime[:].rearrange("p (b k) -> p b k", k=32),
                                in0=ln2_t[:].rearrange("p (b k) -> p b k", k=32),
                                in1=bs_max[:, :, None].broadcast_to([128, 32, 32]), op=ALU.mult))
                            v(lambda e: e.tensor_copy(out=y8_t, in_=xprime[:]))
                            xdw = v(lambda e: e.tensor_tensor(
                                out=xd_tm.rearrange("p (b k) -> p b k", k=32),
                                in0=y8_t.rearrange("p (b k) -> p b k", k=32),
                                in1=bs_s64[:, :, None].broadcast_to([128, 32, 32]), op=ALU.mult))
                            eps_state["xdw_dma"] = w.op("sync", lambda e, row0=row0: e.dma_start(
                                out=xd_s[row0:row0 + 128, :], in_=xd_tm),
                                [at("dve", xdw)], "dxw", 16)

                        def partB(ts4):
                            row0 = tqb * 512 + ts4 * 128
                            pe_ffn = w.ctr["pe"]
                            isalt = alt and ts4 % 2 == 1
                            pslo = psC[:] if isalt else psE[:, 0:512]
                            pshi = psD[:] if isalt else psE[:, 512:1024]
                            if isalt:
                                a(lambda e: e.activation(out=ep_sq[:, 0:512], in_=psC[:],
                                                         func=AF.Square, accum_out=st_sq[:]),
                                  waits=[at("pe", pe_ffn), at("dve", st_guard[0])])
                                sqc = a(lambda e: e.activation(out=ep_sq[:, 512:1024], in_=psD[:],
                                                               func=AF.Square, accum_out=st_t[:]))
                                cdst["act"] = sqc
                            else:
                                sqc = a(lambda e: e.activation(out=ep_sq, in_=psE[:],
                                                               func=AF.Square,
                                                               accum_out=st_sq[:]),
                                        waits=[at("pe", pe_ffn), at("dve", st_guard[0])])
                                eps_state["psE_act"] = sqc
                            if isalt:
                                a(lambda e: e.activation(out=ep_sq[:, 0:512], in_=psC[:],
                                                         func=AF.Identity, accum_out=st_sum[:]))
                                sid = a(lambda e: e.activation(out=ep_sq[:, 512:1024], in_=psD[:],
                                                               func=AF.Identity,
                                                               accum_out=st_sum2[:]))
                                v(lambda e: e.tensor_add(out=st_sq[:], in0=st_sq[:], in1=st_t[:]),
                                  waits=[at("act", sid)])
                                v(lambda e: e.tensor_add(out=st_sum[:], in0=st_sum[:],
                                                         in1=st_sum2[:]))
                            else:
                                sid = a(lambda e: e.activation(out=ep_sq, in_=psE[:],
                                                               func=AF.Identity,
                                                               accum_out=st_sum[:]))
                                eps_state["psE_act"] = sid
                            if isalt:
                                cdst["act"] = sid
                            v(lambda e: e.tensor_scalar_mul(out=st_m[:], in0=st_sum[:],
                                                            scalar1=1.0 / D),
                              waits=[at("act", sid)])
                            v(lambda e: e.tensor_scalar_mul(out=st_v[:], in0=st_sq[:], scalar1=1.0 / D),
                              waits=[cur("act")])
                            v(lambda e: e.tensor_tensor(out=red_t[:], in0=st_m[:], in1=st_m[:], op=ALU.mult))
                            v(lambda e: e.tensor_sub(out=st_v[:], in0=st_v[:], in1=red_t[:]))
                            v(lambda e: e.tensor_scalar_add(out=st_v[:], in0=st_v[:], scalar1=LN_EPS))
                            return lambda: partB_post(ts4, row0)

                        def partB_post(ts4, row0):
                            isalt = alt and ts4 % 2 == 1
                            newton_rstd()
                            v(lambda e: e.tensor_tensor(out=st_beta[:], in0=st_m[:], in1=st_r[:], op=ALU.mult),
                              waits=[cur("act")])
                            v(lambda e: e.tensor_scalar_mul(out=st_beta[:], in0=st_beta[:], scalar1=-1.0))
                            v(lambda e, ts4=ts4: e.tensor_scalar_add(out=xprime[:], in0=y_t[:, ts4, :],
                                                                     scalar1=st_beta[:, 0:1]))
                            if isalt:
                                v(lambda e: e.scalar_tensor_tensor(
                                    out=ln2_t[:, 0:512], in0=psC[:], scalar=st_r[:, 0:1],
                                    in1=xprime[:, 0:512], op0=ALU.mult, op1=ALU.add),
                                    waits=[at("out", ep_out_mark[0])])
                                ow = v(lambda e: e.scalar_tensor_tensor(
                                    out=ln2_t[:, 512:1024], in0=psD[:], scalar=st_r[:, 0:1],
                                    in1=xprime[:, 512:1024], op0=ALU.mult, op1=ALU.add))
                                cdst["dve"] = ow
                            else:
                                ow = v(lambda e: e.scalar_tensor_tensor(
                                    out=ln2_t[:], in0=psE[:], scalar=st_r[:, 0:1], in1=xprime[:],
                                    op0=ALU.mult, op1=ALU.add),
                                    waits=[at("out", ep_out_mark[0])])
                                eps_state["psE_dve"] = ow
                            st_guard[0] = w.ctr["dve"]
                            ep_out_mark[0] = go(lambda e, s=s, row0=row0: e.dma_start(
                                out=out_d[s, row0:row0 + 128, :], in_=ln2_t[:]),
                                waits=[at("dve", ow)])

                        def mm_waits(k):
                            if alt and k % 2 == 1:
                                if cdst["dve"] == 0:
                                    return [at("dve", max(attn_state["norm"],
                                                          norm_done.get(tqb, 0))),
                                            at("dma", wready[1]),
                                            at("act", attn_state["div"])]
                                return [at("dve", max(cdst["dve"], norm_done.get(tqb, 0))),
                                        at("dma", wready[1]), at("act", cdst["act"])]
                            return [at("dve", max(eps_state["psE_dve"],
                                                  norm_done.get(tqb, 0))),
                                    at("dma", wready[1]),
                                    at("act", eps_state["psE_act"])]

                        def a_mms(k, fb):
                            out = []
                            for c in range(NC8):
                                out.append(lambda c=c, k=k, fb=fb: t(
                                    lambda e, fb=fb, c=c, ts4=k: e.matmul(
                                        tgt(k, fb),
                                        out_all[:, par, c, 128 * ts4:128 * (ts4 + 1)],
                                        slotA[:, c, 512 * fb:512 * (fb + 1)],
                                        start=(c == 0), stop=False),
                                    waits=(mm_waits(k) if (fb == 0 and c == 0) else ())))
                            out.append(lambda k=k, fb=fb: t(lambda e, k=k, fb=fb: e.matmul(
                                tgt(k, fb),
                                ones1b[:], bo_bf[0:1, 512 * fb:512 * (fb + 1)],
                                start=False, stop=True)))
                            return out

                        def b_mms(k, fb):
                            out = []

                            def first(k=k, fb=fb):
                                if k == 0 and fb == 0:
                                    w.op("sync", lambda e: None,
                                         [cur("dma"), cur("dxw"),
                                          at("pe", eps_state["ffn_pe"])], None)
                                    for c2 in range(NC8):
                                        w.op("sync", lambda e, c2=c2: e.dma_start(
                                            out=xd_fm[:, c2, :],
                                            in_=xd_s[tqb * 512:(tqb + 1) * 512,
                                                     128 * c2:128 * (c2 + 1)],
                                            transpose=True), (), "dtx", 16)
                                    eps["tpxd"] = w.ctr["dtx"]
                                t(lambda e, fb=fb, ts4=k, k=k: e.matmul(
                                    tgt(k, fb),
                                    xd_fm[:, 0, 128 * ts4:128 * (ts4 + 1)],
                                    slotB[:, 0, 512 * fb:512 * (fb + 1)],
                                    start=True, stop=False),
                                  waits=(([at("dtx", eps["tpxd"])] + mm_waits(k))
                                         if fb == 0 else ()))
                            out.append(first)
                            for c in range(1, NC8):
                                out.append(lambda c=c, k=k, fb=fb: t(
                                    lambda e, fb=fb, c=c, ts4=k, k=k: e.matmul(
                                        tgt(k, fb),
                                        xd_fm[:, c, 128 * ts4:128 * (ts4 + 1)],
                                        slotB[:, c, 512 * fb:512 * (fb + 1)],
                                        start=False, stop=False)))
                            out.append(lambda k=k, fb=fb: t(lambda e, k=k, fb=fb: e.matmul(
                                tgt(k, fb),
                                ones1b[:], fb_bf[0:1, 512 * fb:512 * (fb + 1)],
                                start=False, stop=True)))
                            return out

                        ckl = []
                        for k in range(4):
                            ckl.append({"mms": a_mms(k, 0) + a_mms(k, 1),
                                        "chain": (lambda k=k: partA(k))})
                        for k in range(4):
                            ckl.append({"mms": b_mms(k, 0) + b_mms(k, 1),
                                        "chain": (lambda k=k: partB(k)),
                                        "late": k == 0})
                        return ckl

                    if _PHASE >= 6:
                        chunks = make_ep_chunks(alt=(tqb == 3))
                        if tqb == 3:
                            if attn_state["pending_div"] is not None:
                                attn_state["pending_div"]()
                                attn_state["pending_div"] = None
                            if pend_chain2[0] is not None:
                                pend_chain2[0]()
                                pend_chain2[0] = None
                            for lst in pend_chunks:
                                for ch in lst:
                                    for f in ch["mms"]:
                                        f()
                                    c2 = ch["chain"]()
                                    if c2 is not None:
                                        c2()
                            pend_chunks.clear()
                            for ch in chunks:
                                for f in ch["mms"]:
                                    f()
                                c2 = ch["chain"]()
                                if c2 is not None:
                                    c2()
                        else:
                            pend_chunks.append(chunks)
                    prev_ep_pe = w.ctr["pe"]
                while pa_cp_q:
                    pa_cp_q.pop(0)()
                if attn_state["pending_div"] is not None:
                    attn_state["pending_div"]()
                    attn_state["pending_div"] = None

            w.final_wait("gpsimd", "out")

        for en in ENGINES:
            def make_body(en):
                def body(eng):
                    wk = Walk(en, eng, sems)
                    schedule(wk)
                    _CTR_FOR_DEBUG.update(wk.ctr)
                return body
            getattr(block, en)(make_body(en))
        _SEMS_FOR_DEBUG.update(sems)

    return nc


def _host_constants():
    import ml_dtypes
    inv = 1.0 / (10000.0 ** (np.arange(0, DH, 2, dtype=np.float64) / DH))
    pos = np.arange(SEQ, dtype=np.float64)
    fr = pos[:, None] * inv[None, :]
    emb = np.concatenate([fr, fr], axis=1)
    cos = np.cos(emb)
    sin = np.sin(emb)
    sinsw = np.concatenate([-sin[:, :32], sin[:, 32:]], axis=1)

    def to_tiles(tab, scale):
        t = (tab * scale).reshape(NT, 128, DH).transpose(1, 0, 2).reshape(128, NT * DH)
        return np.ascontiguousarray(t).astype(ml_dtypes.bfloat16)

    cosk = to_tiles(cos, 1.0)
    sinswk = to_tiles(sinsw, 1.0)

    p = np.arange(128)[:, None]
    c = np.arange(128)[None, :]
    mask = np.ascontiguousarray((c >= p).astype(np.float32)).astype(ml_dtypes.bfloat16)
    return cosk, sinswk, mask


_NC_CACHE = None


def kernel(L0_attn_x, L0_attn_Wq, L0_attn_Wkv, L0_attn_Wo, L0_attn_bo,
           L0_attn_ln_g, L0_attn_ln_b, L0_ffn_weight, L0_ffn_bias):
    global _NC_CACHE
    x = np.ascontiguousarray(np.asarray(L0_attn_x, dtype=np.float32))
    wqT = np.ascontiguousarray(np.asarray(L0_attn_Wq, dtype=np.float32).T)
    wkvT = np.ascontiguousarray(np.asarray(L0_attn_Wkv, dtype=np.float32).T)
    woT = np.ascontiguousarray(np.asarray(L0_attn_Wo, dtype=np.float32).T)
    wf = np.ascontiguousarray(np.asarray(L0_ffn_weight, dtype=np.float32))
    bo = np.ascontiguousarray(np.asarray(L0_attn_bo, dtype=np.float32))
    fb = np.ascontiguousarray(np.asarray(L0_ffn_bias, dtype=np.float32))
    g = np.ascontiguousarray(np.asarray(L0_attn_ln_g, dtype=np.float32))
    bb = np.ascontiguousarray(np.asarray(L0_attn_ln_b, dtype=np.float32))

    cosk, sinswk, mask = _host_constants()

    if _NC_CACHE is None:
        _NC_CACHE = build_nc()
    nc = _NC_CACHE

    in_maps = []
    for i in range(NCORES):
        in_maps.append({
            "x": np.ascontiguousarray(x[B_CORE * i:B_CORE * (i + 1)]),
            "wqT": wqT, "wkvT": wkvT, "woT": woT, "wf": wf,
            "bo": bo, "fb": fb, "lng": g, "lnb": bb,
            "cosk": cosk, "sinswk": sinswk, "mask": mask,
        })
    res = run_bass_kernel_spmd(nc, in_maps, core_ids=list(range(NCORES)))
    out = np.concatenate([np.asarray(res.results[i]["out"]) for i in range(NCORES)], axis=0)
    return out.astype(np.float32)
